# version 12
# baseline (speedup 1.0000x reference)
import contextlib
import math
import numpy as np
import ml_dtypes

import concourse.bass as bass
import concourse.mybir as mybir
from concourse.bass_utils import run_bass_kernel_spmd

F32 = mybir.dt.float32
BF16 = mybir.dt.bfloat16
AF = mybir.ActivationFunctionType
ALU = mybir.AluOpType
AX = mybir.AxisListType

T = 2048
D = 1024
DC = D // 128
NKC = 17
NFFT = 4096
MAGIC = 1.5 * 2.0 ** 23
ENGINES = ("tensor", "vector", "scalar", "gpsimd", "sync")
LIMIT_PHASES = None


def _ph(name):
    return LIMIT_PHASES is None or name in LIMIT_PHASES


import os as _os
KNOB = lambda n: _os.environ.get(n, "") == "1"


class Prog:
    def __init__(self, nc, es):
        self.nc = nc
        self.es = es
        self.q = {e: [] for e in ENGINES}
        self.semv = {}
        self.pending = {e: [] for e in ENGINES}

    def sem(self, name):
        h = self.es.enter_context(self.nc.semaphore(name))
        self.semv[id(h)] = 0
        return h

    def val(self, sem):
        return self.semv[id(sem)]

    def barrier(self, sems):
        waits = [(s, self.val(s)) for s in sems]
        for e in ENGINES:
            self.pending[e] = self.pending[e] + list(waits)

    def set_chain(self, eng_sems):
        self.chain = dict(eng_sems)

    def emit(self, eng, fn, inc=None, waits=()):
        waits = list(self.pending[eng]) + [(s, v) for (s, v) in waits if v > 0]
        self.pending[eng] = []
        ch = getattr(self, "chain", {}).get(eng)
        if ch is not None and (inc is None or inc[0] is ch):
            waits.append((ch, self.semv[id(ch)]))
            if inc is None:
                inc = (ch, 1)
        waits = [(s, v) for (s, v) in waits if v > 0]

        def item(engine):
            for (s, v) in waits:
                engine.wait_ge(s, v)
            inst = fn(engine)
            if inc is not None:
                inst.then_inc(inc[0], inc[1])

        self.q[eng].append(item)
        if inc is not None:
            self.semv[id(inc[0])] += inc[1]
            return self.semv[id(inc[0])]
        return None

    def wait(self, eng, sem, value):
        if value <= 0:
            return

        def item(engine):
            engine.wait_ge(sem, value)

        self.q[eng].append(item)

    def finalize(self):
        blk = self.es.enter_context(self.nc.Block())
        for e in ENGINES:
            items = self.q[e]
            if not items:
                continue

            def mk(its):
                def body(engine):
                    for it in its:
                        it(engine)
                return body

            getattr(blk, e)(mk(items))
        return blk


_CONST_CACHE = {}


def host_consts():
    if _CONST_CACHE:
        return _CONST_CACHE
    bf = ml_dtypes.bfloat16
    n = np.arange(T, dtype=np.float64)
    k = np.arange(NKC * 128, dtype=np.float64)
    ang = 2.0 * np.pi * np.outer(n, k) / NFFT
    fwd_cos = np.cos(ang)
    fwd_sin = -np.sin(ang)

    def tile_fwd(m):
        r = m.reshape(16, 128, NKC, 128)
        return np.ascontiguousarray(r.transpose(2, 1, 0, 3)).astype(bf)

    fwd_t = np.stack([tile_fwd(fwd_cos), tile_fwd(fwd_sin)])

    w = np.full(NKC * 128, 2.0 / NFFT)
    w[0] = 1.0 / NFFT
    w[2048] = 1.0 / NFFT
    w[2049:] = 0.0
    tau = np.arange(T, dtype=np.float64)
    ang2 = 2.0 * np.pi * np.outer(k, tau) / NFFT
    inv_cos = w[:, None] * np.cos(ang2)
    inv_sin = -w[:, None] * np.sin(ang2)

    def tile_inv(m):
        return np.ascontiguousarray(m.reshape(NKC, 128, T)).astype(bf)

    inv_t = np.stack([tile_inv(inv_cos), tile_inv(inv_sin)])

    ident = np.eye(128, dtype=np.float32).astype(bf)

    sel = np.zeros((8, 8 * 128), np.float32)
    for e in range(8):
        sel[e, e * 128:(e + 1) * 128] = 1.0
    sel = sel.astype(bf)

    mask16 = np.zeros((16, 128, 128), np.float32)
    for lb in range(16):
        for pp in range(128):
            mask16[lb, pp, lb * 8 + pp // 16] = 1.0
    mask16 = np.ascontiguousarray(mask16.transpose(1, 0, 2)).astype(bf)

    l = np.arange(T, dtype=np.float64)
    t_pos = l / (T - 1)
    phase = 1e-4 * (2.0 * np.pi * l / T)
    pe = np.stack([t_pos, np.cos(phase), -np.sin(phase)], axis=0)
    pe_rev = np.ascontiguousarray(pe[:, ::-1]).astype(np.float32)
    t_rev = np.ascontiguousarray(t_pos[::-1])[None, :].astype(np.float32)
    dmax, dmin = (math.log(0.01) / 0.3, math.log(0.01) / 1.5)
    deltas = np.abs(np.linspace(dmax, dmin, D)).astype(np.float32)
    negdeltas = np.ascontiguousarray((-deltas)[None, :])

    _CONST_CACHE.update(dict(
        fwd_t=fwd_t, inv_t=inv_t, ident=ident, sel=sel, mask16=mask16,
        pe_rev=pe_rev, t_rev=t_rev, negdeltas=negdeltas))
    return _CONST_CACHE


class Ctx:
    pass


def build_model(layers=(0, 1)):
    nc = bass.Bass("TRN2", target_bir_lowering=False)
    es = contextlib.ExitStack()
    p = Prog(nc, es)
    c = Ctx()
    c.nc, c.es, c.p = nc, es, p
    c.input_names = []

    def din(name, shape, dt=F32):
        c.input_names.append(name)
        return nc.dram_tensor(name, list(shape), dt, kind="ExternalInput").ap()

    c.din = din
    c.sb = lambda name, shape, dt=F32: es.enter_context(nc.sbuf_tensor(name, shape, dt))
    c.ps = lambda name, shape, dt=F32: es.enter_context(nc.psum_tensor(name, shape, dt))

    c._uid = [0]

    def sb_scoped(name, shape, dt=F32):
        c._uid[0] += 1
        cm = nc.sbuf_tensor(f"{name}_u{c._uid[0]}", shape, dt)
        h = cm.__enter__()
        return h, (lambda cm=cm: cm.__exit__(None, None, None))

    c.sb_scoped = sb_scoped

    c.s_pe = p.sem("s_pe")
    c.s_ve = p.sem("s_ve")
    c.s_sc = p.sem("s_sc")
    c.s_gp = p.sem("s_gp")
    c.s_out = p.sem("s_out")
    c.s_dbg = p.sem("s_dbg")
    c.all_sems = [c.s_pe, c.s_ve, c.s_sc, c.s_gp]
    p.set_chain({"vector": c.s_ve, "scalar": c.s_sc, "gpsimd": c.s_gp})

    class Burst:
        def __init__(self, name):
            self.sem = p.sem(name)
            self.n = 0

        def load(self, eng, fn, waits=()):
            p.emit(eng, fn, inc=(self.sem, 16), waits=waits)
            self.n += 1

        def done(self):
            return (self.sem, 16 * self.n)

    class Stream:
        def __init__(self, name, nslots):
            self.sems = [p.sem(f"{name}{i}") for i in range(nslots)]
            self.cnt = [0] * nslots

        def load(self, eng, slot, fn, waits=()):
            p.emit(eng, fn, inc=(self.sems[slot], 16), waits=waits)
            self.cnt[slot] += 1
            return (self.sems[slot], 16 * self.cnt[slot])

    c.Burst, c.Stream = Burst, Stream

    x_in = din("x_fm", (D, T))
    y_out = nc.dram_tensor("y_fm", [D, T], F32, kind="ExternalOutput").ap()
    c.x0_spill = nc.dram_tensor("x0_spill", [D, T], BF16, kind="Internal").ap()
    c.v2_spill = nc.dram_tensor("v2_spill", [D, T], BF16, kind="Internal").ap()

    c.fwd_t = din("c_fwd", (2, NKC, 128, 16, 128), BF16)
    c.inv_t = din("c_inv", (2, NKC, 128, T), BF16)
    id_d = din("c_id", (128, 128), BF16)
    idf_d = din("c_idf", (128, 128))
    sel_d = din("c_sel", (8, 8 * 128), BF16)

    c.X = [c.sb(f"X{i}", [128, T]) for i in range(DC)]
    c.ID = c.sb("ID", [128, 128], BF16)
    c.IDF = c.sb("IDF", [128, 128])
    c.SEL = c.sb("SEL", [8, 8 * 128], BF16)
    c.ONES = c.sb("ONESC", [128, 4])
    c.ONESR = c.sb("ONESR", [1, 128])
    c.SCR0 = c.sb("SCR0", [128, T])
    c.SCR1 = c.sb("SCR1", [128, 1024])
    c.SCB = [c.sb(f"SCB{i}", [128, T], BF16) for i in range(2)]
    c.STATS = c.sb("STATS", [1, 1024])

    c.PH = [c.ps(f"PH{i}", [128, 512]) for i in range(2)]
    c.PD = [c.ps(f"PD{i}", [128, 512]) for i in range(4)]
    c.PA = c.ps("PA", [128, 512])
    c.PB = c.ps("PB", [128, 512])

    c.bank_guard = {}
    c.bguard = lambda b: c.bank_guard.get(id(b), (c.s_ve, 0))
    c.set_bguard = lambda b, s, v: c.bank_guard.__setitem__(id(b), (s, v))

    bin_ = c.Burst("s_bin")
    bin_.load("sync", lambda e: e.dma_start(out=c.ID[:, :], in_=id_d))
    bin_.load("sync", lambda e: e.dma_start(out=c.IDF[:, :], in_=idf_d))
    bin_.load("sync", lambda e: e.dma_start(out=c.SEL[:, :], in_=sel_d))
    for dc in range(DC):
        bin_.load("sync", lambda e, dc=dc: e.dma_start(out=c.X[dc][:, :], in_=x_in[dc * 128:(dc + 1) * 128, :]))
    c.x_loaded = bin_.done()
    for col, v in [(0, 1.0), (1, 1e-5), (2, -math.pi), (3, 0.0)]:
        p.emit("vector", lambda e, col=col, v=v: e.memset(c.ONES[:, col:col + 1], v), inc=(c.s_ve, 1))
    p.emit("vector", lambda e: e.memset(c.ONESR[:, :], 1.0), inc=(c.s_ve, 1))
    c.ones_ready = p.val(c.s_ve)
    c.id_ready = c.x_loaded
    for _e in ENGINES:
        p.pending[_e] = [c.x_loaded]

    for li in layers:
        _emit_layer(c, li)

    p.barrier(c.all_sems)
    for dc in range(DC):
        p.emit("sync", lambda e, dc=dc: e.dma_start(out=y_out[dc * 128:(dc + 1) * 128, :], in_=c.X[dc][:, :]),
               inc=(c.s_out, 16))
    p.wait("sync", c.s_out, p.val(c.s_out))

    p.finalize()
    es.close()
    return nc, c.input_names


def _emit_layer(c, li):
    p = c.p
    c.XN, xn_close = c.sb_scoped(f"XN{li}", [128 * 1, T], BF16) if False else (None, None)
    c.XN = [None] * DC
    closers = []
    for i in range(DC):
        h, cl = c.sb_scoped(f"XN{li}_{i}", [128, T], BF16)
        c.XN[i] = h
        closers.append(cl)
    if _ph("ln"):
        _emit_ln(c, li)
    if li == 0:
        if _ph("router"):
            _emit_router(c, li)
        if _ph("ffn"):
            _emit_ffn_l0(c)
        elif _ph("router"):
            p.barrier(c.all_sems)
            for cl in reversed(c._router_closers):
                cl()
        if _ph("uphase"):
            _emit_uphase(c)
        p.barrier(c.all_sems)
        for cl in reversed(closers):
            cl()
        if _ph("conv"):
            _emit_conv_l0(c)
    else:
        if _ph("omega"):
            _emit_omega_residual(c)
        if _ph("router"):
            _emit_router(c, li)
        if _ph("ffn"):
            _emit_ffn_l1(c)
        elif _ph("router"):
            p.barrier(c.all_sems)
            for cl in reversed(c._router_closers):
                cl()
        if _ph("conv"):
            _emit_conv_l1(c)
        p.barrier(c.all_sems)
        for cl in reversed(closers):
            cl()


def _emit_ln(c, li):
    p = c.p
    XN = c.XN
    p.barrier(c.all_sems)
    SQ = c.SCR1[:, 0:512]
    MBC = [c.SCR1[:, 512:1024], c.SCR0[:, 0:512]]
    RBC = c.SCR0[:, 1024:2048]
    mm_guard = 0
    st_dones = {}
    bc_state = {}

    def emit_sums(tt):
        nonlocal mm_guard
        ts_ = slice(tt * 512, (tt + 1) * 512)
        gA = c.bguard(c.PA)
        gB = c.bguard(c.PB)
        for dc in range(DC):
            sqv = p.emit("scalar", lambda e, dc=dc, ts_=ts_: e.activation(
                out=SQ, in_=c.X[dc][:, ts_], func=AF.Square),
                inc=(c.s_sc, 1), waits=[(c.s_pe, mm_guard)])
            p.emit("tensor", lambda e, dc=dc, ts_=ts_: e.matmul(
                c.PA[0:1, 0:512], c.ONES[:, 0:1], c.X[dc][:, ts_],
                start=(dc == 0), stop=(dc == DC - 1)),
                inc=(c.s_pe, 1), waits=[gA] if dc == 0 else ())
            p.emit("tensor", lambda e, dc=dc: e.matmul(
                c.PB[0:1, 0:512], c.ONES[:, 0:1], SQ,
                start=(dc == 0), stop=(dc == DC - 1)),
                inc=(c.s_pe, 1), waits=[(c.s_sc, sqv)] + ([gB] if dc == 0 else []))
            mm_guard = p.val(c.s_pe)
        return p.val(c.s_pe)

    def emit_stats(tt, mm_done, prev_bc_pe):
        mean = c.STATS[0:1, 0:512]
        rstd = c.STATS[0:1, 512:1024]
        p.emit("vector", lambda e, mean=mean: e.tensor_scalar(
            out=mean, in0=c.PA[0:1, 0:512], scalar1=1.0 / 1024.0, scalar2=None, op0=ALU.mult),
            inc=(c.s_ve, 1), waits=[(c.s_pe, mm_done), (c.s_pe, prev_bc_pe)])
        p.emit("vector", lambda e, mean=mean, rstd=rstd: e.scalar_tensor_tensor(
            out=rstd, in0=mean, scalar=-1.0, in1=mean, op0=ALU.mult, op1=ALU.mult),
            inc=(c.s_ve, 1))
        p.emit("vector", lambda e, rstd=rstd: e.scalar_tensor_tensor(
            out=rstd, in0=c.PB[0:1, 0:512], scalar=1.0 / 1024.0, in1=rstd, op0=ALU.mult, op1=ALU.add),
            inc=(c.s_ve, 1))
        vvar = p.val(c.s_ve)
        p.emit("scalar", lambda e, rstd=rstd: e.activation(
            out=rstd, in_=rstd, func=AF.Sqrt, bias=c.ONES[0:1, 1:2], scale=1.0),
            inc=(c.s_sc, 1), waits=[(c.s_ve, vvar)])
        vsq = p.val(c.s_sc)
        p.emit("vector", lambda e, rstd=rstd: e.reciprocal(out=rstd, in_=rstd),
               inc=(c.s_ve, 1), waits=[(c.s_sc, vsq)])
        st_done = p.val(c.s_ve)
        c.set_bguard(c.PA, c.s_ve, st_done)
        c.set_bguard(c.PB, c.s_ve, st_done)
        return st_done, mean, rstd

    def emit_bc_xn(tt, st_done, mean, rstd):
        ts_ = slice(tt * 512, (tt + 1) * 512)
        par = tt % 2
        gC = c.bguard(c.PD[0 + par])
        gD = c.bguard(c.PD[2 + par])
        p.emit("tensor", lambda e, mean=mean, par=par: e.matmul(
            c.PD[0 + par][:, 0:512], c.ONESR[0:1, :], mean, start=True, stop=True),
            inc=(c.s_pe, 1), waits=[(c.s_ve, st_done), gC])
        p.emit("tensor", lambda e, rstd=rstd, par=par: e.matmul(
            c.PD[2 + par][:, 0:512], c.ONESR[0:1, :], rstd, start=True, stop=True),
            inc=(c.s_pe, 1), waits=[gD])
        bc_done = p.val(c.s_pe)
        p.emit("vector", lambda e, par=par: e.tensor_copy(MBC[par], c.PD[0 + par][:, 0:512]),
               inc=(c.s_ve, 1), waits=[(c.s_pe, bc_done)])
        p.emit("vector", lambda e, ts_=ts_, par=par: e.tensor_copy(RBC[:, 0:512] if False else c.SCR0[:, ts_][:, 0:512] if False else MBC[par], MBC[par]),
               inc=(c.s_ve, 1)) if False else None
        for dc in range(DC):
            p.emit("vector", lambda e, dc=dc, ts_=ts_, par=par: e.tensor_tensor(
                out=XN[dc][:, ts_], in0=c.X[dc][:, ts_], in1=MBC[par], op=ALU.subtract),
                inc=(c.s_ve, 1))
            p.emit("vector", lambda e, dc=dc, ts_=ts_, par=par: e.tensor_tensor(
                out=XN[dc][:, ts_], in0=XN[dc][:, ts_], in1=c.PD[2 + par][:, 0:512], op=ALU.mult),
                inc=(c.s_ve, 1))
        ve_done = p.val(c.s_ve)
        c.set_bguard(c.PD[0 + par], c.s_ve, ve_done)
        c.set_bguard(c.PD[2 + par], c.s_ve, ve_done)
        return bc_done

    prev_bc = 0
    pend = None
    for tt in range(4):
        mm_done = emit_sums(tt)
        if pend is not None:
            prev_bc = emit_bc_xn(*pend)
        st_done, mean, rstd = emit_stats(tt, mm_done, prev_bc)
        pend = (tt, st_done, mean, rstd)
    emit_bc_xn(*pend)
    c.xn_ready = p.val(c.s_ve)


def _emit_router(c, li):
    p = c.p
    if li == 0:
        wr = c.din("l0_wr_t", (D, 8))
    else:
        wr = c.din("l1_gate_t", (D, 8))
        beta16 = c.din("l1_beta_r16", (1, 128))
    WRS, wrs_close = c.sb_scoped(f"wrs{li}", [128, DC * 8], BF16)
    c.WT8, c.wt8_close = c.sb_scoped(f"wt8_{li}", [16, 16 * 128], BF16)
    WT8 = c.WT8
    rtb = c.Burst(f"s_rt{li}")
    rtb.load("gpsimd", lambda e: e.dma_start(
        out=WRS.rearrange("pp (dc e) -> pp dc e", e=8),
        in_=wr.rearrange("(dc pp) e -> pp dc e", pp=128)))
    LG = c.SCR1[:, 0:128]
    EL = c.SCR1[:, 128:256]
    MSK = c.SCR1[:, 256:384]
    WB = c.SCR1[:, 384:512]
    TMPA = c.SCR1[:, 512:640]
    TMPB = c.SCR1[:, 640:768]
    BETB = c.SCR1[:, 768:896]
    M16 = c.SCR1[:, 896:912]
    S16 = c.SCR1[:, 912:928]
    RD16 = c.SCR1[:, 928:944]

    def v16(ap):
        return ap.rearrange("pp (a b) -> pp a b", b=8)

    def b16(ap):
        return ap.unsqueeze(-1).broadcast_to([128, 16, 8])

    beta_bc = 0
    if li == 1:
        rtbs = c.Burst(f"s_rts{li}")
        rtbs.load("sync", lambda e: e.dma_start(out=c.STATS[0:1, 0:128], in_=beta16))
        gA = c.bguard(c.PA)
        p.emit("tensor", lambda e: e.matmul(c.PA[:, 0:128], c.ONESR[0:1, :], c.STATS[0:1, 0:128],
                                            start=True, stop=True),
               inc=(c.s_pe, 1), waits=[rtbs.done(), gA, (c.s_ve, c.ones_ready)])
        bv = p.val(c.s_pe)
        p.emit("vector", lambda e: e.tensor_copy(BETB, c.PA[:, 0:128]),
               inc=(c.s_ve, 1), waits=[(c.s_pe, bv)])
        c.set_bguard(c.PA, c.s_ve, p.val(c.s_ve))
        beta_bc = p.val(c.s_ve)

    gA = c.bguard(c.PA)
    for t in range(16):
        for dc in range(DC):
            p.emit("tensor", lambda e, t=t, dc=dc, xn=c.XN[dc]: e.matmul(
                c.PA[:, t * 8:(t + 1) * 8], xn[:, t * 128:(t + 1) * 128],
                WRS[:, dc * 8:(dc + 1) * 8],
                start=(t == 0 and dc == 0), stop=(t == 15 and dc == DC - 1)),
                inc=(c.s_pe, 1) if (t == 15 and dc == DC - 1) else None,
                waits=[(c.s_ve, c.xn_ready), rtb.done(), gA] if (t == 0 and dc == 0) else ())
    lg_mm = p.val(c.s_pe)
    p.emit("vector", lambda e: e.tensor_copy(LG, c.PA[:, 0:128]),
           inc=(c.s_ve, 1), waits=[(c.s_pe, lg_mm)])
    c.set_bguard(c.PA, c.s_ve, p.val(c.s_ve))

    if li == 0:
        p.emit("vector", lambda e: e.reduce_max(M16, v16(LG), axis=AX.X), inc=(c.s_ve, 1))
        p.emit("vector", lambda e: e.tensor_tensor(out=v16(TMPA), in0=v16(LG), in1=b16(M16), op=ALU.subtract),
               inc=(c.s_ve, 1))
        vv = p.val(c.s_ve)
        p.emit("scalar", lambda e: e.activation(out=EL, in_=TMPA, func=AF.Exp),
               inc=(c.s_sc, 1), waits=[(c.s_ve, vv)])
        sc_el = p.val(c.s_sc)
        p.emit("vector", lambda e: e.tensor_tensor(out=v16(MSK), in0=v16(LG), in1=b16(M16), op=ALU.is_ge),
               inc=(c.s_ve, 1))
        p.emit("vector", lambda e: e.tensor_scalar(out=TMPA, in0=MSK, scalar1=-1.0, scalar2=1.0,
                                                   op0=ALU.mult, op1=ALU.add),
               inc=(c.s_ve, 1), waits=[(c.s_sc, sc_el)])
        p.emit("vector", lambda e: e.tensor_tensor(out=TMPA, in0=EL, in1=TMPA, op=ALU.mult),
               inc=(c.s_ve, 1))
        p.emit("vector", lambda e: e.reduce_max(S16, v16(TMPA), axis=AX.X), inc=(c.s_ve, 1))
        p.emit("vector", lambda e: e.scalar_tensor_tensor(out=TMPB, in0=MSK, scalar=-1e30, in1=LG,
                                                          op0=ALU.mult, op1=ALU.add), inc=(c.s_ve, 1))
        p.emit("vector", lambda e: e.reduce_max(M16, v16(TMPB), axis=AX.X), inc=(c.s_ve, 1))
        p.emit("vector", lambda e: e.tensor_tensor(out=v16(MSK), in0=v16(LG), in1=b16(M16), op=ALU.is_ge),
               inc=(c.s_ve, 1))
        p.emit("vector", lambda e: e.tensor_scalar(out=S16, in0=S16, scalar1=1.0, scalar2=None,
                                                   op0=ALU.add), inc=(c.s_ve, 1))
        p.emit("vector", lambda e: e.reciprocal(out=RD16, in_=S16), inc=(c.s_ve, 1))
        p.emit("vector", lambda e: e.tensor_tensor(out=v16(WB), in0=v16(EL), in1=b16(RD16), op=ALU.mult),
               inc=(c.s_ve, 1))
        p.emit("vector", lambda e: e.tensor_tensor(out=WB, in0=WB, in1=MSK, op=ALU.mult),
               inc=(c.s_ve, 1))
    else:
        p.emit("vector", lambda e: e.tensor_tensor(out=TMPA, in0=LG, in1=BETB, op=ALU.add),
               inc=(c.s_ve, 1), waits=[(c.s_ve, beta_bc)])
        p.emit("vector", lambda e: e.reduce_max(M16, v16(TMPA), axis=AX.X), inc=(c.s_ve, 1))
        p.emit("vector", lambda e: e.tensor_tensor(out=v16(MSK), in0=v16(TMPA), in1=b16(M16), op=ALU.is_ge),
               inc=(c.s_ve, 1))
        p.emit("vector", lambda e: e.scalar_tensor_tensor(out=TMPB, in0=MSK, scalar=-1e30, in1=TMPA,
                                                          op0=ALU.mult, op1=ALU.add), inc=(c.s_ve, 1))
        p.emit("vector", lambda e: e.reduce_max(M16, v16(TMPB), axis=AX.X), inc=(c.s_ve, 1))
        p.emit("vector", lambda e: e.tensor_tensor(out=v16(MSK), in0=v16(TMPA), in1=b16(M16), op=ALU.is_ge),
               inc=(c.s_ve, 1))
        vv = p.val(c.s_ve)
        p.emit("scalar", lambda e: e.activation(out=EL, in_=LG, func=AF.Sigmoid),
               inc=(c.s_sc, 1), waits=[(c.s_ve, vv)])
        sg = p.val(c.s_sc)
        p.emit("vector", lambda e: e.tensor_tensor(out=WB, in0=EL, in1=MSK, op=ALU.mult),
               inc=(c.s_ve, 1), waits=[(c.s_sc, sg)])
    w_done = p.val(c.s_ve)

    banks = [c.PA, c.PB]
    for t in range(16):
        bank = banks[t % 2]
        gB = c.bguard(bank)
        p.emit("tensor", lambda e, t=t, ba=bank: e.matmul(
            ba[0:8, 0:128], WB[:, t * 8:(t + 1) * 8], c.IDF[:, :], start=True, stop=True),
            inc=(c.s_pe, 1), waits=[(c.s_ve, w_done), gB, c.id_ready] if t < 2 else [gB])
        tr = p.val(c.s_pe)
        p.emit("vector", lambda e, t=t, ba=bank: e.tensor_copy(WT8[0:8, t * 128:(t + 1) * 128],
                                                               ba[0:8, 0:128]),
               inc=(c.s_ve, 1), waits=[(c.s_pe, tr)])
        c.set_bguard(bank, c.s_ve, p.val(c.s_ve))
    c.router_done = p.val(c.s_ve)
    c._router_closers = [wrs_close, c.wt8_close]


def _mk_wbc(c, e_, into):
    p = c.p
    WT8 = c.WT8
    sel_sl = c.SEL[:, e_ * 128:(e_ + 1) * 128]
    banks = [c.PA, c.PB]
    for g in range(4):
        bank = banks[g % 2]
        gA = c.bguard(bank)
        for s in range(4):
            t = g * 4 + s
            p.emit("tensor", lambda en, t=t, s=s, ba=bank: en.matmul(
                ba[:, s * 128:(s + 1) * 128], sel_sl[0:8, :], WT8[0:8, t * 128:(t + 1) * 128],
                start=(s == 0), stop=(s == 3)),
                inc=(c.s_pe, 1) if s == 3 else None,
                waits=[gA, (c.s_ve, c.router_done)] if s == 0 else ())
        v = p.val(c.s_pe)
        if g % 2 == 0:
            p.emit("vector", lambda en, g=g, ba=bank: en.tensor_copy(into[:, g * 512:(g + 1) * 512], ba[:, 0:512]),
                   inc=(c.s_ve, 1), waits=[(c.s_pe, v)])
            c.set_bguard(bank, c.s_ve, p.val(c.s_ve))
        else:
            p.emit("scalar", lambda en, g=g, ba=bank: en.activation(
                out=into[:, g * 512:(g + 1) * 512], in_=ba[:, 0:512], func=AF.Identity),
                inc=(c.s_sc, 1), waits=[(c.s_pe, v)])
            c.set_bguard(bank, c.s_sc, p.val(c.s_sc))
    p.emit("vector", lambda en: en.engine_nop(), inc=(c.s_ve, 1), waits=[(c.s_sc, p.val(c.s_sc))])
    return p.val(c.s_ve)


def _emit_ffn_l0(c):
    p = c.p
    XN = c.XN
    WT8 = c.WT8
    p.barrier(c.all_sems)
    E, M = 8, 4096
    MC = M // 128
    PIECE, MSL, NBUF = 8, 2, 3
    NP = MC // PIECE
    NSL = PIECE // MSL

    w1 = c.din("l0f_w1t", (E, D, M))
    w2t = c.din("l0f_w2t", (E, M, D))
    b1c = c.din("l0f_b1c", (128, E * MC))
    b2d = c.din("l0f_b2", (8, D))

    closers = []

    def sbsc(name, shape, dt=F32):
        h, cl = c.sb_scoped(name, shape, dt)
        closers.append(cl)
        return h

    W1B = [sbsc(f"l0w1b{i}", [128, DC * MSL * 128], BF16) for i in range(NBUF)]
    W2P = sbsc("l0w2p", [128, PIECE * D], BF16)
    HP = [sbsc(f"l0hp{i}", [128, T], BF16) for i in range(PIECE)]
    B1C = sbsc("l0b1c", [128, E * MC])
    B2S = sbsc("l0b2s", [8, D], BF16)
    WBC = [sbsc(f"l0wbc{i}", [128, T], BF16) for i in range(2)]

    fb0 = c.Burst("s_fb0")
    fb0g = c.Burst("s_fb0g")
    fb0.load("sync", lambda e: e.dma_start(out=B1C[:, :], in_=b1c))
    fb0g.load("gpsimd", lambda e: e.dma_start(out=B2S[:, :], in_=b2d))

    w1_guard = {}
    w1_cnt = [0]
    w1_ld_val = {}

    def load_w1(e_, mp, sl):
        i = w1_cnt[0]
        buf = W1B[i % NBUF]
        m0 = (mp * PIECE + sl * MSL) * 128
        src = w1[e_, :, m0:m0 + MSL * 128].rearrange("(dc pp) m -> pp dc m", pp=128)
        dst = buf.rearrange("pp (dc m) -> pp dc m", m=MSL * 128)
        g = w1_guard.get(i % NBUF)
        w1_ld_val[i] = w1s.load("gpsimd", i % NBUF, lambda en: en.dma_start(out=dst, in_=src),
                                waits=[g] if g else [])
        w1_cnt[0] += 1
        return i

    w2_guard = [None]
    w1s = c.Stream("s_w1", NBUF)
    w2s = c.Stream("s_w2", 1)

    def load_w2(e_, mp):
        src = w2t[e_, mp * PIECE * 128:(mp + 1) * PIECE * 128, :].rearrange("(mi pp) h -> pp mi h", pp=128)
        dst = DPW.rearrange("pp (mi h) -> pp mi h", h=D) if False else W2P.rearrange("pp (mi h) -> pp mi h", h=D)
        g = w2_guard[0]
        return w2s.load("gpsimd", 0, lambda en: en.dma_start(out=dst, in_=src),
                        waits=[g] if g else [])

    order = [(e_, mp) for e_ in range(E) for mp in range(NP)]
    if _os.environ.get("K_NPIECE"):
        order = order[:int(_os.environ["K_NPIECE"])]
    slab_ids = {}
    for k in range(NBUF):
        slab_ids[(0, 0, k)] = load_w1(0, 0, k)

    ph_guard = [0, 0]
    pd_guard = {0: 0, 1: 0, 2: 0, 3: 0}
    h_guard = 0
    wbc_done = {}

    for oi, (e_, mp) in enumerate(order):
        if mp == 0:
            if KNOB("K_NOWBC"):
                wbc_done[e_ % 2] = 0
            else:
                wbc_done[e_ % 2] = _mk_wbc(c, e_, WBC[e_ % 2])
        w2_ld = load_w2(e_, mp)
        for mc_i in range(PIECE):
            sl, mi = divmod(mc_i, MSL)
            sid = slab_ids[(e_, mp, sl)]
            mc = mp * PIECE + mc_i
            for tt in range(4):
                bank = (mc_i * 4 + tt) % 2
                waits = [(c.s_sc, ph_guard[bank])]
                if mi == 0 and tt == 0:
                    waits.append(w1_ld_val[sid])
                if oi == 0 and mc_i == 0 and tt == 0:
                    waits += [(c.s_ve, c.xn_ready), fb0.done()]
                for kc in range(DC):
                    lhsT = W1B[sid % NBUF][:, (kc * MSL + mi) * 128:(kc * MSL + mi + 1) * 128]
                    p.emit("tensor", lambda en, l=lhsT, k=kc, b=bank, t=tt:
                           en.matmul(c.PH[b][:, :], l, XN[k][:, t * 512:(t + 1) * 512],
                                     start=(k == 0), stop=(k == DC - 1)),
                           inc=(c.s_pe, 1) if kc == DC - 1 else None,
                           waits=waits if kc == 0 else ())
                pe_cnt = p.val(c.s_pe)
                bcol = B1C[:, e_ * MC + mc:e_ * MC + mc + 1]
                w = [(c.s_pe, pe_cnt)]
                if mc_i == 0 and tt == 0:
                    w.append((c.s_pe, h_guard))
                p.emit("scalar", lambda en, b=bank, m=mc_i, t=tt, bc=bcol:
                       en.activation(out=HP[m][:, t * 512:(t + 1) * 512], in_=c.PH[b][:, :],
                                     func=AF.Gelu, bias=bc, scale=1.0),
                       inc=(c.s_sc, 1), waits=w)
                sc_v = p.val(c.s_sc)
                ph_guard[bank] = sc_v
                if not KNOB("K_NOGATE"):
                    p.emit("vector", lambda en, m=mc_i, t=tt, b=e_ % 2:
                           en.tensor_tensor(out=HP[m][:, t * 512:(t + 1) * 512],
                                            in0=HP[m][:, t * 512:(t + 1) * 512],
                                            in1=WBC[b][:, t * 512:(t + 1) * 512], op=ALU.mult),
                           inc=(c.s_ve, 1),
                           waits=[(c.s_sc, sc_v)] + ([(c.s_ve, wbc_done[e_ % 2])] if (mc_i == 0 and tt == 0) else []))
                else:
                    p.emit("vector", lambda en: en.engine_nop(), inc=(c.s_ve, 1),
                           waits=[(c.s_sc, sc_v)])
            if mi == MSL - 1:
                w1_guard[sid % NBUF] = (c.s_pe, p.val(c.s_pe))
                gidx = oi * NSL + sl + NBUF
                if gidx < len(order) * NSL:
                    noi, nsl = divmod(gidx, NSL)
                    ne, nmp = order[noi]
                    slab_ids[(ne, nmp, nsl)] = load_w1(ne, nmp, nsl)
        h_done = p.val(c.s_ve)
        last_piece = (e_ == E - 1 and mp == NP - 1)
        for tt in range(4):
            for dc in range(DC):
                bank = (tt * DC + dc) % 4
                waits = [(c.s_ve, pd_guard[bank])]
                if tt == 0 and dc == 0:
                    waits += [(c.s_ve, h_done), w2_ld]
                with_b2 = last_piece and not KNOB("K_NOB2")
                for mc_i in range(PIECE):
                    stop = (mc_i == PIECE - 1) and not last_piece
                    last_inst = (mc_i == PIECE - 1) and not with_b2
                    p.emit("tensor", lambda en, l=W2P[:, mc_i * D + dc * 128:mc_i * D + (dc + 1) * 128],
                           m=mc_i, b=bank, t=tt, st=stop:
                           en.matmul(c.PD[b][:, :], l, HP[m][:, t * 512:(t + 1) * 512],
                                     start=(m == 0), stop=st),
                           inc=(c.s_pe, 1) if last_inst else None,
                           waits=waits if mc_i == 0 else ())
                if with_b2:
                    for st_i in range(4):
                        t_ = tt * 4 + st_i
                        p.emit("tensor", lambda en, t_=t_, s=st_i, b=bank, dc=dc:
                               en.matmul(c.PD[b][:, s * 128:(s + 1) * 128],
                                         B2S[0:8, dc * 128:(dc + 1) * 128],
                                         WT8[0:8, t_ * 128:(t_ + 1) * 128],
                                         start=False, stop=(s == 3)),
                               inc=(c.s_pe, 1) if st_i == 3 else None,
                               waits=[fb0g.done()] if (tt == 0 and dc == 0 and st_i == 0) else ())
                pe_cnt = p.val(c.s_pe)
                p.emit("vector", lambda en, d=dc, b=bank, t=tt:
                       en.tensor_tensor(out=c.X[d][:, t * 512:(t + 1) * 512],
                                        in0=c.X[d][:, t * 512:(t + 1) * 512],
                                        in1=c.PD[b][:, :], op=ALU.add),
                       inc=(c.s_ve, 1), waits=[(c.s_pe, pe_cnt)])
                pd_guard[bank] = p.val(c.s_ve)
        h_guard = p.val(c.s_pe)
        w2_guard[0] = (c.s_pe, h_guard)
    if KNOB("K_DUMP"):
        dbg = c.nc.dram_tensor("dbg_gate", [3, 128, T], F32, kind="ExternalOutput").ap()
        vv = p.val(c.s_ve)
        p.emit("vector", lambda e: e.tensor_copy(c.SCR0[:, :], WBC[0][:, :]), inc=(c.s_ve, 1),
               waits=[(c.s_ve, vv)])
        dv = p.val(c.s_ve)
        p.emit("sync", lambda e: e.dma_start(out=dbg[0], in_=c.SCR0[:, :]), inc=(c.s_dbg, 16),
               waits=[(c.s_ve, dv)])
        p.emit("vector", lambda e: e.tensor_copy(c.SCR0[:, :], WBC[1][:, :]), inc=(c.s_ve, 1),
               waits=[(c.s_dbg, p.val(c.s_dbg))])
        dv = p.val(c.s_ve)
        p.emit("sync", lambda e: e.dma_start(out=dbg[1], in_=c.SCR0[:, :]), inc=(c.s_dbg, 16),
               waits=[(c.s_ve, dv)])
        p.emit("vector", lambda e: e.memset(c.SCR0[:, :], 0.0), inc=(c.s_ve, 1),
               waits=[(c.s_dbg, p.val(c.s_dbg))])
        p.emit("vector", lambda e: e.tensor_copy(c.SCR0[0:16, 0:2048], WT8[:, :]), inc=(c.s_ve, 1))
        dv = p.val(c.s_ve)
        p.emit("sync", lambda e: e.dma_start(out=dbg[2], in_=c.SCR0[:, :]), inc=(c.s_dbg, 16),
               waits=[(c.s_ve, dv)])
        p.wait("sync", c.s_dbg, p.val(c.s_dbg))
    c.set_bguard(c.PH[0], c.s_sc, ph_guard[0])
    c.set_bguard(c.PH[1], c.s_sc, ph_guard[1])
    for b in range(4):
        c.set_bguard(c.PD[b], c.s_ve, pd_guard[b])
    p.barrier(c.all_sems)
    for cl in reversed(closers):
        cl()
    for cl in reversed(c._router_closers):
        cl()


def _emit_uphase(c):
    p = c.p
    XN = c.XN
    W = 3 * D
    WC = W // 128
    win = c.din("l0a_win", (D, W))
    binc = c.din("l0a_binc", (128, WC))
    swc = c.din("l0a_swc", (128, WC * 3))
    sbc = c.din("l0a_sbc", (128, WC))

    NBUF = 3
    closers = []

    def sbsc(name, shape, dt=F32):
        h, cl = c.sb_scoped(name, shape, dt)
        closers.append(cl)
        return h

    WINB = [sbsc(f"winb{i}", [128, DC * 256], BF16) for i in range(NBUF)]
    SMALL = sbsc("usmall", [128, WC * 5])
    U = [sbsc(f"ubuf{i}", [128, T + 8]) for i in range(2)]
    X1 = [sbsc(f"x1_{i}", [128, T], BF16) for i in range(8)]

    ubst = c.Burst("s_ub")
    ubst.load("sync", lambda e: e.dma_start(out=SMALL[:, 0:WC], in_=binc))
    ubst.load("sync", lambda e: e.dma_start(out=SMALL[:, WC:WC * 4], in_=swc))
    ubst.load("sync", lambda e: e.dma_start(out=SMALL[:, WC * 4:WC * 5], in_=sbc))
    for i in range(2):
        p.emit("vector", lambda e, i=i: e.memset(U[i][:, 0:8], 0.0), inc=(c.s_ve, 1))
    u_zero = p.val(c.s_ve)

    guard = {}
    cnt = [0]
    win_ld_val = {}
    wins = c.Stream("s_win", NBUF)
    sp0 = c.Stream("s_sp", 2)
    spv = c.Burst("s_spv")

    def load_win(sl2):
        i = cnt[0]
        buf = WINB[i % NBUF]
        src = win[:, sl2 * 256:(sl2 + 1) * 256].rearrange("(dc pp) m -> pp dc m", pp=128)
        dst = buf.rearrange("pp (dc m) -> pp dc m", m=256)
        g = guard.get(i % NBUF)
        win_ld_val[i] = wins.load("gpsimd", i % NBUF, lambda en: en.dma_start(out=dst, in_=src),
                                  waits=[g] if g else [])
        cnt[0] += 1
        return i

    sids = {}
    for k in range(NBUF):
        sids[k] = load_win(k)

    u_guard = [0, 0]
    spill = [None, None]
    TMP = c.SCR0
    for wc in range(WC):
        sl, half = divmod(wc, 2)
        sid = sids[sl]
        ub = U[wc % 2]
        for tt in range(4):
            bank = tt % 2
            waits = [c.bguard(c.PH[bank])]
            if tt == 0:
                waits += [win_ld_val[sid], (c.s_ve, u_guard[wc % 2])]
                if wc == 0:
                    waits += [(c.s_ve, c.xn_ready), ubst.done(), (c.s_ve, u_zero)]
            for kc in range(DC):
                lhsT = WINB[sid % NBUF][:, (kc * 2 + half) * 128:(kc * 2 + half + 1) * 128]
                p.emit("tensor", lambda en, l=lhsT, k=kc, b=bank, t=tt:
                       en.matmul(c.PH[b][:, :], l, XN[k][:, t * 512:(t + 1) * 512],
                                 start=(k == 0), stop=(k == DC - 1)),
                       inc=(c.s_pe, 1) if kc == DC - 1 else None,
                       waits=waits if kc == 0 else ())
            pe_cnt = p.val(c.s_pe)
            p.emit("scalar", lambda en, b=bank, t=tt, wc=wc, ub=ub:
                   en.activation(out=ub[:, 8 + t * 512:8 + (t + 1) * 512], in_=c.PH[b][:, :],
                                 func=AF.Identity, bias=SMALL[:, wc:wc + 1], scale=1.0),
                   inc=(c.s_sc, 1), waits=[(c.s_pe, pe_cnt)])
            c.set_bguard(c.PH[bank], c.s_sc, p.val(c.s_sc))
        ev_done = p.val(c.s_sc)
        if half == 1:
            guard[sid % NBUF] = (c.s_pe, p.val(c.s_pe))
            if sl + NBUF < 12:
                sids[sl + NBUF] = load_win(sl + NBUF)
        sw0 = SMALL[:, WC + wc * 3:WC + wc * 3 + 1]
        sw1 = SMALL[:, WC + wc * 3 + 1:WC + wc * 3 + 2]
        sw2 = SMALL[:, WC + wc * 3 + 2:WC + wc * 3 + 3]
        sbb = SMALL[:, WC * 4 + wc:WC * 4 + wc + 1]
        p.emit("vector", lambda e, ub=ub, sw1=sw1, sbb=sbb: e.tensor_scalar(
            out=TMP[:, 0:T], in0=ub[:, 7:7 + T], scalar1=sw1, scalar2=sbb, op0=ALU.mult, op1=ALU.add),
            inc=(c.s_ve, 1), waits=[(c.s_sc, ev_done)])
        p.emit("vector", lambda e, ub=ub, sw0=sw0: e.scalar_tensor_tensor(
            out=TMP[:, 0:T], in0=ub[:, 6:6 + T], scalar=sw0, in1=TMP[:, 0:T], op0=ALU.mult, op1=ALU.add),
            inc=(c.s_ve, 1))
        if wc < 8:
            tgt = c.SCB[wc % 2]
            p.emit("vector", lambda e, ub=ub, sw2=sw2, tgt=tgt: e.scalar_tensor_tensor(
                out=tgt[:, :], in0=ub[:, 8:8 + T], scalar=sw2, in1=TMP[:, 0:T], op0=ALU.mult, op1=ALU.add),
                inc=(c.s_ve, 1), waits=[spill[wc % 2]] if spill[wc % 2] else ())
            vv = p.val(c.s_ve)
            spill[wc % 2] = sp0.load("sync", wc % 2, lambda e, wc=wc, tgt=tgt: e.dma_start(
                out=c.x0_spill[wc * 128:(wc + 1) * 128, :], in_=tgt[:, :]),
                waits=[(c.s_ve, vv)])
        elif wc < 16:
            p.emit("vector", lambda e, ub=ub, sw2=sw2, x1=X1[wc - 8]: e.scalar_tensor_tensor(
                out=x1[:, :], in0=ub[:, 8:8 + T], scalar=sw2, in1=TMP[:, 0:T], op0=ALU.mult, op1=ALU.add),
                inc=(c.s_ve, 1))
        else:
            j = wc - 16
            p.emit("vector", lambda e, ub=ub, sw2=sw2: e.scalar_tensor_tensor(
                out=TMP[:, 0:T], in0=ub[:, 8:8 + T], scalar=sw2, in1=TMP[:, 0:T], op0=ALU.mult, op1=ALU.add),
                inc=(c.s_ve, 1))
            p.emit("vector", lambda e, x1=X1[j]: e.tensor_tensor(
                out=x1[:, :], in0=TMP[:, 0:T], in1=x1[:, :], op=ALU.mult),
                inc=(c.s_ve, 1))
            vv = p.val(c.s_ve)
            spv.load("sync", lambda e, j=j, x1=X1[j]: e.dma_start(
                out=c.v2_spill[j * 128:(j + 1) * 128, :], in_=x1[:, :]),
                waits=[(c.s_ve, vv)])
        u_guard[wc % 2] = p.val(c.s_ve)
    c.uphase_done = p.val(c.s_ve)
    c.spills_done = [(sp0.sems[0], 16 * sp0.cnt[0]),
                     (sp0.sems[1], 16 * sp0.cnt[1]), spv.done()]
    p.barrier(c.all_sems)
    for _e in ENGINES:
        p.pending[_e] = p.pending[_e] + list(c.spills_done)
    for cl in reversed(closers):
        cl()


class ConvBundle:
    pass


def _alloc_conv_bundle(c, with_l0_extras):
    b = ConvBundle()
    b.closers = []
    if not hasattr(c, "fws"):
        c.fws = c.Stream("s_fw", 4)
        c.ivs = c.Stream("s_iv", 4)

    def sbsc(name, shape, dt=BF16):
        h, cl = c.sb_scoped(name, shape, dt)
        b.closers.append(cl)
        return h

    b.KF = sbsc("cv_kf", [128, NKC * 512])
    b.FWDB = sbsc("cv_fwdb", [128, 4 * 2048])
    b.INVB = sbsc("cv_invb", [128, 4 * 2048])
    b.SK = sbsc("cv_sk", [128, 16 * 512])
    b.KTMP = c.SCB[1][:, 0:1024]
    if with_l0_extras:
        b.X0Q = sbsc("cv_x0q", [128, 2 * T])
        b.V2Q = sbsc("cv_v2q", [128, 2 * T])
        b.WOUTB = sbsc("cv_woutb", [128, 2 * D])
    b.fwd_guard = {}
    b.inv_guard = {}
    return b


def _conv_kf(b, kc, ri):
    return b.KF[:, kc * 512 + ri * 256:kc * 512 + (ri + 1) * 256]


def _conv_quarter(c, b, sigtm_ready, kerq_ready, out_cb):
    p = c.p

    def load_fwd(kc, which):
        slot = (kc % 2) * 2 + which
        off = slot * 2048
        buf = b.FWDB[:, off:off + 2048]
        g = b.fwd_guard.get(slot)
        v = c.fws.load("sync", slot,
            lambda e, kc=kc, w=which, bu=buf: e.dma_start(
                out=bu, in_=c.fwd_t[w, kc].rearrange("pp nt j -> pp (nt j)")),
            waits=[g] if g else [])
        return v, buf

    fwd_banks = [c.PH[0], c.PH[1], c.PD[0], c.PD[1]]
    for kc in range(NKC):
        lc, fbc = load_fwd(kc, 0)
        ls, fbs = load_fwd(kc, 1)
        sig_banks = [fwd_banks[(kc % 2) * 2 + 0], fwd_banks[(kc % 2) * 2 + 1]]
        for ri in range(2):
            bank = sig_banks[ri]
            g = c.bguard(bank)
            fb = fbc if ri == 0 else fbs
            for nt in range(16):
                p.emit("tensor", lambda e, nt=nt, ba=bank, fb=fb: e.matmul(
                    ba[:, 0:512], fb[:, nt * 128:(nt + 1) * 128],
                    b.SK[:, nt * 512:(nt + 1) * 512],
                    start=(nt == 0), stop=(nt == 15)),
                    inc=(c.s_pe, 1) if nt == 15 else None,
                    waits=[g, lc, ls, (c.s_ve, sigtm_ready), (c.s_ve, kerq_ready)] if nt == 0 else ())
        mm = p.val(c.s_pe)
        ko = (kc % 2) * 512
        kr = b.KTMP[:, ko:ko + 256]
        ki = b.KTMP[:, ko + 256:ko + 512]
        sb0, sb1 = sig_banks
        kg = b.fwd_guard.get(8 + kc % 2)
        p.emit("scalar", lambda e, ba=sb0, kr=kr: e.activation(out=kr, in_=ba[:, 256:512], func=AF.Identity),
               inc=(c.s_sc, 1), waits=[(c.s_pe, mm)] + ([kg] if kg else []))
        p.emit("scalar", lambda e, ba=sb1, ki=ki: e.activation(out=ki, in_=ba[:, 256:512], func=AF.Identity),
               inc=(c.s_sc, 1))
        kev = p.val(c.s_sc)
        t1 = c.SCR0[:, 0:256]; t2 = c.SCR0[:, 256:512]
        t3 = c.SCR0[:, 512:768]; t4 = c.SCR0[:, 768:1024]
        pr = _conv_kf(b, kc, 0); pi = _conv_kf(b, kc, 1)
        w0 = [(c.s_pe, mm), (c.s_sc, kev)]
        p.emit("vector", lambda e, kr=kr, ba=sb0: e.tensor_tensor(out=t1, in0=ba[:, 0:256], in1=kr, op=ALU.mult),
               inc=(c.s_ve, 1), waits=w0)
        p.emit("vector", lambda e, ki=ki, ba=sb1: e.tensor_tensor(out=t2, in0=ba[:, 0:256], in1=ki, op=ALU.mult),
               inc=(c.s_ve, 1))
        p.emit("vector", lambda e, ki=ki, ba=sb0: e.tensor_tensor(out=t3, in0=ba[:, 0:256], in1=ki, op=ALU.mult),
               inc=(c.s_ve, 1))
        p.emit("vector", lambda e, kr=kr, ba=sb1: e.tensor_tensor(out=t4, in0=ba[:, 0:256], in1=kr, op=ALU.mult),
               inc=(c.s_ve, 1))
        p.emit("vector", lambda e, pr=pr: e.tensor_tensor(out=pr, in0=t1, in1=t2, op=ALU.subtract),
               inc=(c.s_ve, 1))
        p.emit("vector", lambda e, pi=pi: e.tensor_tensor(out=pi, in0=t3, in1=t4, op=ALU.add),
               inc=(c.s_ve, 1))
        pw = p.val(c.s_ve)
        c.set_bguard(sb0, c.s_ve, pw)
        c.set_bguard(sb1, c.s_ve, pw)
        b.fwd_guard[(kc % 2) * 2 + 0] = (c.s_pe, p.val(c.s_pe))
        b.fwd_guard[(kc % 2) * 2 + 1] = (c.s_pe, p.val(c.s_pe))
        b.fwd_guard[8 + kc % 2] = (c.s_ve, pw)
    p_done = p.val(c.s_ve)

    inv_banks = [c.PH[0], c.PH[1], c.PD[0], c.PD[1], c.PD[2], c.PD[3], c.PA, c.PB]
    for kc in range(NKC):
        inv_lds = []
        for ri in range(2):
            slot = (kc % 2) * 2 + ri
            off = slot * 2048
            g = b.inv_guard.get(slot)
            inv_lds.append(c.ivs.load("sync" if ri == 0 else "gpsimd", slot,
                lambda e, kc=kc, ri=ri, off=off: e.dma_start(
                    out=b.INVB[:, off:off + 2048], in_=c.inv_t[ri, kc]),
                waits=[g] if g else []))
        for ccl in range(2):
            for tt in range(4):
                bank = inv_banks[ccl * 4 + tt]
                for ri in range(2):
                    w = []
                    if kc == 0 and ri == 0:
                        w = [c.bguard(bank), (c.s_ve, p_done)]
                    if ccl == 0 and tt == 0 and ri == 0:
                        w.extend(inv_lds)
                    off = ((kc % 2) * 2 + ri) * 2048
                    psl = _conv_kf(b, kc, ri)[:, ccl * 128:(ccl + 1) * 128]
                    p.emit("tensor", lambda e, ps=psl, off=off, tt=tt, ba=bank, kc=kc, ri=ri:
                           e.matmul(ba[:, 0:512], ps,
                                    b.INVB[:, off + tt * 512:off + (tt + 1) * 512],
                                    start=(kc == 0 and ri == 0), stop=(kc == NKC - 1 and ri == 1)),
                           inc=(c.s_pe, 1) if (ccl == 1 and tt == 3 and ri == 1) else None,
                           waits=w)
        b.inv_guard[(kc % 2) * 2 + 0] = (c.s_pe, p.val(c.s_pe))
        b.inv_guard[(kc % 2) * 2 + 1] = (c.s_pe, p.val(c.s_pe))
    inv_mm = p.val(c.s_pe)
    for ccl in range(2):
        for tt in range(4):
            out_cb(ccl, tt, inv_banks[ccl * 4 + tt], [(c.s_pe, inv_mm)])


def _transpose_to(c, src_ap_fn, dst_ap_fn, n_tiles, ready, banks=None):
    p = c.p
    banks = banks or [c.PD[0], c.PD[1], c.PD[2], c.PD[3]]
    for i in range(n_tiles):
        bank = banks[i % len(banks)]
        g = c.bguard(bank)
        p.emit("tensor", lambda e, i=i, ba=bank: e.matmul(
            ba[:, 0:128], src_ap_fn(i), c.ID[:, :], start=True, stop=True),
            inc=(c.s_pe, 1),
            waits=[g, (c.s_ve, ready), c.id_ready] if i < len(banks) else [g])
        mm = p.val(c.s_pe)
        if i % 2 == 0:
            p.emit("vector", lambda e, i=i, ba=bank: e.tensor_copy(dst_ap_fn(i), ba[:, 0:128]),
                   inc=(c.s_ve, 1), waits=[(c.s_pe, mm)])
            c.set_bguard(bank, c.s_ve, p.val(c.s_ve))
        else:
            p.emit("scalar", lambda e, i=i, ba=bank: e.activation(
                out=dst_ap_fn(i), in_=ba[:, 0:128], func=AF.Identity),
                inc=(c.s_sc, 1), waits=[(c.s_pe, mm)])
            c.set_bguard(bank, c.s_sc, p.val(c.s_sc))
    p.emit("vector", lambda e: e.engine_nop(), inc=(c.s_ve, 1), waits=[(c.s_sc, p.val(c.s_sc))])
    return p.val(c.s_ve)


def _emit_filter_common(c):
    p = c.p
    f = Ctx()
    c._filt = f
    f.closers = []

    def sbsc(name, shape, dt=F32):
        h, cl = c.sb_scoped(name, shape, dt)
        f.closers.append(cl)
        return h

    pe_rev = c.din("l0a_pe_rev", (3, T))
    w1 = c.din("l0a_w1", (3, 64))
    w2 = c.din("l0a_w2", (64, 64))
    w3 = c.din("l0a_w3", (64, D))
    cols = c.din("l0a_fcols", (64, 4))
    trev = c.din("l0a_trev", (1, T))
    negd = c.din("l0a_negd", (1, D))
    b3r = c.din("l0a_b3r", (1, D))
    f.W3S = sbsc("f_w3", [64, D])
    f.COLS = sbsc("f_cols", [64, 8])
    f.TREV = sbsc("f_trev", [1, T])
    f.NEGD = sbsc("f_negd", [1, D])
    f.H2 = sbsc("f_h2", [64, T])
    f.B3BC = sbsc("f_b3bc", [128, D])
    f.FTA = [sbsc(f"f_fta{i}", [128, 256], BF16) for i in range(2)]
    f.FTB = [sbsc(f"f_ftb{i}", [128, 256], BF16) for i in range(2)]
    tcl = []
    def sbt(name, shape, dt=F32):
        h, cl = c.sb_scoped(name, shape, dt)
        tcl.append(cl)
        return h
    f.W1S = sbt("f_w1", [3, 64])
    f.W2S = sbt("f_w2", [64, 64])
    f.B3R = sbt("f_b3r", [1, D])
    f.PEV = sbt("f_pev", [3, T])
    f.H1 = sbt("f_h1", [64, T])
    flc = c.Burst("s_flc")
    for dst, src in [(f.PEV, pe_rev), (f.W1S, w1), (f.W2S, w2), (f.W3S, w3),
                     (f.TREV, trev), (f.NEGD, negd), (f.B3R, b3r)]:
        flc.load("sync", lambda e, d=dst, s=src: e.dma_start(out=d[:, :], in_=s))
    flc.load("sync", lambda e: e.dma_start(out=f.COLS[:, 0:4], in_=cols))
    ld = flc.done()
    p.emit("vector", lambda e: e.tensor_tensor(out=f.COLS[:, 4:5], in0=f.COLS[:, 0:1],
                                               in1=f.COLS[:, 1:2], op=ALU.mult),
           inc=(c.s_ve, 1), waits=[ld])
    p.emit("vector", lambda e: e.tensor_tensor(out=f.COLS[:, 5:6], in0=f.COLS[:, 2:3],
                                               in1=f.COLS[:, 3:4], op=ALU.mult), inc=(c.s_ve, 1))
    cols_done = p.val(c.s_ve)

    def sin_mlp(dst, rhs, lhsT, fcol, bfcol, extra):
        for tt in range(4):
            g = c.bguard(c.PA)
            p.emit("tensor", lambda e, t=tt, l=lhsT, r=rhs: e.matmul(
                c.PA[0:64, 0:512], l, r[:, t * 512:(t + 1) * 512], start=True, stop=True),
                inc=(c.s_pe, 1), waits=[g] + (extra if tt == 0 else []))
            mm = p.val(c.s_pe)
            sl = slice(tt * 512, (tt + 1) * 512)
            TMP = c.SCR1[0:64, 0:512]
            p.emit("vector", lambda e, sl=sl, fcol=fcol, bfcol=bfcol: e.tensor_scalar(
                out=dst[:, sl], in0=c.PA[0:64, 0:512], scalar1=fcol, scalar2=bfcol,
                op0=ALU.mult, op1=ALU.add), inc=(c.s_ve, 1), waits=[(c.s_pe, mm)])
            c.set_bguard(c.PA, c.s_ve, p.val(c.s_ve))
            p.emit("vector", lambda e, sl=sl: e.tensor_scalar(
                out=TMP, in0=dst[:, sl], scalar1=1.0 / (2 * math.pi), scalar2=None, op0=ALU.mult),
                inc=(c.s_ve, 1))
            p.emit("vector", lambda e: e.tensor_scalar(
                out=TMP, in0=TMP, scalar1=MAGIC, scalar2=MAGIC, op0=ALU.add, op1=ALU.subtract),
                inc=(c.s_ve, 1))
            p.emit("vector", lambda e, sl=sl: e.scalar_tensor_tensor(
                out=dst[:, sl], in0=TMP, scalar=-2 * math.pi, in1=dst[:, sl],
                op0=ALU.mult, op1=ALU.add), inc=(c.s_ve, 1))
            p.emit("vector", lambda e, sl=sl: e.tensor_scalar(
                out=dst[:, sl], in0=dst[:, sl], scalar1=math.pi, scalar2=-math.pi,
                op0=ALU.min, op1=ALU.max), inc=(c.s_ve, 1))
            rr = p.val(c.s_ve)
            p.emit("scalar", lambda e, sl=sl: e.activation(out=dst[:, sl], in_=dst[:, sl], func=AF.Sin),
                   inc=(c.s_sc, 1), waits=[(c.s_ve, rr)])

    sin_mlp(f.H1, f.PEV, f.W1S[0:3, :], f.COLS[:, 1:2], f.COLS[:, 4:5],
            [ld, (c.s_ve, cols_done)])
    h1_done = p.val(c.s_sc)
    sin_mlp(f.H2, f.H1, f.W2S[0:64, :], f.COLS[:, 3:4], f.COLS[:, 5:6], [(c.s_sc, h1_done)])
    f.h2_done = p.val(c.s_sc)
    for hh in range(2):
        g = c.bguard(c.PB)
        p.emit("tensor", lambda e, h=hh: e.matmul(
            c.PB[:, 0:512], c.ONESR[0:1, :], f.B3R[0:1, h * 512:(h + 1) * 512],
            start=True, stop=True), inc=(c.s_pe, 1), waits=[g, ld, (c.s_ve, c.ones_ready)])
        mm = p.val(c.s_pe)
        p.emit("vector", lambda e, h=hh: e.tensor_copy(f.B3BC[:, h * 512:(h + 1) * 512], c.PB[:, 0:512]),
               inc=(c.s_ve, 1), waits=[(c.s_pe, mm)])
        c.set_bguard(c.PB, c.s_ve, p.val(c.s_ve))
    f.b3_done = p.val(c.s_ve)
    p.barrier(c.all_sems)
    for cl in reversed(tcl):
        cl()


def _emit_filter_q(c, q, b):
    p = c.p
    f = c._filt
    sk_free = (c.s_pe, p.val(c.s_pe))
    mmbanks = [c.PA, c.PB]
    dbanks = [c.PD[2], c.PD[3]]
    prev_ta = [0, 0]
    prev_tb = [0, 0]
    prev_ker = [0, 0]
    for lt in range(16):
        ring = lt % 2
        mb = mmbanks[ring]
        db = dbanks[ring]
        gA = c.bguard(mb)
        gB = c.bguard(db)
        d0 = q * 256
        p.emit("tensor", lambda e, lt=lt, d0=d0, mb=mb: e.matmul(
            mb[:, 0:256], f.H2[:, lt * 128:(lt + 1) * 128], f.W3S[0:64, d0:d0 + 256],
            start=True, stop=True),
            inc=(c.s_pe, 1),
            waits=[gA] + ([(c.s_sc, f.h2_done)] if (q == 0 and lt == 0) else []))
        p.emit("tensor", lambda e, lt=lt, d0=d0, db=db: e.matmul(
            db[:, 0:256], f.TREV[0:1, lt * 128:(lt + 1) * 128], f.NEGD[0:1, d0:d0 + 256],
            start=True, stop=True), inc=(c.s_pe, 1), waits=[gB])
        mm = p.val(c.s_pe)
        TA = f.FTA[ring]
        TB = f.FTB[ring]
        p.emit("scalar", lambda e, db=db, TA=TA: e.activation(out=TA[:, :], in_=db[:, 0:256], func=AF.Exp),
               inc=(c.s_sc, 1), waits=[(c.s_pe, mm), (c.s_gp, prev_ker[ring])])
        sc_exp = p.val(c.s_sc)
        c.set_bguard(db, c.s_sc, sc_exp)
        p.emit("vector", lambda e, mb=mb, TB=TB, d0=d0: e.tensor_tensor(
            out=TB[:, :], in0=mb[:, 0:256], in1=f.B3BC[:, d0:d0 + 256], op=ALU.add),
            inc=(c.s_ve, 1),
            waits=[(c.s_pe, mm), (c.s_gp, prev_ker[ring])] + ([(c.s_ve, f.b3_done)] if (q == 0 and lt == 0) else []))
        tb_v = p.val(c.s_ve)
        c.set_bguard(mb, c.s_ve, tb_v)
        prev_ker[ring] = p.emit("gpsimd", lambda e, lt=lt, TA=TA, TB=TB: e.tensor_tensor(
            out=b.SK[:, lt * 512 + 256:(lt + 1) * 512], in0=TB[:, :], in1=TA[:, :], op=ALU.mult),
            inc=(c.s_gp, 1),
            waits=[(c.s_sc, sc_exp), (c.s_ve, tb_v)] + ([sk_free] if lt == 0 else []))
    p.emit("vector", lambda e: e.engine_nop(), inc=(c.s_ve, 1), waits=[(c.s_gp, p.val(c.s_gp))])
    return p.val(c.s_ve)


def _emit_conv_l0(c):
    p = c.p
    fbias = c.din("l0a_fbias", (128, 8))
    bout = c.din("l0a_bout", (128, 8))
    wout = c.din("l0a_wout", (D, D))
    _emit_filter_common(c)
    b = _alloc_conv_bundle(c, True)
    FB, fb_close = c.sb_scoped("l0fb", [128, 16])
    cfb = c.Burst("s_cfb")
    cfb.load("sync", lambda e: e.dma_start(out=FB[:, 0:8], in_=fbias))
    cfb.load("sync", lambda e: e.dma_start(out=FB[:, 8:16], in_=bout))
    fb_ld = cfb.done()
    xv = c.Burst("s_xv")
    x0b = c.Burst("s_x0b")

    wout_guard = [None]
    tr_guard = [None]
    yq_wout_guard = [0]
    wos = c.Stream("s_wo", 1)
    pending_wout = [None]

    def emit_wout(q, yq_tt, wout_ld):
        tmp2_rd = [0]
        for tt in range(4):
            for do in range(DC):
                bank = c.PD[(tt * DC + do) % 4]
                gb = c.bguard(bank)
                for ccl in range(2):
                    w = [gb] if ccl == 0 else []
                    if ccl == 0 and do == 0:
                        w += [(c.s_ve, max(yq_tt.get((0, tt), 0), yq_tt.get((1, tt), 0)))]
                    if ccl == 0 and tt == 0 and do == 0:
                        w += [wout_ld]
                    p.emit("tensor", lambda e, ccl=ccl, do=do, tt=tt, ba=bank: e.matmul(
                        ba[:, 0:512], b.WOUTB[:, ccl * D + do * 128:ccl * D + (do + 1) * 128],
                        b.X0Q[:, ccl * T + tt * 512:ccl * T + (tt + 1) * 512],
                        start=(ccl == 0), stop=(ccl == 1)),
                        inc=(c.s_pe, 1) if ccl == 1 else None, waits=w)
                mm = p.val(c.s_pe)
                if q == 3:
                    TMP2 = c.SCR1[:, 512:1024]
                    p.emit("scalar", lambda e, do=do, ba=bank: e.activation(
                        out=TMP2, in_=ba[:, 0:512], func=AF.Identity,
                        bias=FB[:, 8 + do:9 + do], scale=1.0),
                        inc=(c.s_sc, 1), waits=[(c.s_pe, mm), (c.s_ve, tmp2_rd[0])])
                    sv = p.val(c.s_sc)
                    p.emit("vector", lambda e, do=do, tt=tt: e.tensor_tensor(
                        out=c.X[do][:, tt * 512:(tt + 1) * 512],
                        in0=c.X[do][:, tt * 512:(tt + 1) * 512],
                        in1=TMP2, op=ALU.add),
                        inc=(c.s_ve, 1), waits=[(c.s_sc, sv)])
                    tmp2_rd[0] = p.val(c.s_ve)
                    c.set_bguard(bank, c.s_sc, sv)
                else:
                    p.emit("vector", lambda e, do=do, tt=tt, ba=bank: e.tensor_tensor(
                        out=c.X[do][:, tt * 512:(tt + 1) * 512],
                        in0=c.X[do][:, tt * 512:(tt + 1) * 512],
                        in1=ba[:, 0:512], op=ALU.add),
                        inc=(c.s_ve, 1), waits=[(c.s_pe, mm)])
                    c.set_bguard(bank, c.s_ve, p.val(c.s_ve))
        wout_guard[0] = (c.s_pe, p.val(c.s_pe))
        yq_wout_guard[0] = p.val(c.s_pe)

    for q in range(4):
        ker_ready = _emit_filter_q(c, q, b)
        g = tr_guard[0]
        for ccl in range(2):
            cc = 2 * q + ccl
            xv.load("sync", lambda e, cc=cc, ccl=ccl: e.dma_start(
                out=b.V2Q[:, ccl * T:(ccl + 1) * T], in_=c.v2_spill[cc * 128:(cc + 1) * 128, :]),
                waits=(list(c.spills_done) if (q == 0 and ccl == 0) else []) + ([g] if (g and ccl == 0) else []))
        sig_ld = xv.done()

        def src_fn(i):
            nt, ccl = divmod(i, 2)
            return b.V2Q[:, ccl * T + nt * 128:ccl * T + (nt + 1) * 128]

        def dst_fn(i):
            nt, ccl = divmod(i, 2)
            return b.SK[:, nt * 512 + ccl * 128:nt * 512 + (ccl + 1) * 128]

        p.emit("vector", lambda e: e.engine_nop(), inc=(c.s_ve, 1), waits=[sig_ld])
        sig_rdy = p.val(c.s_ve)
        sig_tm = _transpose_to(c, src_fn, dst_fn, 32, sig_rdy)
        tr_guard[0] = (c.s_pe, p.val(c.s_pe))

        if pending_wout[0] is not None:
            emit_wout(*pending_wout[0])
        g2 = yq_wout_guard[0]
        for ccl in range(2):
            cc = 2 * q + ccl
            x0b.load("sync", lambda e, cc=cc, ccl=ccl: e.dma_start(
                out=b.X0Q[:, ccl * T:(ccl + 1) * T], in_=c.x0_spill[cc * 128:(cc + 1) * 128, :]),
                waits=([(c.s_pe, g2)] if (ccl == 0 and g2) else []) + (list(c.spills_done) if (q == 0 and ccl == 0) else []))
        x0_ld = x0b.done()
        wout_ld = wos.load("gpsimd", 0, lambda e, q=q: e.dma_start(
            out=b.WOUTB.rearrange("pp (ccl do) -> pp ccl do", do=D),
            in_=wout[q * 256:(q + 1) * 256, :].rearrange("(ccl pp) do -> pp ccl do", pp=128)),
            waits=[wout_guard[0]] if wout_guard[0] else [])

        yq_tt = {}
        tmp_rd = [0]

        def out_cb(ccl, tt, bank, waits, q=q, x0_ld=x0_ld, yq_tt=yq_tt, tmp_rd=tmp_rd):
            TMP = c.SCR1[:, 0:512]
            p.emit("scalar", lambda e, ba=bank, q=q, ccl=ccl: e.activation(
                out=TMP, in_=ba[:, 0:512], func=AF.Identity,
                bias=FB[:, 2 * q + ccl:2 * q + ccl + 1], scale=1.0),
                inc=(c.s_sc, 1), waits=waits + [fb_ld, (c.s_ve, tmp_rd[0])])
            sv = p.val(c.s_sc)
            p.emit("vector", lambda e, ccl=ccl, tt=tt: e.tensor_tensor(
                out=b.X0Q[:, ccl * T + tt * 512:ccl * T + (tt + 1) * 512], in0=TMP,
                in1=b.X0Q[:, ccl * T + tt * 512:ccl * T + (tt + 1) * 512], op=ALU.mult),
                inc=(c.s_ve, 1), waits=[(c.s_sc, sv), x0_ld, (c.s_pe, yq_wout_guard[0])])
            c.set_bguard(bank, c.s_ve, p.val(c.s_ve))
            yq_tt[(ccl, tt)] = p.val(c.s_ve)
            tmp_rd[0] = p.val(c.s_ve)

        _conv_quarter(c, b, sig_tm, ker_ready, out_cb)
        pending_wout[0] = (q, yq_tt, wout_ld)
    emit_wout(*pending_wout[0])
    p.barrier(c.all_sems)
    fb_close()
    for cl in reversed(b.closers):
        cl()
    for cl in reversed(c._filt.closers):
        cl()


def _emit_omega_residual(c):
    p = c.p
    om = c.din("l1a_omega", (128, 8))
    OM, om_close = c.sb_scoped("l1om", [128, 8])
    omb = c.Burst("s_om")
    omb.load("sync", lambda e: e.dma_start(out=OM[:, :], in_=om))
    ld = omb.done()
    TMP = c.SCR1[:, 0:512]
    prev = 0
    for dc in range(DC):
        for tt in range(4):
            w = [(c.s_ve, prev)]
            if dc == 0 and tt == 0:
                w += [ld, (c.s_ve, c.xn_ready)]
            p.emit("scalar", lambda e, dc=dc, tt=tt, xn=c.XN[dc]: e.activation(
                out=TMP, in_=xn[:, tt * 512:(tt + 1) * 512],
                func=AF.Identity, bias=c.ONES[:, 3:4], scale=OM[:, dc:dc + 1]),
                inc=(c.s_sc, 1), waits=w)
            sv = p.val(c.s_sc)
            p.emit("vector", lambda e, dc=dc, tt=tt: e.tensor_tensor(
                out=c.X[dc][:, tt * 512:(tt + 1) * 512],
                in0=c.X[dc][:, tt * 512:(tt + 1) * 512],
                in1=TMP, op=ALU.add),
                inc=(c.s_ve, 1), waits=[(c.s_sc, sv)])
            prev = p.val(c.s_ve)
    c.omega_done = prev
    p.barrier(c.all_sems)
    for _e in ENGINES:
        p.pending[_e] = p.pending[_e] + [ld]
    om_close()


def _emit_ffn_l1(c):
    p = c.p
    XN = c.XN
    p.barrier(c.all_sems)
    E, M = 9, 2048
    PIECE, MSL, NBUF = 8, 2, 2
    NP = (M // 128) // PIECE
    NSL = PIECE // MSL

    gw = c.din("l1f_gate9", (E, D, M))
    uw = c.din("l1f_up9", (E, D, M))
    dw = c.din("l1f_down9", (E, M, D))

    closers = []

    def sbsc(name, shape, dt=F32):
        h, cl = c.sb_scoped(name, shape, dt)
        closers.append(cl)
        return h

    GB = [sbsc(f"l1gb{i}", [128, DC * MSL * 128], BF16) for i in range(NBUF)]
    UB = [sbsc(f"l1ub{i}", [128, DC * MSL * 128], BF16) for i in range(NBUF)]
    DP = sbsc("l1dp", [128, PIECE * D], BF16)
    HP = [sbsc(f"l1hp{i}", [128, T], BF16) for i in range(PIECE)]
    WBC = [sbsc(f"l1wbc{i}", [128, T], BF16) for i in range(2)]
    SG = [sbsc(f"l1sg{i}", [128, 512], BF16) for i in range(2)]

    guards = {"g": {}, "u": {}}
    cnts = {"g": [0], "u": [0]}
    slab_ld_val = {"g": {}, "u": {}}
    gstr = c.Stream("s_gs", 2)
    ustr = c.Stream("s_us", 2)
    dstr = c.Stream("s_dpw", 1)

    def load_slab(kind, e_, mp, sl):
        src_t = gw if kind == "g" else uw
        bufs = GB if kind == "g" else UB
        i = cnts[kind][0]
        buf = bufs[i % NBUF]
        m0 = (mp * PIECE + sl * MSL) * 128
        src = src_t[e_, :, m0:m0 + MSL * 128].rearrange("(dc pp) m -> pp dc m", pp=128)
        dst = buf.rearrange("pp (dc m) -> pp dc m", m=MSL * 128)
        g = guards[kind].get(i % NBUF)
        st = gstr if kind == "g" else ustr
        slab_ld_val[kind][i] = st.load("gpsimd", i % NBUF, lambda en: en.dma_start(out=dst, in_=src),
                                       waits=[g] if g else [])
        cnts[kind][0] += 1
        return i

    dpg = [None]

    def load_dp(e_, mp):
        src = dw[e_, mp * PIECE * 128:(mp + 1) * PIECE * 128, :].rearrange("(mi pp) h -> pp mi h", pp=128)
        dst = DP.rearrange("pp (mi h) -> pp mi h", h=D)
        g = dpg[0]
        return dstr.load("gpsimd", 0, lambda en: en.dma_start(out=dst, in_=src),
                         waits=[g] if g else [])

    order = [(e_, mp) for e_ in range(E) for mp in range(NP)]
    sids = {}
    for k in range(NBUF):
        sids[("g", 0, 0, k)] = load_slab("g", 0, 0, k)
        sids[("u", 0, 0, k)] = load_slab("u", 0, 0, k)

    ph_guard = [0, 0]
    pu_guard = [0, 0]
    pdown_guard = {2: 0, 3: 0}
    sg_guard = [0, 0]
    h_guard = 0
    wbc_done = {}

    for oi, (e_, mp) in enumerate(order):
        gated = e_ < 8
        if mp == 0 and gated:
            wbc_done[e_ % 2] = _mk_wbc(c, e_, WBC[e_ % 2])
        dp_ld = load_dp(e_, mp)
        for mc_i in range(PIECE):
            sl, mi = divmod(mc_i, MSL)
            gsid = sids[("g", e_, mp, sl)]
            usid = sids[("u", e_, mp, sl)]
            for tt in range(4):
                bank = (mc_i * 4 + tt) % 2
                w_g = [(c.s_sc, ph_guard[bank])]
                w_u = [(c.s_ve, pu_guard[bank])]
                if mi == 0 and tt == 0:
                    w_g.append(slab_ld_val["g"][gsid])
                    w_u.append(slab_ld_val["u"][usid])
                if oi == 0 and mc_i == 0 and tt == 0:
                    w_g.append((c.s_ve, c.xn_ready))
                for kc in range(DC):
                    lhsT = GB[gsid % NBUF][:, (kc * MSL + mi) * 128:(kc * MSL + mi + 1) * 128]
                    p.emit("tensor", lambda en, l=lhsT, k=kc, b=bank, t=tt:
                           en.matmul(c.PH[b][:, :], l, XN[k][:, t * 512:(t + 1) * 512],
                                     start=(k == 0), stop=(k == DC - 1)),
                           inc=(c.s_pe, 1) if kc == DC - 1 else None,
                           waits=w_g if kc == 0 else ())
                g_mm = p.val(c.s_pe)
                for kc in range(DC):
                    lhsT = UB[usid % NBUF][:, (kc * MSL + mi) * 128:(kc * MSL + mi + 1) * 128]
                    p.emit("tensor", lambda en, l=lhsT, k=kc, b=bank, t=tt:
                           en.matmul(c.PD[b][:, :], l, XN[k][:, t * 512:(t + 1) * 512],
                                     start=(k == 0), stop=(k == DC - 1)),
                           inc=(c.s_pe, 1) if kc == DC - 1 else None,
                           waits=w_u if kc == 0 else ())
                u_mm = p.val(c.s_pe)
                p.emit("scalar", lambda en, b=bank: en.activation(
                    out=SG[b][:, :], in_=c.PH[b][:, :], func=AF.Silu),
                    inc=(c.s_sc, 1), waits=[(c.s_pe, g_mm), (c.s_ve, sg_guard[bank])])
                sg_v = p.val(c.s_sc)
                ph_guard[bank] = sg_v
                w_h = [(c.s_sc, sg_v), (c.s_pe, u_mm)]
                if mc_i == 0 and tt == 0:
                    w_h.append((c.s_pe, h_guard))
                if gated:
                    p.emit("vector", lambda en, b=bank: en.tensor_tensor(
                        out=c.SCB[0][:, 0:512], in0=c.PD[b][:, :], in1=SG[b][:, :], op=ALU.mult),
                        inc=(c.s_ve, 1), waits=w_h)
                    p.emit("vector", lambda en, m=mc_i, t=tt, b=e_ % 2: en.tensor_tensor(
                        out=HP[m][:, t * 512:(t + 1) * 512], in0=c.SCB[0][:, 0:512],
                        in1=WBC[b][:, t * 512:(t + 1) * 512], op=ALU.mult),
                        inc=(c.s_ve, 1),
                        waits=[(c.s_ve, wbc_done[e_ % 2])] if (mc_i == 0 and tt == 0) else ())
                else:
                    p.emit("vector", lambda en, m=mc_i, t=tt, b=bank: en.tensor_tensor(
                        out=HP[m][:, t * 512:(t + 1) * 512], in0=c.PD[b][:, :], in1=SG[b][:, :], op=ALU.mult),
                        inc=(c.s_ve, 1), waits=w_h)
                vv = p.val(c.s_ve)
                pu_guard[bank] = vv
                sg_guard[bank] = vv
            if mi == MSL - 1:
                guards["g"][gsid % NBUF] = (c.s_pe, p.val(c.s_pe))
                guards["u"][usid % NBUF] = (c.s_pe, p.val(c.s_pe))
                gidx = oi * NSL + sl + NBUF
                if gidx < len(order) * NSL:
                    noi, nsl = divmod(gidx, NSL)
                    ne, nmp = order[noi]
                    sids[("g", ne, nmp, nsl)] = load_slab("g", ne, nmp, nsl)
                    sids[("u", ne, nmp, nsl)] = load_slab("u", ne, nmp, nsl)
        h_done = p.val(c.s_ve)
        for tt in range(4):
            for dc in range(DC):
                bank = 2 + (tt * DC + dc) % 2
                waits = [(c.s_ve, pdown_guard[bank])]
                if tt == 0 and dc == 0:
                    waits += [(c.s_ve, h_done), dp_ld]
                for mc_i in range(PIECE):
                    p.emit("tensor", lambda en, l=DP[:, mc_i * D + dc * 128:mc_i * D + (dc + 1) * 128],
                           m=mc_i, b=bank, t=tt:
                           en.matmul(c.PD[b][:, :], l, HP[m][:, t * 512:(t + 1) * 512],
                                     start=(m == 0), stop=(m == PIECE - 1)),
                           inc=(c.s_pe, 1) if mc_i == PIECE - 1 else None,
                           waits=waits if mc_i == 0 else ())
                mm = p.val(c.s_pe)
                p.emit("vector", lambda en, d=dc, b=bank, t=tt: en.tensor_tensor(
                    out=c.X[d][:, t * 512:(t + 1) * 512],
                    in0=c.X[d][:, t * 512:(t + 1) * 512],
                    in1=c.PD[b][:, :], op=ALU.add),
                    inc=(c.s_ve, 1), waits=[(c.s_pe, mm)])
                pdown_guard[bank] = p.val(c.s_ve)
        h_guard = p.val(c.s_pe)
        dpg[0] = (c.s_pe, h_guard)
    p.barrier(c.all_sems)
    for cl in reversed(closers):
        cl()
    for cl in reversed(c._router_closers):
        cl()


def _emit_kergen_common(c):
    p = c.p
    f = Ctx()
    c._l1k = f
    f.closers = []

    def sbsc(name, shape, dt=F32):
        h, cl = c.sb_scoped(name, shape, dt)
        f.closers.append(cl)
        return h

    delta = c.din("l1a_delta_t", (128, 128))
    alpha = c.din("l1a_alpha_t", (128, 128))
    betap = c.din("l1a_beta_t", (128, 128))
    gamma = c.din("l1a_gamma_t", (128, 128))
    mask16 = c.din("c_mask16", (128, 16, 128), BF16)
    f.DSB = sbsc("l1k_d", [128, 128])
    f.ASB = sbsc("l1k_a", [128, 128])
    f.BSB = sbsc("l1k_b", [128, 128])
    f.GSB = sbsc("l1k_g", [128, 128])
    f.LQ = sbsc("l1k_lq", [128, 128])
    f.CC = sbsc("l1k_c", [128, 128])
    f.MSK = sbsc("l1k_msk", [128, 16 * 128], BF16)
    f.ABT = [sbsc(f"l1k_abt{i}", [128, 96]) for i in range(4)]
    f.LH2 = [sbsc(f"l1k_lh{i}", [128, 128], BF16) for i in range(4)]
    f.J32 = sbsc("l1k_j32", [128, 32])
    f.I64 = sbsc("l1k_i64", [128, 64])
    f.TGRID = sbsc("l1k_tg", [128, 2048])
    kg = c.Burst("s_kg")
    for dst, src in [(f.DSB, delta), (f.ASB, alpha), (f.BSB, betap), (f.GSB, gamma)]:
        kg.load("sync", lambda e, d=dst, s=src: e.dma_start(out=d[:, :], in_=s))
    kg.load("sync", lambda e: e.dma_start(
        out=f.MSK.rearrange("pp (l cc) -> pp l cc", cc=128), in_=mask16))
    ld = kg.done()
    p.emit("gpsimd", lambda e: e.iota(f.J32[:, :], pattern=[[1, 32]], base=0,
                                      channel_multiplier=0, allow_small_or_imprecise_dtypes=True),
           inc=(c.s_gp, 1))
    p.emit("gpsimd", lambda e: e.iota(f.I64[:, :], pattern=[[1, 64]], base=0,
                                      channel_multiplier=0, allow_small_or_imprecise_dtypes=True),
           inc=(c.s_gp, 1))
    p.emit("gpsimd", lambda e: e.iota(f.TGRID[:, :], pattern=[[1, 2048]], base=0,
                                      channel_multiplier=0, allow_small_or_imprecise_dtypes=True),
           inc=(c.s_gp, 1))
    f.iota_done = p.val(c.s_gp)
    p.emit("scalar", lambda e: e.activation(out=f.DSB[:, :], in_=f.DSB[:, :], func=AF.Sigmoid),
           inc=(c.s_sc, 1), waits=[ld])
    p.emit("scalar", lambda e: e.activation(out=f.ASB[:, :], in_=f.ASB[:, :], func=AF.Sigmoid),
           inc=(c.s_sc, 1))
    sg = p.val(c.s_sc)
    p.emit("vector", lambda e: e.tensor_tensor(out=f.LQ[:, :], in0=f.DSB[:, :], in1=f.ASB[:, :],
                                               op=ALU.mult), inc=(c.s_ve, 1), waits=[(c.s_sc, sg)])
    p.emit("vector", lambda e: e.tensor_scalar(out=f.LQ[:, :], in0=f.LQ[:, :], scalar1=-1.0,
                                               scalar2=1.0, op0=ALU.mult, op1=ALU.add), inc=(c.s_ve, 1))
    lq_v = p.val(c.s_ve)
    p.emit("scalar", lambda e: e.activation(out=f.LQ[:, :], in_=f.LQ[:, :], func=AF.Ln),
           inc=(c.s_sc, 1), waits=[(c.s_ve, lq_v)])
    ln_v = p.val(c.s_sc)
    p.emit("vector", lambda e: e.tensor_tensor(out=f.CC[:, :], in0=f.DSB[:, :], in1=f.BSB[:, :],
                                               op=ALU.mult), inc=(c.s_ve, 1), waits=[(c.s_sc, ln_v)])
    p.emit("vector", lambda e: e.tensor_tensor(out=f.CC[:, :], in0=f.CC[:, :], in1=f.GSB[:, :],
                                               op=ALU.mult), inc=(c.s_ve, 1))
    f.pre_done = p.val(c.s_ve)


def _emit_kergen_l1_q(c, q, b):
    p = c.p
    f = c._l1k
    kt_guard = (c.s_pe, p.val(c.s_pe))
    EBoff = [i * 2048 for i in range(4)]
    KT = [b.INVB[:, 0:2048], b.INVB[:, 2048:4096]]
    eb_guard = {i: (c.s_pe, p.val(c.s_pe)) for i in range(4)}
    lh_guard = [0, 0, 0, 0]
    ab_guard = [None, None, None, None]
    steps = [(chl, ci_l) for chl in range(2) for ci_l in range(16)]
    adone = {}

    def emitA(idx):
        chl, ci_l = steps[idx]
        if ci_l % 4 == 1:
            adone[idx] = None
            return
        ci = (2 * q + chl) * 16 + ci_l
        ring = ci_l % 4
        w = [ab_guard[ring]] if ab_guard[ring] else []
        if idx == 0:
            w += [(c.s_ve, f.pre_done), (c.s_gp, f.iota_done)]
        p.emit("gpsimd", lambda e, ci=ci, r=ring: e.tensor_scalar(
            out=f.ABT[r][:, 0:32], in0=f.J32[:, :], scalar1=f.LQ[:, ci:ci + 1], scalar2=64.0,
            op0=ALU.mult, op1=ALU.mult), inc=(c.s_gp, 1), waits=w)
        p.emit("gpsimd", lambda e, ci=ci, r=ring: e.tensor_scalar(
            out=f.ABT[r][:, 32:96], in0=f.I64[:, :], scalar1=f.LQ[:, ci:ci + 1], scalar2=None,
            op0=ALU.mult), inc=(c.s_gp, 1))
        vv = p.val(c.s_gp)
        p.emit("scalar", lambda e, r=ring: e.activation(out=f.ABT[r][:, :], in_=f.ABT[r][:, :],
                                                        func=AF.Exp),
               inc=(c.s_sc, 1), waits=[(c.s_gp, vv)])
        adone[idx] = p.val(c.s_sc)

    def emitB(idx):
        chl, ci_l = steps[idx]
        ci = (2 * q + chl) * 16 + ci_l
        lb = ci % 16
        ring = ci_l % 4
        ebuf = b.FWDB[:, EBoff[ci_l % 4]:EBoff[ci_l % 4] + 2048]
        g = eb_guard.get(ci_l % 4)
        e_done = adone[idx]
        if ci_l % 4 == 1:
            w = ([g] if g else [])
            if idx == 1:
                w += [(c.s_ve, f.pre_done), (c.s_gp, f.iota_done)]
            p.emit("scalar", lambda e, ci=ci, eb=ebuf: e.activation(
                out=eb, in_=f.TGRID[:, :], func=AF.Exp, scale=f.LQ[:, ci:ci + 1]),
                inc=(c.s_sc, 1), waits=w)
            eg = (c.s_sc, p.val(c.s_sc))
        else:
            et_eng = "gpsimd" if ci_l % 4 == 0 else "vector"
            et_sem = c.s_gp if ci_l % 4 == 0 else c.s_ve
            p.emit(et_eng, lambda e, r=ring, eb=ebuf: e.tensor_tensor(
                out=eb.rearrange("pp (a bb) -> pp a bb", bb=64),
                in0=f.ABT[r][:, 0:32].unsqueeze(-1).broadcast_to([128, 32, 64]),
                in1=f.ABT[r][:, 32:96].unsqueeze(1).broadcast_to([128, 32, 64]),
                op=ALU.mult),
                inc=(et_sem, 1), waits=[(c.s_sc, e_done)] + ([g] if g else []))
            eg = (et_sem, p.val(et_sem))
            ab_guard[ring] = eg
        p.emit("vector", lambda e, lb=lb, ci=ci, r=ring: e.tensor_scalar(
            out=f.LH2[r][:, :], in0=f.MSK[:, lb * 128:(lb + 1) * 128], scalar1=f.CC[:, ci:ci + 1],
            scalar2=0.25, op0=ALU.mult, op1=ALU.mult),
            inc=(c.s_ve, 1), waits=[(c.s_pe, lh_guard[ring])])
        lh_done = p.val(c.s_ve)
        for tt in range(4):
            bank = c.PD[tt]
            w = [(c.s_ve, lh_done), eg]
            if ci_l == 0:
                w.append(c.bguard(bank))
            p.emit("tensor", lambda e, t=tt, eb=ebuf, ba=bank, ci_l=ci_l, r=ring: e.matmul(
                ba[:, :], f.LH2[r][:, :], eb[:, t * 512:(t + 1) * 512],
                start=(ci_l == 0), stop=(ci_l == 15)),
                inc=(c.s_pe, 1) if tt == 3 else None,
                waits=w)
        eb_guard[ci_l % 4] = (c.s_pe, p.val(c.s_pe))
        lh_guard[ring] = p.val(c.s_pe)
        if ci_l == 15:
            mm = p.val(c.s_pe)
            for tt in range(4):
                bank = c.PD[tt]
                p.emit("vector", lambda e, t=tt, chl=chl, ba=bank: e.tensor_copy(
                    KT[chl][:, t * 512:(t + 1) * 512], ba[:, :]),
                    inc=(c.s_ve, 1), waits=[(c.s_pe, mm), kt_guard])
                c.set_bguard(bank, c.s_ve, p.val(c.s_ve))

    emitA(0)
    for idx in range(32):
        if idx + 1 < 32:
            emitA(idx + 1)
        emitB(idx)
    ktfm_done = p.val(c.s_ve)

    def src_fn(i):
        nt, chl = divmod(i, 2)
        return KT[chl][:, nt * 128:(nt + 1) * 128]

    def dst_fn(i):
        nt, chl = divmod(i, 2)
        return b.SK[:, nt * 512 + 256 + chl * 128:nt * 512 + 256 + (chl + 1) * 128]

    return _transpose_to(c, src_fn, dst_fn, 32, ktfm_done, banks=[c.PA, c.PB, c.PD[2], c.PD[3]])


def _emit_conv_l1(c):
    p = c.p
    p.barrier(c.all_sems)
    _emit_kergen_common(c)
    b = _alloc_conv_bundle(c, False)
    for q in range(4):
        ker_ready = _emit_kergen_l1_q(c, q, b)
        for _i in range(4):
            b.fwd_guard[_i] = (c.s_pe, c.p.val(c.s_pe))
            b.inv_guard[_i] = (c.s_pe, c.p.val(c.s_pe))
        XN = c.XN

        def src_fn(i, q=q, XN=XN):
            nt, ccl = divmod(i, 2)
            return XN[2 * q + ccl][:, nt * 128:(nt + 1) * 128]

        def dst_fn(i):
            nt, ccl = divmod(i, 2)
            return b.SK[:, nt * 512 + ccl * 128:nt * 512 + (ccl + 1) * 128]

        sig_tm = _transpose_to(c, src_fn, dst_fn, 32, ker_ready, banks=[c.PA, c.PB, c.PD[2], c.PD[3]])

        def out_cb(ccl, tt, bank, waits, q=q):
            cc = 2 * q + ccl
            p.emit("vector", lambda e, cc=cc, tt=tt, ba=bank: e.tensor_tensor(
                out=c.X[cc][:, tt * 512:(tt + 1) * 512],
                in0=c.X[cc][:, tt * 512:(tt + 1) * 512],
                in1=ba[:, 0:512], op=ALU.add),
                inc=(c.s_ve, 1), waits=waits)
            c.set_bguard(bank, c.s_ve, p.val(c.s_ve))

        _conv_quarter(c, b, sig_tm, ker_ready, out_cb)
    p.barrier(c.all_sems)
    for cl in reversed(b.closers):
        cl()
    for cl in reversed(c._l1k.closers):
        cl()


def _prep_inputs(inputs):
    cc = host_consts()
    asf = lambda a: np.ascontiguousarray(np.asarray(a), dtype=np.float32)

    common = {"c_fwd": cc["fwd_t"], "c_inv": cc["inv_t"], "c_id": cc["ident"],
              "c_idf": np.eye(128, dtype=np.float32), "c_sel": cc["sel"]}

    d0 = dict(common)
    w1 = asf(inputs["L0_ffn_W1"])
    d0["l0f_w1t"] = np.ascontiguousarray(w1.transpose(0, 2, 1))
    d0["l0f_w2t"] = np.ascontiguousarray(asf(inputs["L0_ffn_W2"]).transpose(0, 2, 1))
    b1 = asf(inputs["L0_ffn_b1"])
    d0["l0f_b1c"] = np.ascontiguousarray(b1.reshape(8, 32, 128).transpose(2, 0, 1).reshape(128, 256))
    d0["l0f_b2"] = asf(inputs["L0_ffn_b2"])
    d0["l0_wr_t"] = np.ascontiguousarray(asf(inputs["L0_ffn_Wr"]).T)
    d0["l0a_win"] = asf(inputs["L0_attn_W_in"])
    d0["l0a_binc"] = np.ascontiguousarray(asf(inputs["L0_attn_b_in"]).reshape(24, 128).T)
    sw = asf(inputs["L0_attn_short_w"])
    d0["l0a_swc"] = np.ascontiguousarray(sw.reshape(24, 128, 3).transpose(1, 0, 2).reshape(128, 72))
    d0["l0a_sbc"] = np.ascontiguousarray(asf(inputs["L0_attn_short_b"]).reshape(24, 128).T)
    d0["l0a_pe_rev"] = cc["pe_rev"]
    d0["l0a_w1"] = asf(inputs["L0_attn_W1"])
    d0["l0a_w2"] = asf(inputs["L0_attn_W2"])
    d0["l0a_w3"] = asf(inputs["L0_attn_W3"])
    d0["l0a_fcols"] = np.ascontiguousarray(np.stack([
        asf(inputs["L0_attn_b1"]), asf(inputs["L0_attn_f1"]),
        asf(inputs["L0_attn_b2"]), asf(inputs["L0_attn_f2"])], axis=1))
    d0["l0a_trev"] = cc["t_rev"]
    d0["l0a_negd"] = cc["negdeltas"]
    d0["l0a_b3r"] = asf(inputs["L0_attn_b3"])[None, :]
    d0["l0a_fbias"] = np.ascontiguousarray(asf(inputs["L0_attn_filter_bias"]).reshape(8, 128).T)
    d0["l0a_bout"] = np.ascontiguousarray(asf(inputs["L0_attn_b_out"]).reshape(8, 128).T)
    d0["l0a_wout"] = asf(inputs["L0_attn_W_out"])

    d1 = dict(common)
    d1["l1_gate_t"] = np.ascontiguousarray(asf(inputs["L1_ffn_gate_w"]).T)
    d1["l1_beta_r16"] = np.ascontiguousarray(np.tile(asf(inputs["L1_ffn_beta"]), 16))[None, :]
    d1["l1f_gate9"] = np.ascontiguousarray(np.concatenate(
        [asf(inputs["L1_ffn_gate_proj_w"]), asf(inputs["L1_ffn_shared_gate_w"])], axis=0))
    d1["l1f_up9"] = np.ascontiguousarray(np.concatenate(
        [asf(inputs["L1_ffn_up_proj_w"]), asf(inputs["L1_ffn_shared_up_w"])], axis=0))
    d1["l1f_down9"] = np.ascontiguousarray(np.concatenate(
        [asf(inputs["L1_ffn_down_proj_w"]), asf(inputs["L1_ffn_shared_down_w"])], axis=0))
    d1["l1a_omega"] = np.ascontiguousarray(asf(inputs["L1_attn_omega"]).reshape(8, 128).T)

    def dn_tile(a):
        return np.ascontiguousarray(asf(a).reshape(128, 128).T)

    d1["l1a_delta_t"] = dn_tile(inputs["L1_attn_delta"])
    d1["l1a_alpha_t"] = dn_tile(inputs["L1_attn_alpha"])
    d1["l1a_beta_t"] = dn_tile(inputs["L1_attn_beta"])
    d1["l1a_gamma_t"] = dn_tile(inputs["L1_attn_gamma"])
    d1["c_mask16"] = cc["mask16"]
    return d0, d1


_BUILD_CACHE = {}


def _get_program(layers):
    key = (tuple(layers), tuple(LIMIT_PHASES) if LIMIT_PHASES is not None else None)
    if key not in _BUILD_CACHE:
        _BUILD_CACHE[key] = build_model(layers)
    return _BUILD_CACHE[key]


def kernel(**inputs):
    d0, d1 = _prep_inputs(inputs)
    x = np.ascontiguousarray(np.asarray(inputs["L0_attn_x"], dtype=np.float32))
    n = 8

    if KNOB("K_UNFUSED"):
        nc0, names0 = _get_program((0,))
        nc1, names1 = _get_program((1,))
        in_maps = []
        for bb in range(n):
            m = {k: v for k, v in d0.items() if k in names0}
            m["x_fm"] = np.ascontiguousarray(x[bb].T)
            in_maps.append(m)
        res0 = run_bass_kernel_spmd(nc0, in_maps, core_ids=list(range(n)))
        in_maps = []
        for bb in range(n):
            m = {k: v for k, v in d1.items() if k in names1}
            m["x_fm"] = np.ascontiguousarray(res0.results[bb]["y_fm"])
            in_maps.append(m)
        res1 = run_bass_kernel_spmd(nc1, in_maps, core_ids=list(range(n)))
        out = np.stack([res1.results[bb]["y_fm"].T for bb in range(n)], axis=0)
        return np.ascontiguousarray(out, dtype=np.float32)

    nc01, names = _get_program((0, 1))
    dall = dict(d0)
    dall.update(d1)
    in_maps = []
    for bb in range(n):
        m = {k: v for k, v in dall.items() if k in names}
        m["x_fm"] = np.ascontiguousarray(x[bb].T)
        in_maps.append(m)
    res = run_bass_kernel_spmd(nc01, in_maps, core_ids=list(range(n)))
    out = np.stack([res.results[bb]["y_fm"].T for bb in range(n)], axis=0)
    return np.ascontiguousarray(out, dtype=np.float32)


# version 14 (final)
# speedup vs baseline: 1.0083x; 1.0083x over previous
import contextlib
import math
import numpy as np
import ml_dtypes

import concourse.bass as bass
import concourse.mybir as mybir
from concourse.bass_utils import run_bass_kernel_spmd

F32 = mybir.dt.float32
BF16 = mybir.dt.bfloat16
AF = mybir.ActivationFunctionType
ALU = mybir.AluOpType
AX = mybir.AxisListType

T = 2048
D = 1024
DC = D // 128
NKC = 17
NFFT = 4096
MAGIC = 1.5 * 2.0 ** 23
ENGINES = ("tensor", "vector", "scalar", "gpsimd", "sync")
LIMIT_PHASES = None


def _ph(name):
    return LIMIT_PHASES is None or name in LIMIT_PHASES


import os as _os
KNOB = lambda n: _os.environ.get(n, "") == "1"


class Prog:
    def __init__(self, nc, es):
        self.nc = nc
        self.es = es
        self.q = {e: [] for e in ENGINES}
        self.semv = {}
        self.pending = {e: [] for e in ENGINES}

    def sem(self, name):
        h = self.es.enter_context(self.nc.semaphore(name))
        self.semv[id(h)] = 0
        return h

    def val(self, sem):
        return self.semv[id(sem)]

    def barrier(self, sems):
        waits = [(s, self.val(s)) for s in sems]
        for e in ENGINES:
            self.pending[e] = self.pending[e] + list(waits)

    def set_chain(self, eng_sems):
        self.chain = dict(eng_sems)

    def emit(self, eng, fn, inc=None, waits=()):
        waits = list(self.pending[eng]) + [(s, v) for (s, v) in waits if v > 0]
        self.pending[eng] = []
        ch = getattr(self, "chain", {}).get(eng)
        if ch is not None and (inc is None or inc[0] is ch):
            waits.append((ch, self.semv[id(ch)]))
            if inc is None:
                inc = (ch, 1)
        waits = [(s, v) for (s, v) in waits if v > 0]

        def item(engine):
            for (s, v) in waits:
                engine.wait_ge(s, v)
            inst = fn(engine)
            if inc is not None:
                inst.then_inc(inc[0], inc[1])

        self.q[eng].append(item)
        if inc is not None:
            self.semv[id(inc[0])] += inc[1]
            return self.semv[id(inc[0])]
        return None

    def wait(self, eng, sem, value):
        if value <= 0:
            return

        def item(engine):
            engine.wait_ge(sem, value)

        self.q[eng].append(item)

    def finalize(self):
        blk = self.es.enter_context(self.nc.Block())
        for e in ENGINES:
            items = self.q[e]
            if not items:
                continue

            def mk(its):
                def body(engine):
                    for it in its:
                        it(engine)
                return body

            getattr(blk, e)(mk(items))
        return blk


_CONST_CACHE = {}


def host_consts():
    if _CONST_CACHE:
        return _CONST_CACHE
    bf = ml_dtypes.bfloat16
    n = np.arange(T, dtype=np.float64)
    k = np.arange(NKC * 128, dtype=np.float64)
    ang = 2.0 * np.pi * np.outer(n, k) / NFFT
    fwd_cos = np.cos(ang)
    fwd_sin = -np.sin(ang)

    def tile_fwd(m):
        r = m.reshape(16, 128, NKC, 128)
        return np.ascontiguousarray(r.transpose(2, 1, 0, 3)).astype(bf)

    fwd_t = np.stack([tile_fwd(fwd_cos), tile_fwd(fwd_sin)])

    w = np.full(NKC * 128, 2.0 / NFFT)
    w[0] = 1.0 / NFFT
    w[2048] = 1.0 / NFFT
    w[2049:] = 0.0
    tau = np.arange(T, dtype=np.float64)
    ang2 = 2.0 * np.pi * np.outer(k, tau) / NFFT
    inv_cos = w[:, None] * np.cos(ang2)
    inv_sin = -w[:, None] * np.sin(ang2)

    def tile_inv(m):
        return np.ascontiguousarray(m.reshape(NKC, 128, T)).astype(bf)

    inv_t = np.stack([tile_inv(inv_cos), tile_inv(inv_sin)])

    ident = np.eye(128, dtype=np.float32).astype(bf)

    sel = np.zeros((8, 8 * 128), np.float32)
    for e in range(8):
        sel[e, e * 128:(e + 1) * 128] = 1.0
    sel = sel.astype(bf)

    mask16 = np.zeros((16, 128, 128), np.float32)
    for lb in range(16):
        for pp in range(128):
            mask16[lb, pp, lb * 8 + pp // 16] = 1.0
    mask16 = np.ascontiguousarray(mask16.transpose(1, 0, 2)).astype(bf)

    l = np.arange(T, dtype=np.float64)
    t_pos = l / (T - 1)
    phase = 1e-4 * (2.0 * np.pi * l / T)
    pe = np.stack([t_pos, np.cos(phase), -np.sin(phase)], axis=0)
    pe_rev = np.ascontiguousarray(pe[:, ::-1]).astype(np.float32)
    t_rev = np.ascontiguousarray(t_pos[::-1])[None, :].astype(np.float32)
    dmax, dmin = (math.log(0.01) / 0.3, math.log(0.01) / 1.5)
    deltas = np.abs(np.linspace(dmax, dmin, D)).astype(np.float32)
    negdeltas = np.ascontiguousarray((-deltas)[None, :])

    _CONST_CACHE.update(dict(
        fwd_t=fwd_t, inv_t=inv_t, ident=ident, sel=sel, mask16=mask16,
        pe_rev=pe_rev, t_rev=t_rev, negdeltas=negdeltas))
    return _CONST_CACHE


class Ctx:
    pass


def build_model(layers=(0, 1)):
    nc = bass.Bass("TRN2", target_bir_lowering=False)
    es = contextlib.ExitStack()
    p = Prog(nc, es)
    c = Ctx()
    c.nc, c.es, c.p = nc, es, p
    c.input_names = []

    def din(name, shape, dt=F32):
        c.input_names.append(name)
        return nc.dram_tensor(name, list(shape), dt, kind="ExternalInput").ap()

    c.din = din
    c.sb = lambda name, shape, dt=F32: es.enter_context(nc.sbuf_tensor(name, shape, dt))
    c.ps = lambda name, shape, dt=F32: es.enter_context(nc.psum_tensor(name, shape, dt))

    c._uid = [0]

    def sb_scoped(name, shape, dt=F32):
        c._uid[0] += 1
        cm = nc.sbuf_tensor(f"{name}_u{c._uid[0]}", shape, dt)
        h = cm.__enter__()
        return h, (lambda cm=cm: cm.__exit__(None, None, None))

    c.sb_scoped = sb_scoped

    c.s_pe = p.sem("s_pe")
    c.s_ve = p.sem("s_ve")
    c.s_sc = p.sem("s_sc")
    c.s_gp = p.sem("s_gp")
    c.s_out = p.sem("s_out")
    c.s_dbg = p.sem("s_dbg")
    c.all_sems = [c.s_pe, c.s_ve, c.s_sc, c.s_gp]
    p.set_chain({"vector": c.s_ve, "scalar": c.s_sc, "gpsimd": c.s_gp})

    class Burst:
        def __init__(self, name):
            self.sem = p.sem(name)
            self.n = 0

        def load(self, eng, fn, waits=()):
            p.emit(eng, fn, inc=(self.sem, 16), waits=waits)
            self.n += 1

        def done(self):
            return (self.sem, 16 * self.n)

    class Stream:
        def __init__(self, name, nslots):
            self.sems = [p.sem(f"{name}{i}") for i in range(nslots)]
            self.cnt = [0] * nslots

        def load(self, eng, slot, fn, waits=()):
            p.emit(eng, fn, inc=(self.sems[slot], 16), waits=waits)
            self.cnt[slot] += 1
            return (self.sems[slot], 16 * self.cnt[slot])

    c.Burst, c.Stream = Burst, Stream

    x_in = din("x_fm", (D, T))
    y_out = nc.dram_tensor("y_fm", [D, T], F32, kind="ExternalOutput").ap()
    c.x0_spill = nc.dram_tensor("x0_spill", [D, T], BF16, kind="Internal").ap()
    c.v2_spill = nc.dram_tensor("v2_spill", [D, T], BF16, kind="Internal").ap()

    c.fwd_t = din("c_fwd", (2, NKC, 128, 16, 128), BF16)
    c.inv_t = din("c_inv", (2, NKC, 128, T), BF16)
    id_d = din("c_id", (128, 128), BF16)
    idf_d = din("c_idf", (128, 128))
    sel_d = din("c_sel", (8, 8 * 128), BF16)

    c.X = [c.sb(f"X{i}", [128, T]) for i in range(DC)]
    c.ID = c.sb("ID", [128, 128], BF16)
    c.IDF = c.sb("IDF", [128, 128])
    c.SEL = c.sb("SEL", [8, 8 * 128], BF16)
    c.ONES = c.sb("ONESC", [128, 4])
    c.ONESR = c.sb("ONESR", [1, 128])
    c.SCR0 = c.sb("SCR0", [128, T])
    c.SCR1 = c.sb("SCR1", [128, 1024])
    c.SCB = [c.sb(f"SCB{i}", [128, T], BF16) for i in range(2)]
    c.STATS = c.sb("STATS", [1, 1024])

    c.PH = [c.ps(f"PH{i}", [128, 512]) for i in range(2)]
    c.PD = [c.ps(f"PD{i}", [128, 512]) for i in range(4)]
    c.PA = c.ps("PA", [128, 512])
    c.PB = c.ps("PB", [128, 512])

    c.bank_guard = {}
    c.bguard = lambda b: c.bank_guard.get(id(b), (c.s_ve, 0))
    c.set_bguard = lambda b, s, v: c.bank_guard.__setitem__(id(b), (s, v))

    bin_ = c.Burst("s_bin")
    bin_.load("sync", lambda e: e.dma_start(out=c.ID[:, :], in_=id_d))
    bin_.load("sync", lambda e: e.dma_start(out=c.IDF[:, :], in_=idf_d))
    bin_.load("sync", lambda e: e.dma_start(out=c.SEL[:, :], in_=sel_d))
    for dc in range(DC):
        bin_.load("sync", lambda e, dc=dc: e.dma_start(out=c.X[dc][:, :], in_=x_in[dc * 128:(dc + 1) * 128, :]))
    c.x_loaded = bin_.done()
    for col, v in [(0, 1.0), (1, 1e-5), (2, -math.pi), (3, 0.0)]:
        p.emit("vector", lambda e, col=col, v=v: e.memset(c.ONES[:, col:col + 1], v), inc=(c.s_ve, 1))
    p.emit("vector", lambda e: e.memset(c.ONESR[:, :], 1.0), inc=(c.s_ve, 1))
    c.ones_ready = p.val(c.s_ve)
    c.id_ready = c.x_loaded
    for _e in ENGINES:
        p.pending[_e] = [c.x_loaded]

    for li in layers:
        _emit_layer(c, li)

    p.barrier(c.all_sems)
    for dc in range(DC):
        p.emit("sync", lambda e, dc=dc: e.dma_start(out=y_out[dc * 128:(dc + 1) * 128, :], in_=c.X[dc][:, :]),
               inc=(c.s_out, 16))
    p.wait("sync", c.s_out, p.val(c.s_out))

    p.finalize()
    es.close()
    return nc, c.input_names


def _emit_layer(c, li):
    p = c.p
    c.XN, xn_close = c.sb_scoped(f"XN{li}", [128 * 1, T], BF16) if False else (None, None)
    c.XN = [None] * DC
    closers = []
    for i in range(DC):
        h, cl = c.sb_scoped(f"XN{li}_{i}", [128, T], BF16)
        c.XN[i] = h
        closers.append(cl)
    if _ph("ln"):
        _emit_ln(c, li)
    if li == 0:
        if _ph("router"):
            _emit_router(c, li)
        if _ph("ffn"):
            _emit_ffn_l0(c)
        elif _ph("router"):
            p.barrier(c.all_sems)
            for cl in reversed(c._router_closers):
                cl()
        if _ph("uphase"):
            _emit_uphase(c)
        p.barrier(c.all_sems)
        for cl in reversed(closers):
            cl()
        if _ph("conv"):
            _emit_conv_l0(c)
    else:
        if _ph("omega"):
            _emit_omega_residual(c)
        if _ph("router"):
            _emit_router(c, li)
        if _ph("ffn"):
            _emit_ffn_l1(c)
        elif _ph("router"):
            p.barrier(c.all_sems)
            for cl in reversed(c._router_closers):
                cl()
        if _ph("conv"):
            _emit_conv_l1(c)
        p.barrier(c.all_sems)
        for cl in reversed(closers):
            cl()


def _emit_ln(c, li):
    p = c.p
    XN = c.XN
    p.barrier(c.all_sems)
    SQ = c.SCR1[:, 0:512]
    MBC = [c.SCR1[:, 512:1024], c.SCR0[:, 0:512]]
    RBC = c.SCR0[:, 1024:2048]
    mm_guard = 0
    st_dones = {}
    bc_state = {}

    def emit_sums(tt):
        nonlocal mm_guard
        ts_ = slice(tt * 512, (tt + 1) * 512)
        gA = c.bguard(c.PA)
        gB = c.bguard(c.PB)
        for dc in range(DC):
            sqv = p.emit("scalar", lambda e, dc=dc, ts_=ts_: e.activation(
                out=SQ, in_=c.X[dc][:, ts_], func=AF.Square),
                inc=(c.s_sc, 1), waits=[(c.s_pe, mm_guard)])
            p.emit("tensor", lambda e, dc=dc, ts_=ts_: e.matmul(
                c.PA[0:1, 0:512], c.ONES[:, 0:1], c.X[dc][:, ts_],
                start=(dc == 0), stop=(dc == DC - 1)),
                inc=(c.s_pe, 1), waits=[gA] if dc == 0 else ())
            p.emit("tensor", lambda e, dc=dc: e.matmul(
                c.PB[0:1, 0:512], c.ONES[:, 0:1], SQ,
                start=(dc == 0), stop=(dc == DC - 1)),
                inc=(c.s_pe, 1), waits=[(c.s_sc, sqv)] + ([gB] if dc == 0 else []))
            mm_guard = p.val(c.s_pe)
        return p.val(c.s_pe)

    def emit_stats(tt, mm_done, prev_bc_pe):
        mean = c.STATS[0:1, 0:512]
        rstd = c.STATS[0:1, 512:1024]
        p.emit("vector", lambda e, mean=mean: e.tensor_scalar(
            out=mean, in0=c.PA[0:1, 0:512], scalar1=1.0 / 1024.0, scalar2=None, op0=ALU.mult),
            inc=(c.s_ve, 1), waits=[(c.s_pe, mm_done), (c.s_pe, prev_bc_pe)])
        p.emit("vector", lambda e, mean=mean, rstd=rstd: e.scalar_tensor_tensor(
            out=rstd, in0=mean, scalar=-1.0, in1=mean, op0=ALU.mult, op1=ALU.mult),
            inc=(c.s_ve, 1))
        p.emit("vector", lambda e, rstd=rstd: e.scalar_tensor_tensor(
            out=rstd, in0=c.PB[0:1, 0:512], scalar=1.0 / 1024.0, in1=rstd, op0=ALU.mult, op1=ALU.add),
            inc=(c.s_ve, 1))
        vvar = p.val(c.s_ve)
        p.emit("scalar", lambda e, rstd=rstd: e.activation(
            out=rstd, in_=rstd, func=AF.Sqrt, bias=c.ONES[0:1, 1:2], scale=1.0),
            inc=(c.s_sc, 1), waits=[(c.s_ve, vvar)])
        vsq = p.val(c.s_sc)
        p.emit("vector", lambda e, rstd=rstd: e.reciprocal(out=rstd, in_=rstd),
               inc=(c.s_ve, 1), waits=[(c.s_sc, vsq)])
        st_done = p.val(c.s_ve)
        c.set_bguard(c.PA, c.s_ve, st_done)
        c.set_bguard(c.PB, c.s_ve, st_done)
        return st_done, mean, rstd

    def emit_bc_xn(tt, st_done, mean, rstd):
        ts_ = slice(tt * 512, (tt + 1) * 512)
        par = tt % 2
        gC = c.bguard(c.PD[0 + par])
        gD = c.bguard(c.PD[2 + par])
        p.emit("tensor", lambda e, mean=mean, par=par: e.matmul(
            c.PD[0 + par][:, 0:512], c.ONESR[0:1, :], mean, start=True, stop=True),
            inc=(c.s_pe, 1), waits=[(c.s_ve, st_done), gC])
        p.emit("tensor", lambda e, rstd=rstd, par=par: e.matmul(
            c.PD[2 + par][:, 0:512], c.ONESR[0:1, :], rstd, start=True, stop=True),
            inc=(c.s_pe, 1), waits=[gD])
        bc_done = p.val(c.s_pe)
        p.emit("vector", lambda e, par=par: e.tensor_copy(MBC[par], c.PD[0 + par][:, 0:512]),
               inc=(c.s_ve, 1), waits=[(c.s_pe, bc_done)])
        p.emit("vector", lambda e, ts_=ts_, par=par: e.tensor_copy(RBC[:, 0:512] if False else c.SCR0[:, ts_][:, 0:512] if False else MBC[par], MBC[par]),
               inc=(c.s_ve, 1)) if False else None
        for dc in range(DC):
            p.emit("vector", lambda e, dc=dc, ts_=ts_, par=par: e.tensor_tensor(
                out=XN[dc][:, ts_], in0=c.X[dc][:, ts_], in1=MBC[par], op=ALU.subtract),
                inc=(c.s_ve, 1))
            p.emit("vector", lambda e, dc=dc, ts_=ts_, par=par: e.tensor_tensor(
                out=XN[dc][:, ts_], in0=XN[dc][:, ts_], in1=c.PD[2 + par][:, 0:512], op=ALU.mult),
                inc=(c.s_ve, 1))
        ve_done = p.val(c.s_ve)
        c.set_bguard(c.PD[0 + par], c.s_ve, ve_done)
        c.set_bguard(c.PD[2 + par], c.s_ve, ve_done)
        return bc_done

    prev_bc = 0
    pend = None
    for tt in range(4):
        mm_done = emit_sums(tt)
        if pend is not None:
            prev_bc = emit_bc_xn(*pend)
        st_done, mean, rstd = emit_stats(tt, mm_done, prev_bc)
        pend = (tt, st_done, mean, rstd)
    emit_bc_xn(*pend)
    c.xn_ready = p.val(c.s_ve)


def _emit_router(c, li):
    p = c.p
    if li == 0:
        wr = c.din("l0_wr_t", (D, 8))
    else:
        wr = c.din("l1_gate_t", (D, 8))
        beta16 = c.din("l1_beta_r16", (1, 128))
    WRS, wrs_close = c.sb_scoped(f"wrs{li}", [128, DC * 8], BF16)
    c.WT8, c.wt8_close = c.sb_scoped(f"wt8_{li}", [16, 16 * 128], BF16)
    WT8 = c.WT8
    rtb = c.Burst(f"s_rt{li}")
    rtb.load("gpsimd", lambda e: e.dma_start(
        out=WRS.rearrange("pp (dc e) -> pp dc e", e=8),
        in_=wr.rearrange("(dc pp) e -> pp dc e", pp=128)))
    LG = c.SCR1[:, 0:128]
    EL = c.SCR1[:, 128:256]
    MSK = c.SCR1[:, 256:384]
    WB = c.SCR1[:, 384:512]
    TMPA = c.SCR1[:, 512:640]
    TMPB = c.SCR1[:, 640:768]
    BETB = c.SCR1[:, 768:896]
    M16 = c.SCR1[:, 896:912]
    S16 = c.SCR1[:, 912:928]
    RD16 = c.SCR1[:, 928:944]

    def v16(ap):
        return ap.rearrange("pp (a b) -> pp a b", b=8)

    def b16(ap):
        return ap.unsqueeze(-1).broadcast_to([128, 16, 8])

    beta_bc = 0
    if li == 1:
        rtbs = c.Burst(f"s_rts{li}")
        rtbs.load("sync", lambda e: e.dma_start(out=c.STATS[0:1, 0:128], in_=beta16))
        gA = c.bguard(c.PA)
        p.emit("tensor", lambda e: e.matmul(c.PA[:, 0:128], c.ONESR[0:1, :], c.STATS[0:1, 0:128],
                                            start=True, stop=True),
               inc=(c.s_pe, 1), waits=[rtbs.done(), gA, (c.s_ve, c.ones_ready)])
        bv = p.val(c.s_pe)
        p.emit("vector", lambda e: e.tensor_copy(BETB, c.PA[:, 0:128]),
               inc=(c.s_ve, 1), waits=[(c.s_pe, bv)])
        c.set_bguard(c.PA, c.s_ve, p.val(c.s_ve))
        beta_bc = p.val(c.s_ve)

    gA = c.bguard(c.PA)
    for t in range(16):
        for dc in range(DC):
            p.emit("tensor", lambda e, t=t, dc=dc, xn=c.XN[dc]: e.matmul(
                c.PA[:, t * 8:(t + 1) * 8], xn[:, t * 128:(t + 1) * 128],
                WRS[:, dc * 8:(dc + 1) * 8],
                start=(t == 0 and dc == 0), stop=(t == 15 and dc == DC - 1)),
                inc=(c.s_pe, 1) if (t == 15 and dc == DC - 1) else None,
                waits=[(c.s_ve, c.xn_ready), rtb.done(), gA] if (t == 0 and dc == 0) else ())
    lg_mm = p.val(c.s_pe)
    p.emit("vector", lambda e: e.tensor_copy(LG, c.PA[:, 0:128]),
           inc=(c.s_ve, 1), waits=[(c.s_pe, lg_mm)])
    c.set_bguard(c.PA, c.s_ve, p.val(c.s_ve))

    if li == 0:
        p.emit("vector", lambda e: e.reduce_max(M16, v16(LG), axis=AX.X), inc=(c.s_ve, 1))
        p.emit("vector", lambda e: e.tensor_tensor(out=v16(TMPA), in0=v16(LG), in1=b16(M16), op=ALU.subtract),
               inc=(c.s_ve, 1))
        vv = p.val(c.s_ve)
        p.emit("scalar", lambda e: e.activation(out=EL, in_=TMPA, func=AF.Exp),
               inc=(c.s_sc, 1), waits=[(c.s_ve, vv)])
        sc_el = p.val(c.s_sc)
        p.emit("vector", lambda e: e.tensor_tensor(out=v16(MSK), in0=v16(LG), in1=b16(M16), op=ALU.is_ge),
               inc=(c.s_ve, 1))
        p.emit("vector", lambda e: e.tensor_scalar(out=TMPA, in0=MSK, scalar1=-1.0, scalar2=1.0,
                                                   op0=ALU.mult, op1=ALU.add),
               inc=(c.s_ve, 1), waits=[(c.s_sc, sc_el)])
        p.emit("vector", lambda e: e.tensor_tensor(out=TMPA, in0=EL, in1=TMPA, op=ALU.mult),
               inc=(c.s_ve, 1))
        p.emit("vector", lambda e: e.reduce_max(S16, v16(TMPA), axis=AX.X), inc=(c.s_ve, 1))
        p.emit("vector", lambda e: e.scalar_tensor_tensor(out=TMPB, in0=MSK, scalar=-1e30, in1=LG,
                                                          op0=ALU.mult, op1=ALU.add), inc=(c.s_ve, 1))
        p.emit("vector", lambda e: e.reduce_max(M16, v16(TMPB), axis=AX.X), inc=(c.s_ve, 1))
        p.emit("vector", lambda e: e.tensor_tensor(out=v16(MSK), in0=v16(LG), in1=b16(M16), op=ALU.is_ge),
               inc=(c.s_ve, 1))
        p.emit("vector", lambda e: e.tensor_scalar(out=S16, in0=S16, scalar1=1.0, scalar2=None,
                                                   op0=ALU.add), inc=(c.s_ve, 1))
        p.emit("vector", lambda e: e.reciprocal(out=RD16, in_=S16), inc=(c.s_ve, 1))
        p.emit("vector", lambda e: e.tensor_tensor(out=v16(WB), in0=v16(EL), in1=b16(RD16), op=ALU.mult),
               inc=(c.s_ve, 1))
        p.emit("vector", lambda e: e.tensor_tensor(out=WB, in0=WB, in1=MSK, op=ALU.mult),
               inc=(c.s_ve, 1))
    else:
        p.emit("vector", lambda e: e.tensor_tensor(out=TMPA, in0=LG, in1=BETB, op=ALU.add),
               inc=(c.s_ve, 1), waits=[(c.s_ve, beta_bc)])
        p.emit("vector", lambda e: e.reduce_max(M16, v16(TMPA), axis=AX.X), inc=(c.s_ve, 1))
        p.emit("vector", lambda e: e.tensor_tensor(out=v16(MSK), in0=v16(TMPA), in1=b16(M16), op=ALU.is_ge),
               inc=(c.s_ve, 1))
        p.emit("vector", lambda e: e.scalar_tensor_tensor(out=TMPB, in0=MSK, scalar=-1e30, in1=TMPA,
                                                          op0=ALU.mult, op1=ALU.add), inc=(c.s_ve, 1))
        p.emit("vector", lambda e: e.reduce_max(M16, v16(TMPB), axis=AX.X), inc=(c.s_ve, 1))
        p.emit("vector", lambda e: e.tensor_tensor(out=v16(MSK), in0=v16(TMPA), in1=b16(M16), op=ALU.is_ge),
               inc=(c.s_ve, 1))
        vv = p.val(c.s_ve)
        p.emit("scalar", lambda e: e.activation(out=EL, in_=LG, func=AF.Sigmoid),
               inc=(c.s_sc, 1), waits=[(c.s_ve, vv)])
        sg = p.val(c.s_sc)
        p.emit("vector", lambda e: e.tensor_tensor(out=WB, in0=EL, in1=MSK, op=ALU.mult),
               inc=(c.s_ve, 1), waits=[(c.s_sc, sg)])
    w_done = p.val(c.s_ve)

    banks = [c.PA, c.PB]
    for t in range(16):
        bank = banks[t % 2]
        gB = c.bguard(bank)
        p.emit("tensor", lambda e, t=t, ba=bank: e.matmul(
            ba[0:8, 0:128], WB[:, t * 8:(t + 1) * 8], c.IDF[:, :], start=True, stop=True),
            inc=(c.s_pe, 1), waits=[(c.s_ve, w_done), gB, c.id_ready] if t < 2 else [gB])
        tr = p.val(c.s_pe)
        p.emit("vector", lambda e, t=t, ba=bank: e.tensor_copy(WT8[0:8, t * 128:(t + 1) * 128],
                                                               ba[0:8, 0:128]),
               inc=(c.s_ve, 1), waits=[(c.s_pe, tr)])
        c.set_bguard(bank, c.s_ve, p.val(c.s_ve))
    c.router_done = p.val(c.s_ve)
    c._router_closers = [wrs_close, c.wt8_close]


def _mk_wbc(c, e_, into):
    p = c.p
    WT8 = c.WT8
    sel_sl = c.SEL[:, e_ * 128:(e_ + 1) * 128]
    banks = [c.PA, c.PB]
    for g in range(4):
        bank = banks[g % 2]
        gA = c.bguard(bank)
        for s in range(4):
            t = g * 4 + s
            p.emit("tensor", lambda en, t=t, s=s, ba=bank: en.matmul(
                ba[:, s * 128:(s + 1) * 128], sel_sl[0:8, :], WT8[0:8, t * 128:(t + 1) * 128],
                start=(s == 0), stop=(s == 3)),
                inc=(c.s_pe, 1) if s == 3 else None,
                waits=[gA, (c.s_ve, c.router_done)] if s == 0 else ())
        v = p.val(c.s_pe)
        if g % 2 == 0:
            p.emit("vector", lambda en, g=g, ba=bank: en.tensor_copy(into[:, g * 512:(g + 1) * 512], ba[:, 0:512]),
                   inc=(c.s_ve, 1), waits=[(c.s_pe, v)])
            c.set_bguard(bank, c.s_ve, p.val(c.s_ve))
        else:
            p.emit("scalar", lambda en, g=g, ba=bank: en.activation(
                out=into[:, g * 512:(g + 1) * 512], in_=ba[:, 0:512], func=AF.Identity),
                inc=(c.s_sc, 1), waits=[(c.s_pe, v)])
            c.set_bguard(bank, c.s_sc, p.val(c.s_sc))
    p.emit("vector", lambda en: en.engine_nop(), inc=(c.s_ve, 1), waits=[(c.s_sc, p.val(c.s_sc))])
    return p.val(c.s_ve)


def _emit_ffn_l0(c):
    p = c.p
    XN = c.XN
    WT8 = c.WT8
    p.barrier(c.all_sems)
    E, M = 8, 4096
    MC = M // 128
    PIECE, MSL, NBUF = 8, 2, 3
    NP = MC // PIECE
    NSL = PIECE // MSL

    w1 = c.din("l0f_w1t", (E, D, M))
    w2t = c.din("l0f_w2t", (E, M, D))
    b1c = c.din("l0f_b1c", (128, E * MC))
    b2d = c.din("l0f_b2", (8, D))

    closers = []

    def sbsc(name, shape, dt=F32):
        h, cl = c.sb_scoped(name, shape, dt)
        closers.append(cl)
        return h

    W1B = [sbsc(f"l0w1b{i}", [128, DC * MSL * 128], BF16) for i in range(NBUF)]
    W2P = sbsc("l0w2p", [128, PIECE * D], BF16)
    HP = [sbsc(f"l0hp{i}", [128, T], BF16) for i in range(PIECE)]
    B1C = sbsc("l0b1c", [128, E * MC])
    B2S = sbsc("l0b2s", [8, D], BF16)
    WBC = [sbsc(f"l0wbc{i}", [128, T], BF16) for i in range(2)]

    fb0 = c.Burst("s_fb0")
    fb0g = c.Burst("s_fb0g")
    fb0.load("sync", lambda e: e.dma_start(out=B1C[:, :], in_=b1c))
    fb0g.load("gpsimd", lambda e: e.dma_start(out=B2S[:, :], in_=b2d))

    w1_guard = {}
    w1_cnt = [0]
    w1_ld_val = {}

    def load_w1(e_, mp, sl):
        i = w1_cnt[0]
        buf = W1B[i % NBUF]
        m0 = (mp * PIECE + sl * MSL) * 128
        src = w1[e_, :, m0:m0 + MSL * 128].rearrange("(dc pp) m -> pp dc m", pp=128)
        dst = buf.rearrange("pp (dc m) -> pp dc m", m=MSL * 128)
        g = w1_guard.get(i % NBUF)
        w1_ld_val[i] = w1s.load("gpsimd", i % NBUF, lambda en: en.dma_start(out=dst, in_=src),
                                waits=[g] if g else [])
        w1_cnt[0] += 1
        return i

    w2_guard = [None]
    w1s = c.Stream("s_w1", NBUF)
    w2s = c.Stream("s_w2", 1)

    def load_w2(e_, mp):
        src = w2t[e_, mp * PIECE * 128:(mp + 1) * PIECE * 128, :].rearrange("(mi pp) h -> pp mi h", pp=128)
        dst = DPW.rearrange("pp (mi h) -> pp mi h", h=D) if False else W2P.rearrange("pp (mi h) -> pp mi h", h=D)
        g = w2_guard[0]
        return w2s.load("gpsimd", 0, lambda en: en.dma_start(out=dst, in_=src),
                        waits=[g] if g else [])

    order = [(e_, mp) for e_ in range(E) for mp in range(NP)]
    if _os.environ.get("K_NPIECE"):
        order = order[:int(_os.environ["K_NPIECE"])]
    slab_ids = {}
    for k in range(NBUF):
        slab_ids[(0, 0, k)] = load_w1(0, 0, k)

    ph_guard = [0, 0]
    pd_guard = {0: 0, 1: 0, 2: 0, 3: 0}
    h_guard = 0
    wbc_done = {}

    for oi, (e_, mp) in enumerate(order):
        if mp == 0:
            if KNOB("K_NOWBC"):
                wbc_done[e_ % 2] = 0
            else:
                wbc_done[e_ % 2] = _mk_wbc(c, e_, WBC[e_ % 2])
        w2_ld = load_w2(e_, mp)
        for mc_i in range(PIECE):
            sl, mi = divmod(mc_i, MSL)
            sid = slab_ids[(e_, mp, sl)]
            mc = mp * PIECE + mc_i
            for tt in range(4):
                bank = (mc_i * 4 + tt) % 2
                waits = [(c.s_sc, ph_guard[bank])]
                if mi == 0 and tt == 0:
                    waits.append(w1_ld_val[sid])
                if oi == 0 and mc_i == 0 and tt == 0:
                    waits += [(c.s_ve, c.xn_ready), fb0.done()]
                for kc in range(DC):
                    lhsT = W1B[sid % NBUF][:, (kc * MSL + mi) * 128:(kc * MSL + mi + 1) * 128]
                    p.emit("tensor", lambda en, l=lhsT, k=kc, b=bank, t=tt:
                           en.matmul(c.PH[b][:, :], l, XN[k][:, t * 512:(t + 1) * 512],
                                     start=(k == 0), stop=(k == DC - 1)),
                           inc=(c.s_pe, 1) if kc == DC - 1 else None,
                           waits=waits if kc == 0 else ())
                pe_cnt = p.val(c.s_pe)
                bcol = B1C[:, e_ * MC + mc:e_ * MC + mc + 1]
                w = [(c.s_pe, pe_cnt)]
                if mc_i == 0 and tt == 0:
                    w.append((c.s_pe, h_guard))
                p.emit("scalar", lambda en, b=bank, m=mc_i, t=tt, bc=bcol:
                       en.activation(out=HP[m][:, t * 512:(t + 1) * 512], in_=c.PH[b][:, :],
                                     func=AF.Gelu, bias=bc, scale=1.0),
                       inc=(c.s_sc, 1), waits=w)
                sc_v = p.val(c.s_sc)
                ph_guard[bank] = sc_v
                if not KNOB("K_NOGATE"):
                    p.emit("vector", lambda en, m=mc_i, t=tt, b=e_ % 2:
                           en.tensor_tensor(out=HP[m][:, t * 512:(t + 1) * 512],
                                            in0=HP[m][:, t * 512:(t + 1) * 512],
                                            in1=WBC[b][:, t * 512:(t + 1) * 512], op=ALU.mult),
                           inc=(c.s_ve, 1),
                           waits=[(c.s_sc, sc_v)] + ([(c.s_ve, wbc_done[e_ % 2])] if (mc_i == 0 and tt == 0) else []))
                else:
                    p.emit("vector", lambda en: en.engine_nop(), inc=(c.s_ve, 1),
                           waits=[(c.s_sc, sc_v)])
            if mi == MSL - 1:
                w1_guard[sid % NBUF] = (c.s_pe, p.val(c.s_pe))
                gidx = oi * NSL + sl + NBUF
                if gidx < len(order) * NSL:
                    noi, nsl = divmod(gidx, NSL)
                    ne, nmp = order[noi]
                    slab_ids[(ne, nmp, nsl)] = load_w1(ne, nmp, nsl)
        h_done = p.val(c.s_ve)
        last_piece = (e_ == E - 1 and mp == NP - 1)
        for tt in range(4):
            for dc in range(DC):
                bank = (tt * DC + dc) % 4
                waits = [(c.s_ve, pd_guard[bank])]
                if tt == 0 and dc == 0:
                    waits += [(c.s_ve, h_done), w2_ld]
                with_b2 = last_piece and not KNOB("K_NOB2")
                for mc_i in range(PIECE):
                    stop = (mc_i == PIECE - 1) and not last_piece
                    last_inst = (mc_i == PIECE - 1) and not with_b2
                    p.emit("tensor", lambda en, l=W2P[:, mc_i * D + dc * 128:mc_i * D + (dc + 1) * 128],
                           m=mc_i, b=bank, t=tt, st=stop:
                           en.matmul(c.PD[b][:, :], l, HP[m][:, t * 512:(t + 1) * 512],
                                     start=(m == 0), stop=st),
                           inc=(c.s_pe, 1) if last_inst else None,
                           waits=waits if mc_i == 0 else ())
                if with_b2:
                    for st_i in range(4):
                        t_ = tt * 4 + st_i
                        p.emit("tensor", lambda en, t_=t_, s=st_i, b=bank, dc=dc:
                               en.matmul(c.PD[b][:, s * 128:(s + 1) * 128],
                                         B2S[0:8, dc * 128:(dc + 1) * 128],
                                         WT8[0:8, t_ * 128:(t_ + 1) * 128],
                                         start=False, stop=(s == 3)),
                               inc=(c.s_pe, 1) if st_i == 3 else None,
                               waits=[fb0g.done()] if (tt == 0 and dc == 0 and st_i == 0) else ())
                pe_cnt = p.val(c.s_pe)
                p.emit("vector", lambda en, d=dc, b=bank, t=tt:
                       en.tensor_tensor(out=c.X[d][:, t * 512:(t + 1) * 512],
                                        in0=c.X[d][:, t * 512:(t + 1) * 512],
                                        in1=c.PD[b][:, :], op=ALU.add),
                       inc=(c.s_ve, 1), waits=[(c.s_pe, pe_cnt)])
                pd_guard[bank] = p.val(c.s_ve)
        h_guard = p.val(c.s_pe)
        w2_guard[0] = (c.s_pe, h_guard)
    if KNOB("K_DUMP"):
        dbg = c.nc.dram_tensor("dbg_gate", [3, 128, T], F32, kind="ExternalOutput").ap()
        vv = p.val(c.s_ve)
        p.emit("vector", lambda e: e.tensor_copy(c.SCR0[:, :], WBC[0][:, :]), inc=(c.s_ve, 1),
               waits=[(c.s_ve, vv)])
        dv = p.val(c.s_ve)
        p.emit("sync", lambda e: e.dma_start(out=dbg[0], in_=c.SCR0[:, :]), inc=(c.s_dbg, 16),
               waits=[(c.s_ve, dv)])
        p.emit("vector", lambda e: e.tensor_copy(c.SCR0[:, :], WBC[1][:, :]), inc=(c.s_ve, 1),
               waits=[(c.s_dbg, p.val(c.s_dbg))])
        dv = p.val(c.s_ve)
        p.emit("sync", lambda e: e.dma_start(out=dbg[1], in_=c.SCR0[:, :]), inc=(c.s_dbg, 16),
               waits=[(c.s_ve, dv)])
        p.emit("vector", lambda e: e.memset(c.SCR0[:, :], 0.0), inc=(c.s_ve, 1),
               waits=[(c.s_dbg, p.val(c.s_dbg))])
        p.emit("vector", lambda e: e.tensor_copy(c.SCR0[0:16, 0:2048], WT8[:, :]), inc=(c.s_ve, 1))
        dv = p.val(c.s_ve)
        p.emit("sync", lambda e: e.dma_start(out=dbg[2], in_=c.SCR0[:, :]), inc=(c.s_dbg, 16),
               waits=[(c.s_ve, dv)])
        p.wait("sync", c.s_dbg, p.val(c.s_dbg))
    c.set_bguard(c.PH[0], c.s_sc, ph_guard[0])
    c.set_bguard(c.PH[1], c.s_sc, ph_guard[1])
    for b in range(4):
        c.set_bguard(c.PD[b], c.s_ve, pd_guard[b])
    p.barrier(c.all_sems)
    for cl in reversed(closers):
        cl()
    for cl in reversed(c._router_closers):
        cl()


def _emit_uphase(c):
    p = c.p
    XN = c.XN
    W = 3 * D
    WC = W // 128
    win = c.din("l0a_win", (D, W))
    binc = c.din("l0a_binc", (128, WC))
    swc = c.din("l0a_swc", (128, WC * 3))
    sbc = c.din("l0a_sbc", (128, WC))

    NBUF = 3
    closers = []

    def sbsc(name, shape, dt=F32):
        h, cl = c.sb_scoped(name, shape, dt)
        closers.append(cl)
        return h

    WINB = [sbsc(f"winb{i}", [128, DC * 256], BF16) for i in range(NBUF)]
    SMALL = sbsc("usmall", [128, WC * 5])
    U = [sbsc(f"ubuf{i}", [128, T + 8]) for i in range(2)]
    X1 = [sbsc(f"x1_{i}", [128, T], BF16) for i in range(8)]

    ubst = c.Burst("s_ub")
    ubst.load("sync", lambda e: e.dma_start(out=SMALL[:, 0:WC], in_=binc))
    ubst.load("sync", lambda e: e.dma_start(out=SMALL[:, WC:WC * 4], in_=swc))
    ubst.load("sync", lambda e: e.dma_start(out=SMALL[:, WC * 4:WC * 5], in_=sbc))
    for i in range(2):
        p.emit("vector", lambda e, i=i: e.memset(U[i][:, 0:8], 0.0), inc=(c.s_ve, 1))
    u_zero = p.val(c.s_ve)

    guard = {}
    cnt = [0]
    win_ld_val = {}
    wins = c.Stream("s_win", NBUF)
    sp0 = c.Stream("s_sp", 2)
    spv = c.Burst("s_spv")

    def load_win(sl2):
        i = cnt[0]
        buf = WINB[i % NBUF]
        src = win[:, sl2 * 256:(sl2 + 1) * 256].rearrange("(dc pp) m -> pp dc m", pp=128)
        dst = buf.rearrange("pp (dc m) -> pp dc m", m=256)
        g = guard.get(i % NBUF)
        win_ld_val[i] = wins.load("gpsimd", i % NBUF, lambda en: en.dma_start(out=dst, in_=src),
                                  waits=[g] if g else [])
        cnt[0] += 1
        return i

    sids = {}
    for k in range(NBUF):
        sids[k] = load_win(k)

    u_guard = [0, 0]
    spill = [None, None]
    TMP = c.SCR0
    for wc in range(WC):
        sl, half = divmod(wc, 2)
        sid = sids[sl]
        ub = U[wc % 2]
        for tt in range(4):
            bank = tt % 2
            waits = [c.bguard(c.PH[bank])]
            if tt == 0:
                waits += [win_ld_val[sid], (c.s_ve, u_guard[wc % 2])]
                if wc == 0:
                    waits += [(c.s_ve, c.xn_ready), ubst.done(), (c.s_ve, u_zero)]
            for kc in range(DC):
                lhsT = WINB[sid % NBUF][:, (kc * 2 + half) * 128:(kc * 2 + half + 1) * 128]
                p.emit("tensor", lambda en, l=lhsT, k=kc, b=bank, t=tt:
                       en.matmul(c.PH[b][:, :], l, XN[k][:, t * 512:(t + 1) * 512],
                                 start=(k == 0), stop=(k == DC - 1)),
                       inc=(c.s_pe, 1) if kc == DC - 1 else None,
                       waits=waits if kc == 0 else ())
            pe_cnt = p.val(c.s_pe)
            p.emit("scalar", lambda en, b=bank, t=tt, wc=wc, ub=ub:
                   en.activation(out=ub[:, 8 + t * 512:8 + (t + 1) * 512], in_=c.PH[b][:, :],
                                 func=AF.Identity, bias=SMALL[:, wc:wc + 1], scale=1.0),
                   inc=(c.s_sc, 1), waits=[(c.s_pe, pe_cnt)])
            c.set_bguard(c.PH[bank], c.s_sc, p.val(c.s_sc))
        ev_done = p.val(c.s_sc)
        if half == 1:
            guard[sid % NBUF] = (c.s_pe, p.val(c.s_pe))
            if sl + NBUF < 12:
                sids[sl + NBUF] = load_win(sl + NBUF)
        sw0 = SMALL[:, WC + wc * 3:WC + wc * 3 + 1]
        sw1 = SMALL[:, WC + wc * 3 + 1:WC + wc * 3 + 2]
        sw2 = SMALL[:, WC + wc * 3 + 2:WC + wc * 3 + 3]
        sbb = SMALL[:, WC * 4 + wc:WC * 4 + wc + 1]
        p.emit("vector", lambda e, ub=ub, sw1=sw1, sbb=sbb: e.tensor_scalar(
            out=TMP[:, 0:T], in0=ub[:, 7:7 + T], scalar1=sw1, scalar2=sbb, op0=ALU.mult, op1=ALU.add),
            inc=(c.s_ve, 1), waits=[(c.s_sc, ev_done)])
        p.emit("vector", lambda e, ub=ub, sw0=sw0: e.scalar_tensor_tensor(
            out=TMP[:, 0:T], in0=ub[:, 6:6 + T], scalar=sw0, in1=TMP[:, 0:T], op0=ALU.mult, op1=ALU.add),
            inc=(c.s_ve, 1))
        if wc < 8:
            tgt = c.SCB[wc % 2]
            p.emit("vector", lambda e, ub=ub, sw2=sw2, tgt=tgt: e.scalar_tensor_tensor(
                out=tgt[:, :], in0=ub[:, 8:8 + T], scalar=sw2, in1=TMP[:, 0:T], op0=ALU.mult, op1=ALU.add),
                inc=(c.s_ve, 1), waits=[spill[wc % 2]] if spill[wc % 2] else ())
            vv = p.val(c.s_ve)
            spill[wc % 2] = sp0.load("sync", wc % 2, lambda e, wc=wc, tgt=tgt: e.dma_start(
                out=c.x0_spill[wc * 128:(wc + 1) * 128, :], in_=tgt[:, :]),
                waits=[(c.s_ve, vv)])
        elif wc < 16:
            p.emit("vector", lambda e, ub=ub, sw2=sw2, x1=X1[wc - 8]: e.scalar_tensor_tensor(
                out=x1[:, :], in0=ub[:, 8:8 + T], scalar=sw2, in1=TMP[:, 0:T], op0=ALU.mult, op1=ALU.add),
                inc=(c.s_ve, 1))
        else:
            j = wc - 16
            p.emit("vector", lambda e, ub=ub, sw2=sw2: e.scalar_tensor_tensor(
                out=TMP[:, 0:T], in0=ub[:, 8:8 + T], scalar=sw2, in1=TMP[:, 0:T], op0=ALU.mult, op1=ALU.add),
                inc=(c.s_ve, 1))
            p.emit("vector", lambda e, x1=X1[j]: e.tensor_tensor(
                out=x1[:, :], in0=TMP[:, 0:T], in1=x1[:, :], op=ALU.mult),
                inc=(c.s_ve, 1))
            vv = p.val(c.s_ve)
            spv.load("sync", lambda e, j=j, x1=X1[j]: e.dma_start(
                out=c.v2_spill[j * 128:(j + 1) * 128, :], in_=x1[:, :]),
                waits=[(c.s_ve, vv)])
        u_guard[wc % 2] = p.val(c.s_ve)
    c.uphase_done = p.val(c.s_ve)
    c.spills_done = [(sp0.sems[0], 16 * sp0.cnt[0]),
                     (sp0.sems[1], 16 * sp0.cnt[1]), spv.done()]
    p.barrier(c.all_sems)
    for _e in ENGINES:
        p.pending[_e] = p.pending[_e] + list(c.spills_done)
    for cl in reversed(closers):
        cl()


class ConvBundle:
    pass


def _alloc_conv_bundle(c, with_l0_extras):
    b = ConvBundle()
    b.closers = []
    if not hasattr(c, "fws"):
        c.fws = c.Stream("s_fw", 4)
        c.ivs = c.Stream("s_iv", 4)

    def sbsc(name, shape, dt=BF16):
        h, cl = c.sb_scoped(name, shape, dt)
        b.closers.append(cl)
        return h

    b.KF = sbsc("cv_kf", [128, NKC * 512])
    b.FWDB = sbsc("cv_fwdb", [128, 4 * 2048])
    b.INVB = sbsc("cv_invb", [128, 4 * 2048])
    b.SK = sbsc("cv_sk", [128, 16 * 512])
    b.KTMP = c.SCB[1][:, 0:1024]
    if with_l0_extras:
        b.X0Q = sbsc("cv_x0q", [128, 2 * T])
        b.V2Q = sbsc("cv_v2q", [128, 2 * T])
        b.WOUTB = sbsc("cv_woutb", [128, 2 * D])
    b.fwd_guard = {}
    b.inv_guard = {}
    return b


def _conv_kf(b, kc, ri):
    return b.KF[:, kc * 512 + ri * 256:kc * 512 + (ri + 1) * 256]


def _conv_quarter(c, b, sigtm_ready, kerq_ready, out_cb):
    p = c.p

    def load_fwd(kc, which):
        slot = (kc % 2) * 2 + which
        off = slot * 2048
        buf = b.FWDB[:, off:off + 2048]
        g = b.fwd_guard.get(slot)
        v = c.fws.load("sync", slot,
            lambda e, kc=kc, w=which, bu=buf: e.dma_start(
                out=bu, in_=c.fwd_t[w, kc].rearrange("pp nt j -> pp (nt j)")),
            waits=[g] if g else [])
        return v, buf

    fwd_banks = [c.PH[0], c.PH[1], c.PD[0], c.PD[1]]
    for kc in range(NKC):
        lc, fbc = load_fwd(kc, 0)
        ls, fbs = load_fwd(kc, 1)
        sig_banks = [fwd_banks[(kc % 2) * 2 + 0], fwd_banks[(kc % 2) * 2 + 1]]
        for ri in range(2):
            bank = sig_banks[ri]
            g = c.bguard(bank)
            fb = fbc if ri == 0 else fbs
            for nt in range(16):
                p.emit("tensor", lambda e, nt=nt, ba=bank, fb=fb: e.matmul(
                    ba[:, 0:512], fb[:, nt * 128:(nt + 1) * 128],
                    b.SK[:, nt * 512:(nt + 1) * 512],
                    start=(nt == 0), stop=(nt == 15)),
                    inc=(c.s_pe, 1) if nt == 15 else None,
                    waits=[g, lc, ls, (c.s_ve, sigtm_ready), (c.s_ve, kerq_ready)] if nt == 0 else ())
        mm = p.val(c.s_pe)
        ko = (kc % 2) * 512
        kr = b.KTMP[:, ko:ko + 256]
        ki = b.KTMP[:, ko + 256:ko + 512]
        sb0, sb1 = sig_banks
        kg = b.fwd_guard.get(8 + kc % 2)
        p.emit("scalar", lambda e, ba=sb0, kr=kr: e.activation(out=kr, in_=ba[:, 256:512], func=AF.Identity),
               inc=(c.s_sc, 1), waits=[(c.s_pe, mm)] + ([kg] if kg else []))
        p.emit("scalar", lambda e, ba=sb1, ki=ki: e.activation(out=ki, in_=ba[:, 256:512], func=AF.Identity),
               inc=(c.s_sc, 1))
        kev = p.val(c.s_sc)
        t1 = c.SCR0[:, 0:256]; t2 = c.SCR0[:, 256:512]
        t3 = c.SCR0[:, 512:768]; t4 = c.SCR0[:, 768:1024]
        pr = _conv_kf(b, kc, 0); pi = _conv_kf(b, kc, 1)
        w0 = [(c.s_pe, mm), (c.s_sc, kev)]
        p.emit("vector", lambda e, kr=kr, ba=sb0: e.tensor_tensor(out=t1, in0=ba[:, 0:256], in1=kr, op=ALU.mult),
               inc=(c.s_ve, 1), waits=w0)
        p.emit("vector", lambda e, ki=ki, ba=sb1: e.tensor_tensor(out=t2, in0=ba[:, 0:256], in1=ki, op=ALU.mult),
               inc=(c.s_ve, 1))
        p.emit("vector", lambda e, ki=ki, ba=sb0: e.tensor_tensor(out=t3, in0=ba[:, 0:256], in1=ki, op=ALU.mult),
               inc=(c.s_ve, 1))
        p.emit("vector", lambda e, kr=kr, ba=sb1: e.tensor_tensor(out=t4, in0=ba[:, 0:256], in1=kr, op=ALU.mult),
               inc=(c.s_ve, 1))
        p.emit("vector", lambda e, pr=pr: e.tensor_tensor(out=pr, in0=t1, in1=t2, op=ALU.subtract),
               inc=(c.s_ve, 1))
        p.emit("vector", lambda e, pi=pi: e.tensor_tensor(out=pi, in0=t3, in1=t4, op=ALU.add),
               inc=(c.s_ve, 1))
        pw = p.val(c.s_ve)
        c.set_bguard(sb0, c.s_ve, pw)
        c.set_bguard(sb1, c.s_ve, pw)
        b.fwd_guard[(kc % 2) * 2 + 0] = (c.s_pe, p.val(c.s_pe))
        b.fwd_guard[(kc % 2) * 2 + 1] = (c.s_pe, p.val(c.s_pe))
        b.fwd_guard[8 + kc % 2] = (c.s_ve, pw)
    p_done = p.val(c.s_ve)

    inv_banks = [c.PH[0], c.PH[1], c.PD[0], c.PD[1], c.PD[2], c.PD[3], c.PA, c.PB]
    for kc in range(NKC):
        inv_lds = []
        for ri in range(2):
            slot = (kc % 2) * 2 + ri
            off = slot * 2048
            g = b.inv_guard.get(slot)
            inv_lds.append(c.ivs.load("sync" if ri == 0 else "gpsimd", slot,
                lambda e, kc=kc, ri=ri, off=off: e.dma_start(
                    out=b.INVB[:, off:off + 2048], in_=c.inv_t[ri, kc]),
                waits=[g] if g else []))
        for ccl in range(2):
            for tt in range(4):
                bank = inv_banks[ccl * 4 + tt]
                for ri in range(2):
                    w = []
                    if kc == 0 and ri == 0:
                        w = [c.bguard(bank), (c.s_ve, p_done)]
                    if ccl == 0 and tt == 0 and ri == 0:
                        w.extend(inv_lds)
                    off = ((kc % 2) * 2 + ri) * 2048
                    psl = _conv_kf(b, kc, ri)[:, ccl * 128:(ccl + 1) * 128]
                    p.emit("tensor", lambda e, ps=psl, off=off, tt=tt, ba=bank, kc=kc, ri=ri:
                           e.matmul(ba[:, 0:512], ps,
                                    b.INVB[:, off + tt * 512:off + (tt + 1) * 512],
                                    start=(kc == 0 and ri == 0), stop=(kc == NKC - 1 and ri == 1)),
                           inc=(c.s_pe, 1) if (ccl == 1 and tt == 3 and ri == 1) else None,
                           waits=w)
        b.inv_guard[(kc % 2) * 2 + 0] = (c.s_pe, p.val(c.s_pe))
        b.inv_guard[(kc % 2) * 2 + 1] = (c.s_pe, p.val(c.s_pe))
    inv_mm = p.val(c.s_pe)
    for ccl in range(2):
        for tt in range(4):
            out_cb(ccl, tt, inv_banks[ccl * 4 + tt], [(c.s_pe, inv_mm)])


def _transpose_to(c, src_ap_fn, dst_ap_fn, n_tiles, ready, banks=None):
    p = c.p
    banks = banks or [c.PD[0], c.PD[1], c.PD[2], c.PD[3]]
    for i in range(n_tiles):
        bank = banks[i % len(banks)]
        g = c.bguard(bank)
        p.emit("tensor", lambda e, i=i, ba=bank: e.matmul(
            ba[:, 0:128], src_ap_fn(i), c.ID[:, :], start=True, stop=True),
            inc=(c.s_pe, 1),
            waits=[g, (c.s_ve, ready), c.id_ready] if i < len(banks) else [g])
        mm = p.val(c.s_pe)
        if i % 2 == 0:
            p.emit("vector", lambda e, i=i, ba=bank: e.tensor_copy(dst_ap_fn(i), ba[:, 0:128]),
                   inc=(c.s_ve, 1), waits=[(c.s_pe, mm)])
            c.set_bguard(bank, c.s_ve, p.val(c.s_ve))
        else:
            p.emit("scalar", lambda e, i=i, ba=bank: e.activation(
                out=dst_ap_fn(i), in_=ba[:, 0:128], func=AF.Identity),
                inc=(c.s_sc, 1), waits=[(c.s_pe, mm)])
            c.set_bguard(bank, c.s_sc, p.val(c.s_sc))
    p.emit("vector", lambda e: e.engine_nop(), inc=(c.s_ve, 1), waits=[(c.s_sc, p.val(c.s_sc))])
    return p.val(c.s_ve)


def _emit_filter_common(c):
    p = c.p
    f = Ctx()
    c._filt = f
    f.closers = []

    def sbsc(name, shape, dt=F32):
        h, cl = c.sb_scoped(name, shape, dt)
        f.closers.append(cl)
        return h

    pe_rev = c.din("l0a_pe_rev", (3, T))
    w1 = c.din("l0a_w1", (3, 64))
    w2 = c.din("l0a_w2", (64, 64))
    w3 = c.din("l0a_w3", (64, D))
    cols = c.din("l0a_fcols", (64, 4))
    trev = c.din("l0a_trev", (1, T))
    negd = c.din("l0a_negd", (1, D))
    b3r = c.din("l0a_b3r", (1, D))
    f.W3S = sbsc("f_w3", [64, D])
    f.COLS = sbsc("f_cols", [64, 8])
    f.TREV = sbsc("f_trev", [1, T])
    f.NEGD = sbsc("f_negd", [1, D])
    f.H2 = sbsc("f_h2", [64, T])
    f.B3BC = sbsc("f_b3bc", [128, D])
    f.FTA = [sbsc(f"f_fta{i}", [128, 256], BF16) for i in range(2)]
    f.FTB = [sbsc(f"f_ftb{i}", [128, 256], BF16) for i in range(2)]
    tcl = []
    def sbt(name, shape, dt=F32):
        h, cl = c.sb_scoped(name, shape, dt)
        tcl.append(cl)
        return h
    f.W1S = sbt("f_w1", [3, 64])
    f.W2S = sbt("f_w2", [64, 64])
    f.B3R = sbt("f_b3r", [1, D])
    f.PEV = sbt("f_pev", [3, T])
    f.H1 = sbt("f_h1", [64, T])
    flc = c.Burst("s_flc")
    for dst, src in [(f.PEV, pe_rev), (f.W1S, w1), (f.W2S, w2), (f.W3S, w3),
                     (f.TREV, trev), (f.NEGD, negd), (f.B3R, b3r)]:
        flc.load("sync", lambda e, d=dst, s=src: e.dma_start(out=d[:, :], in_=s))
    flc.load("sync", lambda e: e.dma_start(out=f.COLS[:, 0:4], in_=cols))
    ld = flc.done()
    p.emit("vector", lambda e: e.tensor_tensor(out=f.COLS[:, 4:5], in0=f.COLS[:, 0:1],
                                               in1=f.COLS[:, 1:2], op=ALU.mult),
           inc=(c.s_ve, 1), waits=[ld])
    p.emit("vector", lambda e: e.tensor_tensor(out=f.COLS[:, 5:6], in0=f.COLS[:, 2:3],
                                               in1=f.COLS[:, 3:4], op=ALU.mult), inc=(c.s_ve, 1))
    cols_done = p.val(c.s_ve)

    def sin_mlp(dst, rhs, lhsT, fcol, bfcol, extra):
        for tt in range(4):
            g = c.bguard(c.PA)
            p.emit("tensor", lambda e, t=tt, l=lhsT, r=rhs: e.matmul(
                c.PA[0:64, 0:512], l, r[:, t * 512:(t + 1) * 512], start=True, stop=True),
                inc=(c.s_pe, 1), waits=[g] + (extra if tt == 0 else []))
            mm = p.val(c.s_pe)
            sl = slice(tt * 512, (tt + 1) * 512)
            TMP = c.SCR1[0:64, 0:512]
            p.emit("vector", lambda e, sl=sl, fcol=fcol, bfcol=bfcol: e.tensor_scalar(
                out=dst[:, sl], in0=c.PA[0:64, 0:512], scalar1=fcol, scalar2=bfcol,
                op0=ALU.mult, op1=ALU.add), inc=(c.s_ve, 1), waits=[(c.s_pe, mm)])
            c.set_bguard(c.PA, c.s_ve, p.val(c.s_ve))
            p.emit("vector", lambda e, sl=sl: e.tensor_scalar(
                out=TMP, in0=dst[:, sl], scalar1=1.0 / (2 * math.pi), scalar2=None, op0=ALU.mult),
                inc=(c.s_ve, 1))
            p.emit("vector", lambda e: e.tensor_scalar(
                out=TMP, in0=TMP, scalar1=MAGIC, scalar2=MAGIC, op0=ALU.add, op1=ALU.subtract),
                inc=(c.s_ve, 1))
            p.emit("vector", lambda e, sl=sl: e.scalar_tensor_tensor(
                out=dst[:, sl], in0=TMP, scalar=-2 * math.pi, in1=dst[:, sl],
                op0=ALU.mult, op1=ALU.add), inc=(c.s_ve, 1))
            p.emit("vector", lambda e, sl=sl: e.tensor_scalar(
                out=dst[:, sl], in0=dst[:, sl], scalar1=math.pi, scalar2=-math.pi,
                op0=ALU.min, op1=ALU.max), inc=(c.s_ve, 1))
            rr = p.val(c.s_ve)
            p.emit("scalar", lambda e, sl=sl: e.activation(out=dst[:, sl], in_=dst[:, sl], func=AF.Sin),
                   inc=(c.s_sc, 1), waits=[(c.s_ve, rr)])

    sin_mlp(f.H1, f.PEV, f.W1S[0:3, :], f.COLS[:, 1:2], f.COLS[:, 4:5],
            [ld, (c.s_ve, cols_done)])
    h1_done = p.val(c.s_sc)
    sin_mlp(f.H2, f.H1, f.W2S[0:64, :], f.COLS[:, 3:4], f.COLS[:, 5:6], [(c.s_sc, h1_done)])
    f.h2_done = p.val(c.s_sc)
    for hh in range(2):
        g = c.bguard(c.PB)
        p.emit("tensor", lambda e, h=hh: e.matmul(
            c.PB[:, 0:512], c.ONESR[0:1, :], f.B3R[0:1, h * 512:(h + 1) * 512],
            start=True, stop=True), inc=(c.s_pe, 1), waits=[g, ld, (c.s_ve, c.ones_ready)])
        mm = p.val(c.s_pe)
        p.emit("vector", lambda e, h=hh: e.tensor_copy(f.B3BC[:, h * 512:(h + 1) * 512], c.PB[:, 0:512]),
               inc=(c.s_ve, 1), waits=[(c.s_pe, mm)])
        c.set_bguard(c.PB, c.s_ve, p.val(c.s_ve))
    f.b3_done = p.val(c.s_ve)
    p.barrier(c.all_sems)
    for cl in reversed(tcl):
        cl()


def _emit_filter_q(c, q, b):
    p = c.p
    f = c._filt
    sk_free = (c.s_pe, p.val(c.s_pe))
    mmbanks = [c.PA, c.PB]
    dbanks = [c.PD[2], c.PD[3]]
    prev_ta = [0, 0]
    prev_tb = [0, 0]
    prev_ker = [0, 0]
    for lt in range(16):
        ring = lt % 2
        mb = mmbanks[ring]
        db = dbanks[ring]
        gA = c.bguard(mb)
        gB = c.bguard(db)
        d0 = q * 256
        p.emit("tensor", lambda e, lt=lt, d0=d0, mb=mb: e.matmul(
            mb[:, 0:256], f.H2[:, lt * 128:(lt + 1) * 128], f.W3S[0:64, d0:d0 + 256],
            start=True, stop=True),
            inc=(c.s_pe, 1),
            waits=[gA] + ([(c.s_sc, f.h2_done)] if (q == 0 and lt == 0) else []))
        p.emit("tensor", lambda e, lt=lt, d0=d0, db=db: e.matmul(
            db[:, 0:256], f.TREV[0:1, lt * 128:(lt + 1) * 128], f.NEGD[0:1, d0:d0 + 256],
            start=True, stop=True), inc=(c.s_pe, 1), waits=[gB])
        mm = p.val(c.s_pe)
        TA = f.FTA[ring]
        TB = f.FTB[ring]
        p.emit("scalar", lambda e, db=db, TA=TA: e.activation(out=TA[:, :], in_=db[:, 0:256], func=AF.Exp),
               inc=(c.s_sc, 1), waits=[(c.s_pe, mm), (c.s_gp, prev_ker[ring])])
        sc_exp = p.val(c.s_sc)
        c.set_bguard(db, c.s_sc, sc_exp)
        p.emit("vector", lambda e, mb=mb, TB=TB, d0=d0: e.tensor_tensor(
            out=TB[:, :], in0=mb[:, 0:256], in1=f.B3BC[:, d0:d0 + 256], op=ALU.add),
            inc=(c.s_ve, 1),
            waits=[(c.s_pe, mm), (c.s_gp, prev_ker[ring])] + ([(c.s_ve, f.b3_done)] if (q == 0 and lt == 0) else []))
        tb_v = p.val(c.s_ve)
        c.set_bguard(mb, c.s_ve, tb_v)
        prev_ker[ring] = p.emit("gpsimd", lambda e, lt=lt, TA=TA, TB=TB: e.tensor_tensor(
            out=b.SK[:, lt * 512 + 256:(lt + 1) * 512], in0=TB[:, :], in1=TA[:, :], op=ALU.mult),
            inc=(c.s_gp, 1),
            waits=[(c.s_sc, sc_exp), (c.s_ve, tb_v)] + ([sk_free] if lt == 0 else []))
    p.emit("vector", lambda e: e.engine_nop(), inc=(c.s_ve, 1), waits=[(c.s_gp, p.val(c.s_gp))])
    return p.val(c.s_ve)


def _emit_conv_l0(c):
    p = c.p
    fbias = c.din("l0a_fbias", (128, 8))
    bout = c.din("l0a_bout", (128, 8))
    wout = c.din("l0a_wout", (D, D))
    _emit_filter_common(c)
    b = _alloc_conv_bundle(c, True)
    FB, fb_close = c.sb_scoped("l0fb", [128, 16])
    cfb = c.Burst("s_cfb")
    cfb.load("sync", lambda e: e.dma_start(out=FB[:, 0:8], in_=fbias))
    cfb.load("sync", lambda e: e.dma_start(out=FB[:, 8:16], in_=bout))
    fb_ld = cfb.done()
    xv = c.Burst("s_xv")
    x0b = c.Burst("s_x0b")

    wout_guard = [None]
    tr_guard = [None]
    yq_wout_guard = [0]
    wos = c.Stream("s_wo", 1)
    pending_wout = [None]

    def emit_wout(q, yq_tt, wout_ld):
        tmp2_rd = [0]
        for tt in range(4):
            for do in range(DC):
                bank = c.PD[(tt * DC + do) % 4]
                gb = c.bguard(bank)
                for ccl in range(2):
                    w = [gb] if ccl == 0 else []
                    if ccl == 0 and do == 0:
                        w += [(c.s_ve, max(yq_tt.get((0, tt), 0), yq_tt.get((1, tt), 0)))]
                    if ccl == 0 and tt == 0 and do == 0:
                        w += [wout_ld]
                    p.emit("tensor", lambda e, ccl=ccl, do=do, tt=tt, ba=bank: e.matmul(
                        ba[:, 0:512], b.WOUTB[:, ccl * D + do * 128:ccl * D + (do + 1) * 128],
                        b.X0Q[:, ccl * T + tt * 512:ccl * T + (tt + 1) * 512],
                        start=(ccl == 0), stop=(ccl == 1)),
                        inc=(c.s_pe, 1) if ccl == 1 else None, waits=w)
                mm = p.val(c.s_pe)
                if q == 3:
                    TMP2 = c.SCR1[:, 512:1024]
                    p.emit("scalar", lambda e, do=do, ba=bank: e.activation(
                        out=TMP2, in_=ba[:, 0:512], func=AF.Identity,
                        bias=FB[:, 8 + do:9 + do], scale=1.0),
                        inc=(c.s_sc, 1), waits=[(c.s_pe, mm), (c.s_ve, tmp2_rd[0])])
                    sv = p.val(c.s_sc)
                    p.emit("vector", lambda e, do=do, tt=tt: e.tensor_tensor(
                        out=c.X[do][:, tt * 512:(tt + 1) * 512],
                        in0=c.X[do][:, tt * 512:(tt + 1) * 512],
                        in1=TMP2, op=ALU.add),
                        inc=(c.s_ve, 1), waits=[(c.s_sc, sv)])
                    tmp2_rd[0] = p.val(c.s_ve)
                    c.set_bguard(bank, c.s_sc, sv)
                else:
                    p.emit("vector", lambda e, do=do, tt=tt, ba=bank: e.tensor_tensor(
                        out=c.X[do][:, tt * 512:(tt + 1) * 512],
                        in0=c.X[do][:, tt * 512:(tt + 1) * 512],
                        in1=ba[:, 0:512], op=ALU.add),
                        inc=(c.s_ve, 1), waits=[(c.s_pe, mm)])
                    c.set_bguard(bank, c.s_ve, p.val(c.s_ve))
        wout_guard[0] = (c.s_pe, p.val(c.s_pe))
        yq_wout_guard[0] = p.val(c.s_pe)

    for q in range(4):
        ker_ready = _emit_filter_q(c, q, b)
        g = tr_guard[0]
        for ccl in range(2):
            cc = 2 * q + ccl
            xv.load("sync", lambda e, cc=cc, ccl=ccl: e.dma_start(
                out=b.V2Q[:, ccl * T:(ccl + 1) * T], in_=c.v2_spill[cc * 128:(cc + 1) * 128, :]),
                waits=(list(c.spills_done) if (q == 0 and ccl == 0) else []) + ([g] if (g and ccl == 0) else []))
        sig_ld = xv.done()

        def src_fn(i):
            nt, ccl = divmod(i, 2)
            return b.V2Q[:, ccl * T + nt * 128:ccl * T + (nt + 1) * 128]

        def dst_fn(i):
            nt, ccl = divmod(i, 2)
            return b.SK[:, nt * 512 + ccl * 128:nt * 512 + (ccl + 1) * 128]

        p.emit("vector", lambda e: e.engine_nop(), inc=(c.s_ve, 1), waits=[sig_ld])
        sig_rdy = p.val(c.s_ve)
        sig_tm = _transpose_to(c, src_fn, dst_fn, 32, sig_rdy)
        tr_guard[0] = (c.s_pe, p.val(c.s_pe))

        if pending_wout[0] is not None:
            emit_wout(*pending_wout[0])
        g2 = yq_wout_guard[0]
        for ccl in range(2):
            cc = 2 * q + ccl
            x0b.load("sync", lambda e, cc=cc, ccl=ccl: e.dma_start(
                out=b.X0Q[:, ccl * T:(ccl + 1) * T], in_=c.x0_spill[cc * 128:(cc + 1) * 128, :]),
                waits=([(c.s_pe, g2)] if (ccl == 0 and g2) else []) + (list(c.spills_done) if (q == 0 and ccl == 0) else []))
        x0_ld = x0b.done()
        wout_ld = wos.load("gpsimd", 0, lambda e, q=q: e.dma_start(
            out=b.WOUTB.rearrange("pp (ccl do) -> pp ccl do", do=D),
            in_=wout[q * 256:(q + 1) * 256, :].rearrange("(ccl pp) do -> pp ccl do", pp=128)),
            waits=[wout_guard[0]] if wout_guard[0] else [])

        yq_tt = {}
        tmp_rd = [0]

        def out_cb(ccl, tt, bank, waits, q=q, x0_ld=x0_ld, yq_tt=yq_tt, tmp_rd=tmp_rd):
            TMP = c.SCR1[:, 0:512]
            p.emit("scalar", lambda e, ba=bank, q=q, ccl=ccl: e.activation(
                out=TMP, in_=ba[:, 0:512], func=AF.Identity,
                bias=FB[:, 2 * q + ccl:2 * q + ccl + 1], scale=1.0),
                inc=(c.s_sc, 1), waits=waits + [fb_ld, (c.s_ve, tmp_rd[0])])
            sv = p.val(c.s_sc)
            p.emit("vector", lambda e, ccl=ccl, tt=tt: e.tensor_tensor(
                out=b.X0Q[:, ccl * T + tt * 512:ccl * T + (tt + 1) * 512], in0=TMP,
                in1=b.X0Q[:, ccl * T + tt * 512:ccl * T + (tt + 1) * 512], op=ALU.mult),
                inc=(c.s_ve, 1), waits=[(c.s_sc, sv), x0_ld, (c.s_pe, yq_wout_guard[0])])
            c.set_bguard(bank, c.s_ve, p.val(c.s_ve))
            yq_tt[(ccl, tt)] = p.val(c.s_ve)
            tmp_rd[0] = p.val(c.s_ve)

        _conv_quarter(c, b, sig_tm, ker_ready, out_cb)
        pending_wout[0] = (q, yq_tt, wout_ld)
    emit_wout(*pending_wout[0])
    p.barrier(c.all_sems)
    fb_close()
    for cl in reversed(b.closers):
        cl()
    for cl in reversed(c._filt.closers):
        cl()


def _emit_omega_residual(c):
    p = c.p
    om = c.din("l1a_omega", (128, 8))
    OM, om_close = c.sb_scoped("l1om", [128, 8])
    omb = c.Burst("s_om")
    omb.load("sync", lambda e: e.dma_start(out=OM[:, :], in_=om))
    ld = omb.done()
    TMP = c.SCR1[:, 0:512]
    prev = 0
    for dc in range(DC):
        for tt in range(4):
            w = [(c.s_ve, prev)]
            if dc == 0 and tt == 0:
                w += [ld, (c.s_ve, c.xn_ready)]
            p.emit("scalar", lambda e, dc=dc, tt=tt, xn=c.XN[dc]: e.activation(
                out=TMP, in_=xn[:, tt * 512:(tt + 1) * 512],
                func=AF.Identity, bias=c.ONES[:, 3:4], scale=OM[:, dc:dc + 1]),
                inc=(c.s_sc, 1), waits=w)
            sv = p.val(c.s_sc)
            p.emit("vector", lambda e, dc=dc, tt=tt: e.tensor_tensor(
                out=c.X[dc][:, tt * 512:(tt + 1) * 512],
                in0=c.X[dc][:, tt * 512:(tt + 1) * 512],
                in1=TMP, op=ALU.add),
                inc=(c.s_ve, 1), waits=[(c.s_sc, sv)])
            prev = p.val(c.s_ve)
    c.omega_done = prev
    p.barrier(c.all_sems)
    for _e in ENGINES:
        p.pending[_e] = p.pending[_e] + [ld]
    om_close()


def _emit_ffn_l1(c):
    p = c.p
    XN = c.XN
    p.barrier(c.all_sems)
    E, M = 9, 2048
    PIECE, MSL, NBUF = 8, 2, 2
    NP = (M // 128) // PIECE
    NSL = PIECE // MSL

    gw = c.din("l1f_gate9", (E, D, M))
    uw = c.din("l1f_up9", (E, D, M))
    dw = c.din("l1f_down9", (E, M, D))

    closers = []

    def sbsc(name, shape, dt=F32):
        h, cl = c.sb_scoped(name, shape, dt)
        closers.append(cl)
        return h

    GB = [sbsc(f"l1gb{i}", [128, DC * MSL * 128], BF16) for i in range(NBUF)]
    UB = [sbsc(f"l1ub{i}", [128, DC * MSL * 128], BF16) for i in range(NBUF)]
    DP = sbsc("l1dp", [128, PIECE * D], BF16)
    HP = [sbsc(f"l1hp{i}", [128, T], BF16) for i in range(PIECE)]
    WBC = [sbsc(f"l1wbc{i}", [128, T], BF16) for i in range(2)]
    SG = [sbsc(f"l1sg{i}", [128, 512], BF16) for i in range(2)]

    guards = {"g": {}, "u": {}}
    cnts = {"g": [0], "u": [0]}
    slab_ld_val = {"g": {}, "u": {}}
    gstr = c.Stream("s_gs", 2)
    ustr = c.Stream("s_us", 2)
    dstr = c.Stream("s_dpw", 1)

    def load_slab(kind, e_, mp, sl):
        src_t = gw if kind == "g" else uw
        bufs = GB if kind == "g" else UB
        i = cnts[kind][0]
        buf = bufs[i % NBUF]
        m0 = (mp * PIECE + sl * MSL) * 128
        src = src_t[e_, :, m0:m0 + MSL * 128].rearrange("(dc pp) m -> pp dc m", pp=128)
        dst = buf.rearrange("pp (dc m) -> pp dc m", m=MSL * 128)
        g = guards[kind].get(i % NBUF)
        st = gstr if kind == "g" else ustr
        slab_ld_val[kind][i] = st.load("gpsimd", i % NBUF, lambda en: en.dma_start(out=dst, in_=src),
                                       waits=[g] if g else [])
        cnts[kind][0] += 1
        return i

    dpg = [None]

    def load_dp(e_, mp):
        src = dw[e_, mp * PIECE * 128:(mp + 1) * PIECE * 128, :].rearrange("(mi pp) h -> pp mi h", pp=128)
        dst = DP.rearrange("pp (mi h) -> pp mi h", h=D)
        g = dpg[0]
        return dstr.load("gpsimd", 0, lambda en: en.dma_start(out=dst, in_=src),
                         waits=[g] if g else [])

    order = [(e_, mp) for e_ in range(E) for mp in range(NP)]
    sids = {}
    for k in range(NBUF):
        sids[("g", 0, 0, k)] = load_slab("g", 0, 0, k)
        sids[("u", 0, 0, k)] = load_slab("u", 0, 0, k)

    ph_guard = [0, 0]
    pu_guard = [0, 0]
    pdown_guard = {2: 0, 3: 0}
    sg_guard = [0, 0]
    h_guard = 0
    wbc_done = {}

    for oi, (e_, mp) in enumerate(order):
        gated = e_ < 8
        if mp == 0 and gated:
            wbc_done[e_ % 2] = _mk_wbc(c, e_, WBC[e_ % 2])
        dp_ld = load_dp(e_, mp)
        for mc_i in range(PIECE):
            sl, mi = divmod(mc_i, MSL)
            gsid = sids[("g", e_, mp, sl)]
            usid = sids[("u", e_, mp, sl)]
            for tt in range(4):
                bank = (mc_i * 4 + tt) % 2
                w_g = [(c.s_sc, ph_guard[bank])]
                w_u = [(c.s_ve, pu_guard[bank])]
                if mi == 0 and tt == 0:
                    w_g.append(slab_ld_val["g"][gsid])
                    w_u.append(slab_ld_val["u"][usid])
                if oi == 0 and mc_i == 0 and tt == 0:
                    w_g.append((c.s_ve, c.xn_ready))
                for kc in range(DC):
                    lhsT = GB[gsid % NBUF][:, (kc * MSL + mi) * 128:(kc * MSL + mi + 1) * 128]
                    p.emit("tensor", lambda en, l=lhsT, k=kc, b=bank, t=tt:
                           en.matmul(c.PH[b][:, :], l, XN[k][:, t * 512:(t + 1) * 512],
                                     start=(k == 0), stop=(k == DC - 1)),
                           inc=(c.s_pe, 1) if kc == DC - 1 else None,
                           waits=w_g if kc == 0 else ())
                g_mm = p.val(c.s_pe)
                for kc in range(DC):
                    lhsT = UB[usid % NBUF][:, (kc * MSL + mi) * 128:(kc * MSL + mi + 1) * 128]
                    p.emit("tensor", lambda en, l=lhsT, k=kc, b=bank, t=tt:
                           en.matmul(c.PD[b][:, :], l, XN[k][:, t * 512:(t + 1) * 512],
                                     start=(k == 0), stop=(k == DC - 1)),
                           inc=(c.s_pe, 1) if kc == DC - 1 else None,
                           waits=w_u if kc == 0 else ())
                u_mm = p.val(c.s_pe)
                p.emit("scalar", lambda en, b=bank: en.activation(
                    out=SG[b][:, :], in_=c.PH[b][:, :], func=AF.Silu),
                    inc=(c.s_sc, 1), waits=[(c.s_pe, g_mm), (c.s_ve, sg_guard[bank])])
                sg_v = p.val(c.s_sc)
                ph_guard[bank] = sg_v
                w_h = [(c.s_sc, sg_v), (c.s_pe, u_mm)]
                if mc_i == 0 and tt == 0:
                    w_h.append((c.s_pe, h_guard))
                if gated:
                    p.emit("vector", lambda en, b=bank: en.tensor_tensor(
                        out=c.SCB[0][:, 0:512], in0=c.PD[b][:, :], in1=SG[b][:, :], op=ALU.mult),
                        inc=(c.s_ve, 1), waits=w_h)
                    p.emit("vector", lambda en, m=mc_i, t=tt, b=e_ % 2: en.tensor_tensor(
                        out=HP[m][:, t * 512:(t + 1) * 512], in0=c.SCB[0][:, 0:512],
                        in1=WBC[b][:, t * 512:(t + 1) * 512], op=ALU.mult),
                        inc=(c.s_ve, 1),
                        waits=[(c.s_ve, wbc_done[e_ % 2])] if (mc_i == 0 and tt == 0) else ())
                else:
                    p.emit("vector", lambda en, m=mc_i, t=tt, b=bank: en.tensor_tensor(
                        out=HP[m][:, t * 512:(t + 1) * 512], in0=c.PD[b][:, :], in1=SG[b][:, :], op=ALU.mult),
                        inc=(c.s_ve, 1), waits=w_h)
                vv = p.val(c.s_ve)
                pu_guard[bank] = vv
                sg_guard[bank] = vv
            if mi == MSL - 1:
                guards["g"][gsid % NBUF] = (c.s_pe, p.val(c.s_pe))
                guards["u"][usid % NBUF] = (c.s_pe, p.val(c.s_pe))
                gidx = oi * NSL + sl + NBUF
                if gidx < len(order) * NSL:
                    noi, nsl = divmod(gidx, NSL)
                    ne, nmp = order[noi]
                    sids[("g", ne, nmp, nsl)] = load_slab("g", ne, nmp, nsl)
                    sids[("u", ne, nmp, nsl)] = load_slab("u", ne, nmp, nsl)
        h_done = p.val(c.s_ve)
        for tt in range(4):
            for dc in range(DC):
                bank = 2 + (tt * DC + dc) % 2
                waits = [(c.s_ve, pdown_guard[bank])]
                if tt == 0 and dc == 0:
                    waits += [(c.s_ve, h_done), dp_ld]
                for mc_i in range(PIECE):
                    p.emit("tensor", lambda en, l=DP[:, mc_i * D + dc * 128:mc_i * D + (dc + 1) * 128],
                           m=mc_i, b=bank, t=tt:
                           en.matmul(c.PD[b][:, :], l, HP[m][:, t * 512:(t + 1) * 512],
                                     start=(m == 0), stop=(m == PIECE - 1)),
                           inc=(c.s_pe, 1) if mc_i == PIECE - 1 else None,
                           waits=waits if mc_i == 0 else ())
                mm = p.val(c.s_pe)
                p.emit("vector", lambda en, d=dc, b=bank, t=tt: en.tensor_tensor(
                    out=c.X[d][:, t * 512:(t + 1) * 512],
                    in0=c.X[d][:, t * 512:(t + 1) * 512],
                    in1=c.PD[b][:, :], op=ALU.add),
                    inc=(c.s_ve, 1), waits=[(c.s_pe, mm)])
                pdown_guard[bank] = p.val(c.s_ve)
        h_guard = p.val(c.s_pe)
        dpg[0] = (c.s_pe, h_guard)
    p.barrier(c.all_sems)
    for cl in reversed(closers):
        cl()
    for cl in reversed(c._router_closers):
        cl()


def _emit_kergen_common(c):
    p = c.p
    f = Ctx()
    c._l1k = f
    f.closers = []

    def sbsc(name, shape, dt=F32):
        h, cl = c.sb_scoped(name, shape, dt)
        f.closers.append(cl)
        return h

    delta = c.din("l1a_delta_t", (128, 128))
    alpha = c.din("l1a_alpha_t", (128, 128))
    betap = c.din("l1a_beta_t", (128, 128))
    gamma = c.din("l1a_gamma_t", (128, 128))
    mask16 = c.din("c_mask16", (128, 16, 128), BF16)
    f.DSB = sbsc("l1k_d", [128, 128])
    f.ASB = sbsc("l1k_a", [128, 128])
    f.BSB = sbsc("l1k_b", [128, 128])
    f.GSB = sbsc("l1k_g", [128, 128])
    f.LQ = sbsc("l1k_lq", [128, 128])
    f.CC = sbsc("l1k_c", [128, 128])
    f.MSK = sbsc("l1k_msk", [128, 16 * 128], BF16)
    f.ABT = [sbsc(f"l1k_abt{i}", [128, 96]) for i in range(4)]
    f.LH2 = [sbsc(f"l1k_lh{i}", [128, 128], BF16) for i in range(4)]
    f.J32 = sbsc("l1k_j32", [128, 32])
    f.I64 = sbsc("l1k_i64", [128, 64])
    f.CC4 = sbsc("l1k_c4", [128, 128])
    f.TGRID = sbsc("l1k_tg", [128, 2048])
    kg = c.Burst("s_kg")
    for dst, src in [(f.DSB, delta), (f.ASB, alpha), (f.BSB, betap), (f.GSB, gamma)]:
        kg.load("sync", lambda e, d=dst, s=src: e.dma_start(out=d[:, :], in_=s))
    kg.load("sync", lambda e: e.dma_start(
        out=f.MSK.rearrange("pp (l cc) -> pp l cc", cc=128), in_=mask16))
    ld = kg.done()
    p.emit("gpsimd", lambda e: e.iota(f.J32[:, :], pattern=[[1, 32]], base=0,
                                      channel_multiplier=0, allow_small_or_imprecise_dtypes=True),
           inc=(c.s_gp, 1))
    p.emit("gpsimd", lambda e: e.iota(f.I64[:, :], pattern=[[1, 64]], base=0,
                                      channel_multiplier=0, allow_small_or_imprecise_dtypes=True),
           inc=(c.s_gp, 1))
    p.emit("gpsimd", lambda e: e.iota(f.TGRID[:, :], pattern=[[1, 2048]], base=0,
                                      channel_multiplier=0, allow_small_or_imprecise_dtypes=True),
           inc=(c.s_gp, 1))
    f.iota_done = p.val(c.s_gp)
    p.emit("scalar", lambda e: e.activation(out=f.DSB[:, :], in_=f.DSB[:, :], func=AF.Sigmoid),
           inc=(c.s_sc, 1), waits=[ld])
    p.emit("scalar", lambda e: e.activation(out=f.ASB[:, :], in_=f.ASB[:, :], func=AF.Sigmoid),
           inc=(c.s_sc, 1))
    sg = p.val(c.s_sc)
    p.emit("vector", lambda e: e.tensor_tensor(out=f.LQ[:, :], in0=f.DSB[:, :], in1=f.ASB[:, :],
                                               op=ALU.mult), inc=(c.s_ve, 1), waits=[(c.s_sc, sg)])
    p.emit("vector", lambda e: e.tensor_scalar(out=f.LQ[:, :], in0=f.LQ[:, :], scalar1=-1.0,
                                               scalar2=1.0, op0=ALU.mult, op1=ALU.add), inc=(c.s_ve, 1))
    lq_v = p.val(c.s_ve)
    p.emit("scalar", lambda e: e.activation(out=f.LQ[:, :], in_=f.LQ[:, :], func=AF.Ln),
           inc=(c.s_sc, 1), waits=[(c.s_ve, lq_v)])
    ln_v = p.val(c.s_sc)
    p.emit("vector", lambda e: e.tensor_tensor(out=f.CC[:, :], in0=f.DSB[:, :], in1=f.BSB[:, :],
                                               op=ALU.mult), inc=(c.s_ve, 1), waits=[(c.s_sc, ln_v)])
    p.emit("vector", lambda e: e.tensor_tensor(out=f.CC[:, :], in0=f.CC[:, :], in1=f.GSB[:, :],
                                               op=ALU.mult), inc=(c.s_ve, 1))
    p.emit("vector", lambda e: e.tensor_scalar(out=f.CC4[:, :], in0=f.CC[:, :], scalar1=0.25,
                                               scalar2=None, op0=ALU.mult), inc=(c.s_ve, 1))
    f.pre_done = p.val(c.s_ve)


def _emit_kergen_l1_q(c, q, b):
    p = c.p
    f = c._l1k
    kt_guard = (c.s_pe, p.val(c.s_pe))
    EBoff = [i * 2048 for i in range(4)]
    KT = [b.INVB[:, 0:2048], b.INVB[:, 2048:4096]]
    eb_guard = {i: (c.s_pe, p.val(c.s_pe)) for i in range(4)}
    lh_guard = [0, 0, 0, 0]
    ab_guard = [None, None, None, None]
    steps = [(chl, ci_l) for chl in range(2) for ci_l in range(16)]
    adone = {}

    def emitA(idx):
        chl, ci_l = steps[idx]
        if ci_l % 2 == 1:
            adone[idx] = None
            return
        ci = (2 * q + chl) * 16 + ci_l
        ring = ci_l % 4
        w = [ab_guard[ring]] if ab_guard[ring] else []
        if idx == 0:
            w += [(c.s_ve, f.pre_done), (c.s_gp, f.iota_done)]
        p.emit("vector", lambda e, ci=ci, r=ring: e.tensor_scalar(
            out=f.ABT[r][:, 0:32], in0=f.J32[:, :], scalar1=f.LQ[:, ci:ci + 1], scalar2=64.0,
            op0=ALU.mult, op1=ALU.mult), inc=(c.s_ve, 1), waits=w)
        p.emit("vector", lambda e, ci=ci, r=ring: e.tensor_scalar(
            out=f.ABT[r][:, 32:96], in0=f.I64[:, :], scalar1=f.LQ[:, ci:ci + 1], scalar2=None,
            op0=ALU.mult), inc=(c.s_ve, 1))
        vv = p.val(c.s_ve)
        p.emit("scalar", lambda e, r=ring: e.activation(out=f.ABT[r][:, :], in_=f.ABT[r][:, :],
                                                        func=AF.Exp),
               inc=(c.s_sc, 1), waits=[(c.s_ve, vv)])
        adone[idx] = p.val(c.s_sc)

    def emitB(idx):
        chl, ci_l = steps[idx]
        ci = (2 * q + chl) * 16 + ci_l
        lb = ci % 16
        ring = ci_l % 4
        ebuf = b.FWDB[:, EBoff[ci_l % 4]:EBoff[ci_l % 4] + 2048]
        g = eb_guard.get(ci_l % 4)
        e_done = adone[idx]
        if ci_l % 2 == 1:
            w = ([g] if g else [])
            if idx == 1:
                w += [(c.s_ve, f.pre_done), (c.s_gp, f.iota_done)]
            p.emit("scalar", lambda e, ci=ci, eb=ebuf: e.activation(
                out=eb, in_=f.TGRID[:, :], func=AF.Exp, scale=f.LQ[:, ci:ci + 1]),
                inc=(c.s_sc, 1), waits=w)
            eg = (c.s_sc, p.val(c.s_sc))
        else:
            p.emit("vector", lambda e, r=ring, eb=ebuf: e.tensor_tensor(
                out=eb.rearrange("pp (a bb) -> pp a bb", bb=64),
                in0=f.ABT[r][:, 0:32].unsqueeze(-1).broadcast_to([128, 32, 64]),
                in1=f.ABT[r][:, 32:96].unsqueeze(1).broadcast_to([128, 32, 64]),
                op=ALU.mult),
                inc=(c.s_ve, 1), waits=[(c.s_sc, e_done)] + ([g] if g else []))
            eg = (c.s_ve, p.val(c.s_ve))
            ab_guard[ring] = eg
        lh_w = [(c.s_pe, lh_guard[ring])]
        if idx == 0:
            lh_w.append((c.s_ve, f.pre_done))
        p.emit("gpsimd", lambda e, lb=lb, ci=ci, r=ring: e.tensor_scalar(
            out=f.LH2[r][:, :], in0=f.MSK[:, lb * 128:(lb + 1) * 128], scalar1=f.CC4[:, ci:ci + 1],
            scalar2=None, op0=ALU.mult),
            inc=(c.s_gp, 1), waits=lh_w)
        lh_done = p.val(c.s_gp)
        for tt in range(4):
            bank = c.PD[tt]
            w = [(c.s_gp, lh_done), eg]
            if ci_l == 0:
                w.append(c.bguard(bank))
            p.emit("tensor", lambda e, t=tt, eb=ebuf, ba=bank, ci_l=ci_l, r=ring: e.matmul(
                ba[:, :], f.LH2[r][:, :], eb[:, t * 512:(t + 1) * 512],
                start=(ci_l == 0), stop=(ci_l == 15)),
                inc=(c.s_pe, 1) if tt == 3 else None,
                waits=w)
        eb_guard[ci_l % 4] = (c.s_pe, p.val(c.s_pe))
        lh_guard[ring] = p.val(c.s_pe)
        if ci_l == 15:
            mm = p.val(c.s_pe)
            for tt in range(4):
                bank = c.PD[tt]
                p.emit("vector", lambda e, t=tt, chl=chl, ba=bank: e.tensor_copy(
                    KT[chl][:, t * 512:(t + 1) * 512], ba[:, :]),
                    inc=(c.s_ve, 1), waits=[(c.s_pe, mm), kt_guard])
                c.set_bguard(bank, c.s_ve, p.val(c.s_ve))

    emitA(0)
    emitA(1)
    emitA(2)
    for idx in range(32):
        if idx + 3 < 32:
            emitA(idx + 3)
        emitB(idx)
    ktfm_done = p.val(c.s_ve)

    def src_fn(i):
        nt, chl = divmod(i, 2)
        return KT[chl][:, nt * 128:(nt + 1) * 128]

    def dst_fn(i):
        nt, chl = divmod(i, 2)
        return b.SK[:, nt * 512 + 256 + chl * 128:nt * 512 + 256 + (chl + 1) * 128]

    return _transpose_to(c, src_fn, dst_fn, 32, ktfm_done, banks=[c.PA, c.PB, c.PD[2], c.PD[3]])


def _emit_conv_l1(c):
    p = c.p
    p.barrier(c.all_sems)
    _emit_kergen_common(c)
    b = _alloc_conv_bundle(c, False)
    for q in range(4):
        ker_ready = _emit_kergen_l1_q(c, q, b)
        for _i in range(4):
            b.fwd_guard[_i] = (c.s_pe, c.p.val(c.s_pe))
            b.inv_guard[_i] = (c.s_pe, c.p.val(c.s_pe))
        XN = c.XN

        def src_fn(i, q=q, XN=XN):
            nt, ccl = divmod(i, 2)
            return XN[2 * q + ccl][:, nt * 128:(nt + 1) * 128]

        def dst_fn(i):
            nt, ccl = divmod(i, 2)
            return b.SK[:, nt * 512 + ccl * 128:nt * 512 + (ccl + 1) * 128]

        sig_tm = _transpose_to(c, src_fn, dst_fn, 32, ker_ready, banks=[c.PA, c.PB, c.PD[2], c.PD[3]])

        def out_cb(ccl, tt, bank, waits, q=q):
            cc = 2 * q + ccl
            p.emit("vector", lambda e, cc=cc, tt=tt, ba=bank: e.tensor_tensor(
                out=c.X[cc][:, tt * 512:(tt + 1) * 512],
                in0=c.X[cc][:, tt * 512:(tt + 1) * 512],
                in1=ba[:, 0:512], op=ALU.add),
                inc=(c.s_ve, 1), waits=waits)
            c.set_bguard(bank, c.s_ve, p.val(c.s_ve))

        _conv_quarter(c, b, sig_tm, ker_ready, out_cb)
    p.barrier(c.all_sems)
    for cl in reversed(b.closers):
        cl()
    for cl in reversed(c._l1k.closers):
        cl()


def _prep_inputs(inputs):
    cc = host_consts()
    asf = lambda a: np.ascontiguousarray(np.asarray(a), dtype=np.float32)

    common = {"c_fwd": cc["fwd_t"], "c_inv": cc["inv_t"], "c_id": cc["ident"],
              "c_idf": np.eye(128, dtype=np.float32), "c_sel": cc["sel"]}

    d0 = dict(common)
    w1 = asf(inputs["L0_ffn_W1"])
    d0["l0f_w1t"] = np.ascontiguousarray(w1.transpose(0, 2, 1))
    d0["l0f_w2t"] = np.ascontiguousarray(asf(inputs["L0_ffn_W2"]).transpose(0, 2, 1))
    b1 = asf(inputs["L0_ffn_b1"])
    d0["l0f_b1c"] = np.ascontiguousarray(b1.reshape(8, 32, 128).transpose(2, 0, 1).reshape(128, 256))
    d0["l0f_b2"] = asf(inputs["L0_ffn_b2"])
    d0["l0_wr_t"] = np.ascontiguousarray(asf(inputs["L0_ffn_Wr"]).T)
    d0["l0a_win"] = asf(inputs["L0_attn_W_in"])
    d0["l0a_binc"] = np.ascontiguousarray(asf(inputs["L0_attn_b_in"]).reshape(24, 128).T)
    sw = asf(inputs["L0_attn_short_w"])
    d0["l0a_swc"] = np.ascontiguousarray(sw.reshape(24, 128, 3).transpose(1, 0, 2).reshape(128, 72))
    d0["l0a_sbc"] = np.ascontiguousarray(asf(inputs["L0_attn_short_b"]).reshape(24, 128).T)
    d0["l0a_pe_rev"] = cc["pe_rev"]
    d0["l0a_w1"] = asf(inputs["L0_attn_W1"])
    d0["l0a_w2"] = asf(inputs["L0_attn_W2"])
    d0["l0a_w3"] = asf(inputs["L0_attn_W3"])
    d0["l0a_fcols"] = np.ascontiguousarray(np.stack([
        asf(inputs["L0_attn_b1"]), asf(inputs["L0_attn_f1"]),
        asf(inputs["L0_attn_b2"]), asf(inputs["L0_attn_f2"])], axis=1))
    d0["l0a_trev"] = cc["t_rev"]
    d0["l0a_negd"] = cc["negdeltas"]
    d0["l0a_b3r"] = asf(inputs["L0_attn_b3"])[None, :]
    d0["l0a_fbias"] = np.ascontiguousarray(asf(inputs["L0_attn_filter_bias"]).reshape(8, 128).T)
    d0["l0a_bout"] = np.ascontiguousarray(asf(inputs["L0_attn_b_out"]).reshape(8, 128).T)
    d0["l0a_wout"] = asf(inputs["L0_attn_W_out"])

    d1 = dict(common)
    d1["l1_gate_t"] = np.ascontiguousarray(asf(inputs["L1_ffn_gate_w"]).T)
    d1["l1_beta_r16"] = np.ascontiguousarray(np.tile(asf(inputs["L1_ffn_beta"]), 16))[None, :]
    d1["l1f_gate9"] = np.ascontiguousarray(np.concatenate(
        [asf(inputs["L1_ffn_gate_proj_w"]), asf(inputs["L1_ffn_shared_gate_w"])], axis=0))
    d1["l1f_up9"] = np.ascontiguousarray(np.concatenate(
        [asf(inputs["L1_ffn_up_proj_w"]), asf(inputs["L1_ffn_shared_up_w"])], axis=0))
    d1["l1f_down9"] = np.ascontiguousarray(np.concatenate(
        [asf(inputs["L1_ffn_down_proj_w"]), asf(inputs["L1_ffn_shared_down_w"])], axis=0))
    d1["l1a_omega"] = np.ascontiguousarray(asf(inputs["L1_attn_omega"]).reshape(8, 128).T)

    def dn_tile(a):
        return np.ascontiguousarray(asf(a).reshape(128, 128).T)

    d1["l1a_delta_t"] = dn_tile(inputs["L1_attn_delta"])
    d1["l1a_alpha_t"] = dn_tile(inputs["L1_attn_alpha"])
    d1["l1a_beta_t"] = dn_tile(inputs["L1_attn_beta"])
    d1["l1a_gamma_t"] = dn_tile(inputs["L1_attn_gamma"])
    d1["c_mask16"] = cc["mask16"]
    return d0, d1


_BUILD_CACHE = {}


def _get_program(layers):
    key = (tuple(layers), tuple(LIMIT_PHASES) if LIMIT_PHASES is not None else None)
    if key not in _BUILD_CACHE:
        _BUILD_CACHE[key] = build_model(layers)
    return _BUILD_CACHE[key]


def kernel(**inputs):
    d0, d1 = _prep_inputs(inputs)
    x = np.ascontiguousarray(np.asarray(inputs["L0_attn_x"], dtype=np.float32))
    n = 8

    if KNOB("K_UNFUSED"):
        nc0, names0 = _get_program((0,))
        nc1, names1 = _get_program((1,))
        in_maps = []
        for bb in range(n):
            m = {k: v for k, v in d0.items() if k in names0}
            m["x_fm"] = np.ascontiguousarray(x[bb].T)
            in_maps.append(m)
        res0 = run_bass_kernel_spmd(nc0, in_maps, core_ids=list(range(n)))
        in_maps = []
        for bb in range(n):
            m = {k: v for k, v in d1.items() if k in names1}
            m["x_fm"] = np.ascontiguousarray(res0.results[bb]["y_fm"])
            in_maps.append(m)
        res1 = run_bass_kernel_spmd(nc1, in_maps, core_ids=list(range(n)))
        out = np.stack([res1.results[bb]["y_fm"].T for bb in range(n)], axis=0)
        return np.ascontiguousarray(out, dtype=np.float32)

    nc01, names = _get_program((0, 1))
    dall = dict(d0)
    dall.update(d1)
    in_maps = []
    for bb in range(n):
        m = {k: v for k, v in dall.items() if k in names}
        m["x_fm"] = np.ascontiguousarray(x[bb].T)
        in_maps.append(m)
    res = run_bass_kernel_spmd(nc01, in_maps, core_ids=list(range(n)))
    out = np.stack([res.results[bb]["y_fm"].T for bb in range(n)], axis=0)
    return np.ascontiguousarray(out, dtype=np.float32)


# version 15
# speedup vs baseline: 1.0833x; 1.0744x over previous
import contextlib
import math
import numpy as np
import ml_dtypes

import concourse.bass as bass
import concourse.mybir as mybir
from concourse.bass_utils import run_bass_kernel_spmd

F32 = mybir.dt.float32
BF16 = mybir.dt.bfloat16
AF = mybir.ActivationFunctionType
ALU = mybir.AluOpType
AX = mybir.AxisListType

T = 2048
D = 1024
DC = D // 128
NKC = 17
NFFT = 4096
MAGIC = 1.5 * 2.0 ** 23
ENGINES = ("tensor", "vector", "scalar", "gpsimd", "sync")
LIMIT_PHASES = None


def _ph(name):
    return LIMIT_PHASES is None or name in LIMIT_PHASES


import os as _os
KNOB = lambda n: _os.environ.get(n, "") == "1"


class Prog:
    def __init__(self, nc, es):
        self.nc = nc
        self.es = es
        self.q = {e: [] for e in ENGINES}
        self.semv = {}
        self.pending = {e: [] for e in ENGINES}

    def sem(self, name):
        h = self.es.enter_context(self.nc.semaphore(name))
        self.semv[id(h)] = 0
        return h

    def val(self, sem):
        return self.semv[id(sem)]

    def barrier(self, sems):
        waits = [(s, self.val(s)) for s in sems]
        for e in ENGINES:
            self.pending[e] = self.pending[e] + list(waits)

    def set_chain(self, eng_sems):
        self.chain = dict(eng_sems)

    def emit(self, eng, fn, inc=None, waits=()):
        waits = list(self.pending[eng]) + [(s, v) for (s, v) in waits if v > 0]
        self.pending[eng] = []
        ch = getattr(self, "chain", {}).get(eng)
        if ch is not None and (inc is None or inc[0] is ch):
            waits.append((ch, self.semv[id(ch)]))
            if inc is None:
                inc = (ch, 1)
        waits = [(s, v) for (s, v) in waits if v > 0]

        def item(engine):
            for (s, v) in waits:
                engine.wait_ge(s, v)
            inst = fn(engine)
            if inc is not None:
                inst.then_inc(inc[0], inc[1])

        self.q[eng].append(item)
        if inc is not None:
            self.semv[id(inc[0])] += inc[1]
            return self.semv[id(inc[0])]
        return None

    def wait(self, eng, sem, value):
        if value <= 0:
            return

        def item(engine):
            engine.wait_ge(sem, value)

        self.q[eng].append(item)

    def finalize(self):
        blk = self.es.enter_context(self.nc.Block())
        for e in ENGINES:
            items = self.q[e]
            if not items:
                continue

            def mk(its):
                def body(engine):
                    for it in its:
                        it(engine)
                return body

            getattr(blk, e)(mk(items))
        return blk


_CONST_CACHE = {}


def host_consts():
    if _CONST_CACHE:
        return _CONST_CACHE
    bf = ml_dtypes.bfloat16
    n = np.arange(T, dtype=np.float64)
    k = np.arange(NKC * 128, dtype=np.float64)
    ang = 2.0 * np.pi * np.outer(n, k) / NFFT
    fwd_cos = np.cos(ang)
    fwd_sin = -np.sin(ang)

    def tile_fwd(m):
        r = m.reshape(16, 128, NKC, 128)
        return np.ascontiguousarray(r.transpose(2, 1, 0, 3)).astype(bf)

    fwd_t = np.stack([tile_fwd(fwd_cos), tile_fwd(fwd_sin)])

    w = np.full(NKC * 128, 2.0 / NFFT)
    w[0] = 1.0 / NFFT
    w[2048] = 1.0 / NFFT
    w[2049:] = 0.0
    tau = np.arange(T, dtype=np.float64)
    ang2 = 2.0 * np.pi * np.outer(k, tau) / NFFT
    inv_cos = w[:, None] * np.cos(ang2)
    inv_sin = -w[:, None] * np.sin(ang2)

    def tile_inv(m):
        return np.ascontiguousarray(m.reshape(NKC, 128, T)).astype(bf)

    inv_t = np.stack([tile_inv(inv_cos), tile_inv(inv_sin)])

    ident = np.eye(128, dtype=np.float32).astype(bf)

    sel = np.zeros((8, 8 * 128), np.float32)
    for e in range(8):
        sel[e, e * 128:(e + 1) * 128] = 1.0
    sel = sel.astype(bf)

    mask16 = np.zeros((16, 128, 128), np.float32)
    for lb in range(16):
        for pp in range(128):
            mask16[lb, pp, lb * 8 + pp // 16] = 1.0
    mask16 = np.ascontiguousarray(mask16.transpose(1, 0, 2)).astype(bf)

    l = np.arange(T, dtype=np.float64)
    t_pos = l / (T - 1)
    phase = 1e-4 * (2.0 * np.pi * l / T)
    pe = np.stack([t_pos, np.cos(phase), -np.sin(phase)], axis=0)
    pe_rev = np.ascontiguousarray(pe[:, ::-1]).astype(np.float32)
    t_rev = np.ascontiguousarray(t_pos[::-1])[None, :].astype(np.float32)
    dmax, dmin = (math.log(0.01) / 0.3, math.log(0.01) / 1.5)
    deltas = np.abs(np.linspace(dmax, dmin, D)).astype(np.float32)
    negdeltas = np.ascontiguousarray((-deltas)[None, :])

    _CONST_CACHE.update(dict(
        fwd_t=fwd_t, inv_t=inv_t, ident=ident, sel=sel, mask16=mask16,
        pe_rev=pe_rev, t_rev=t_rev, negdeltas=negdeltas))
    return _CONST_CACHE


class Ctx:
    pass


def build_model(layers=(0, 1)):
    nc = bass.Bass("TRN2", target_bir_lowering=False)
    es = contextlib.ExitStack()
    p = Prog(nc, es)
    c = Ctx()
    c.nc, c.es, c.p = nc, es, p
    c.input_names = []

    def din(name, shape, dt=F32):
        c.input_names.append(name)
        return nc.dram_tensor(name, list(shape), dt, kind="ExternalInput").ap()

    c.din = din
    c.sb = lambda name, shape, dt=F32: es.enter_context(nc.sbuf_tensor(name, shape, dt))
    c.ps = lambda name, shape, dt=F32: es.enter_context(nc.psum_tensor(name, shape, dt))

    c._uid = [0]

    def sb_scoped(name, shape, dt=F32):
        c._uid[0] += 1
        cm = nc.sbuf_tensor(f"{name}_u{c._uid[0]}", shape, dt)
        h = cm.__enter__()
        return h, (lambda cm=cm: cm.__exit__(None, None, None))

    c.sb_scoped = sb_scoped

    c.s_pe = p.sem("s_pe")
    c.s_ve = p.sem("s_ve")
    c.s_sc = p.sem("s_sc")
    c.s_gp = p.sem("s_gp")
    c.s_out = p.sem("s_out")
    c.s_dbg = p.sem("s_dbg")
    c.all_sems = [c.s_pe, c.s_ve, c.s_sc, c.s_gp]
    p.set_chain({"vector": c.s_ve, "scalar": c.s_sc, "gpsimd": c.s_gp})

    class Burst:
        def __init__(self, name):
            self.sem = p.sem(name)
            self.n = 0

        def load(self, eng, fn, waits=()):
            p.emit(eng, fn, inc=(self.sem, 16), waits=waits)
            self.n += 1

        def done(self):
            return (self.sem, 16 * self.n)

    class Stream:
        def __init__(self, name, nslots):
            self.sems = [p.sem(f"{name}{i}") for i in range(nslots)]
            self.cnt = [0] * nslots

        def load(self, eng, slot, fn, waits=()):
            p.emit(eng, fn, inc=(self.sems[slot], 16), waits=waits)
            self.cnt[slot] += 1
            return (self.sems[slot], 16 * self.cnt[slot])

    c.Burst, c.Stream = Burst, Stream

    x_in = din("x_fm", (D, T))
    y_out = nc.dram_tensor("y_fm", [D, T], F32, kind="ExternalOutput").ap()
    c.x0_spill = nc.dram_tensor("x0_spill", [D, T], BF16, kind="Internal").ap()
    c.v2_spill = nc.dram_tensor("v2_spill", [D, T], BF16, kind="Internal").ap()

    c.fwd_t = din("c_fwd", (2, NKC, 128, 16, 128), BF16)
    c.inv_t = din("c_inv", (2, NKC, 128, T), BF16)
    id_d = din("c_id", (128, 128), BF16)
    idf_d = din("c_idf", (128, 128))
    sel_d = din("c_sel", (8, 8 * 128), BF16)

    c.X = [c.sb(f"X{i}", [128, T]) for i in range(DC)]
    c.ID = c.sb("ID", [128, 128], BF16)
    c.IDF = c.sb("IDF", [128, 128])
    c.SEL = c.sb("SEL", [8, 8 * 128], BF16)
    c.ONES = c.sb("ONESC", [128, 4])
    c.ONESR = c.sb("ONESR", [1, 128])
    c.SCR0 = c.sb("SCR0", [128, T])
    c.SCR1 = c.sb("SCR1", [128, 1024])
    c.SCB = [c.sb(f"SCB{i}", [128, T], BF16) for i in range(2)]
    c.STATS = c.sb("STATS", [1, 1024])

    c.PH = [c.ps(f"PH{i}", [128, 512]) for i in range(2)]
    c.PD = [c.ps(f"PD{i}", [128, 512]) for i in range(4)]
    c.PA = c.ps("PA", [128, 512])
    c.PB = c.ps("PB", [128, 512])

    c.bank_guard = {}
    c.bguard = lambda b: c.bank_guard.get(id(b), (c.s_ve, 0))
    c.set_bguard = lambda b, s, v: c.bank_guard.__setitem__(id(b), (s, v))

    bin_ = c.Burst("s_bin")
    bin_.load("sync", lambda e: e.dma_start(out=c.ID[:, :], in_=id_d))
    bin_.load("sync", lambda e: e.dma_start(out=c.IDF[:, :], in_=idf_d))
    bin_.load("sync", lambda e: e.dma_start(out=c.SEL[:, :], in_=sel_d))
    for dc in range(DC):
        bin_.load("sync", lambda e, dc=dc: e.dma_start(out=c.X[dc][:, :], in_=x_in[dc * 128:(dc + 1) * 128, :]))
    c.x_loaded = bin_.done()
    for col, v in [(0, 1.0), (1, 1e-5), (2, -math.pi), (3, 0.0)]:
        p.emit("vector", lambda e, col=col, v=v: e.memset(c.ONES[:, col:col + 1], v), inc=(c.s_ve, 1))
    p.emit("vector", lambda e: e.memset(c.ONESR[:, :], 1.0), inc=(c.s_ve, 1))
    c.ones_ready = p.val(c.s_ve)
    c.id_ready = c.x_loaded
    for _e in ENGINES:
        p.pending[_e] = [c.x_loaded]

    for li in layers:
        _emit_layer(c, li)

    p.barrier(c.all_sems)
    for dc in range(DC):
        p.emit("sync", lambda e, dc=dc: e.dma_start(out=y_out[dc * 128:(dc + 1) * 128, :], in_=c.X[dc][:, :]),
               inc=(c.s_out, 16))
    p.wait("sync", c.s_out, p.val(c.s_out))

    p.finalize()
    es.close()
    return nc, c.input_names


def _emit_layer(c, li):
    p = c.p
    c.XN, xn_close = c.sb_scoped(f"XN{li}", [128 * 1, T], BF16) if False else (None, None)
    c.XN = [None] * DC
    closers = []
    for i in range(DC):
        h, cl = c.sb_scoped(f"XN{li}_{i}", [128, T], BF16)
        c.XN[i] = h
        closers.append(cl)
    if _ph("ln"):
        _emit_ln(c, li)
    if li == 0:
        if _ph("router"):
            _emit_router(c, li)
        if _ph("ffn"):
            _emit_ffn_l0(c)
        elif _ph("router"):
            p.barrier(c.all_sems)
            for cl in reversed(c._router_closers):
                cl()
        if _ph("uphase"):
            _emit_uphase(c)
        p.barrier(c.all_sems)
        for cl in reversed(closers):
            cl()
        if _ph("conv"):
            _emit_conv_l0(c)
    else:
        if _ph("omega"):
            _emit_omega_residual(c)
        if _ph("router"):
            _emit_router(c, li)
        if _ph("ffn"):
            _emit_ffn_l1(c)
        elif _ph("router"):
            p.barrier(c.all_sems)
            for cl in reversed(c._router_closers):
                cl()
        if _ph("conv"):
            _emit_conv_l1(c)
        p.barrier(c.all_sems)
        for cl in reversed(closers):
            cl()


def _emit_ln(c, li):
    p = c.p
    XN = c.XN
    p.barrier(c.all_sems)
    SQ = c.SCR1[:, 0:512]
    MBC = [c.SCR1[:, 512:1024], c.SCR0[:, 0:512]]
    RBC = c.SCR0[:, 1024:2048]
    mm_guard = 0
    st_dones = {}
    bc_state = {}

    def emit_sums(tt):
        nonlocal mm_guard
        ts_ = slice(tt * 512, (tt + 1) * 512)
        gA = c.bguard(c.PA)
        gB = c.bguard(c.PB)
        for dc in range(DC):
            sqv = p.emit("scalar", lambda e, dc=dc, ts_=ts_: e.activation(
                out=SQ, in_=c.X[dc][:, ts_], func=AF.Square),
                inc=(c.s_sc, 1), waits=[(c.s_pe, mm_guard)])
            p.emit("tensor", lambda e, dc=dc, ts_=ts_: e.matmul(
                c.PA[0:1, 0:512], c.ONES[:, 0:1], c.X[dc][:, ts_],
                start=(dc == 0), stop=(dc == DC - 1)),
                inc=(c.s_pe, 1), waits=[gA] if dc == 0 else ())
            p.emit("tensor", lambda e, dc=dc: e.matmul(
                c.PB[0:1, 0:512], c.ONES[:, 0:1], SQ,
                start=(dc == 0), stop=(dc == DC - 1)),
                inc=(c.s_pe, 1), waits=[(c.s_sc, sqv)] + ([gB] if dc == 0 else []))
            mm_guard = p.val(c.s_pe)
        return p.val(c.s_pe)

    def emit_stats(tt, mm_done, prev_bc_pe):
        mean = c.STATS[0:1, 0:512]
        rstd = c.STATS[0:1, 512:1024]
        p.emit("vector", lambda e, mean=mean: e.tensor_scalar(
            out=mean, in0=c.PA[0:1, 0:512], scalar1=1.0 / 1024.0, scalar2=None, op0=ALU.mult),
            inc=(c.s_ve, 1), waits=[(c.s_pe, mm_done), (c.s_pe, prev_bc_pe)])
        p.emit("vector", lambda e, mean=mean, rstd=rstd: e.scalar_tensor_tensor(
            out=rstd, in0=mean, scalar=-1.0, in1=mean, op0=ALU.mult, op1=ALU.mult),
            inc=(c.s_ve, 1))
        p.emit("vector", lambda e, rstd=rstd: e.scalar_tensor_tensor(
            out=rstd, in0=c.PB[0:1, 0:512], scalar=1.0 / 1024.0, in1=rstd, op0=ALU.mult, op1=ALU.add),
            inc=(c.s_ve, 1))
        vvar = p.val(c.s_ve)
        p.emit("scalar", lambda e, rstd=rstd: e.activation(
            out=rstd, in_=rstd, func=AF.Sqrt, bias=c.ONES[0:1, 1:2], scale=1.0),
            inc=(c.s_sc, 1), waits=[(c.s_ve, vvar)])
        vsq = p.val(c.s_sc)
        p.emit("vector", lambda e, rstd=rstd: e.reciprocal(out=rstd, in_=rstd),
               inc=(c.s_ve, 1), waits=[(c.s_sc, vsq)])
        st_done = p.val(c.s_ve)
        c.set_bguard(c.PA, c.s_ve, st_done)
        c.set_bguard(c.PB, c.s_ve, st_done)
        return st_done, mean, rstd

    def emit_bc_xn(tt, st_done, mean, rstd):
        ts_ = slice(tt * 512, (tt + 1) * 512)
        par = tt % 2
        gC = c.bguard(c.PD[0 + par])
        gD = c.bguard(c.PD[2 + par])
        p.emit("tensor", lambda e, mean=mean, par=par: e.matmul(
            c.PD[0 + par][:, 0:512], c.ONESR[0:1, :], mean, start=True, stop=True),
            inc=(c.s_pe, 1), waits=[(c.s_ve, st_done), gC])
        p.emit("tensor", lambda e, rstd=rstd, par=par: e.matmul(
            c.PD[2 + par][:, 0:512], c.ONESR[0:1, :], rstd, start=True, stop=True),
            inc=(c.s_pe, 1), waits=[gD])
        bc_done = p.val(c.s_pe)
        p.emit("vector", lambda e, par=par: e.tensor_copy(MBC[par], c.PD[0 + par][:, 0:512]),
               inc=(c.s_ve, 1), waits=[(c.s_pe, bc_done)])
        p.emit("vector", lambda e, ts_=ts_, par=par: e.tensor_copy(RBC[:, 0:512] if False else c.SCR0[:, ts_][:, 0:512] if False else MBC[par], MBC[par]),
               inc=(c.s_ve, 1)) if False else None
        for dc in range(DC):
            p.emit("vector", lambda e, dc=dc, ts_=ts_, par=par: e.tensor_tensor(
                out=XN[dc][:, ts_], in0=c.X[dc][:, ts_], in1=MBC[par], op=ALU.subtract),
                inc=(c.s_ve, 1))
            p.emit("vector", lambda e, dc=dc, ts_=ts_, par=par: e.tensor_tensor(
                out=XN[dc][:, ts_], in0=XN[dc][:, ts_], in1=c.PD[2 + par][:, 0:512], op=ALU.mult),
                inc=(c.s_ve, 1))
        ve_done = p.val(c.s_ve)
        c.set_bguard(c.PD[0 + par], c.s_ve, ve_done)
        c.set_bguard(c.PD[2 + par], c.s_ve, ve_done)
        return bc_done

    prev_bc = 0
    pend = None
    for tt in range(4):
        mm_done = emit_sums(tt)
        if pend is not None:
            prev_bc = emit_bc_xn(*pend)
        st_done, mean, rstd = emit_stats(tt, mm_done, prev_bc)
        pend = (tt, st_done, mean, rstd)
    emit_bc_xn(*pend)
    c.xn_ready = p.val(c.s_ve)


def _emit_router(c, li):
    p = c.p
    if li == 0:
        wr = c.din("l0_wr_t", (D, 8))
    else:
        wr = c.din("l1_gate_t", (D, 8))
        beta16 = c.din("l1_beta_r16", (1, 128))
    WRS, wrs_close = c.sb_scoped(f"wrs{li}", [128, DC * 8], BF16)
    c.WT8, c.wt8_close = c.sb_scoped(f"wt8_{li}", [16, 16 * 128], BF16)
    WT8 = c.WT8
    rtb = c.Burst(f"s_rt{li}")
    rtb.load("gpsimd", lambda e: e.dma_start(
        out=WRS.rearrange("pp (dc e) -> pp dc e", e=8),
        in_=wr.rearrange("(dc pp) e -> pp dc e", pp=128)))
    LG = c.SCR1[:, 0:128]
    EL = c.SCR1[:, 128:256]
    MSK = c.SCR1[:, 256:384]
    WB = c.SCR1[:, 384:512]
    TMPA = c.SCR1[:, 512:640]
    TMPB = c.SCR1[:, 640:768]
    BETB = c.SCR1[:, 768:896]
    M16 = c.SCR1[:, 896:912]
    S16 = c.SCR1[:, 912:928]
    RD16 = c.SCR1[:, 928:944]

    def v16(ap):
        return ap.rearrange("pp (a b) -> pp a b", b=8)

    def b16(ap):
        return ap.unsqueeze(-1).broadcast_to([128, 16, 8])

    beta_bc = 0
    if li == 1:
        rtbs = c.Burst(f"s_rts{li}")
        rtbs.load("sync", lambda e: e.dma_start(out=c.STATS[0:1, 0:128], in_=beta16))
        gA = c.bguard(c.PA)
        p.emit("tensor", lambda e: e.matmul(c.PA[:, 0:128], c.ONESR[0:1, :], c.STATS[0:1, 0:128],
                                            start=True, stop=True),
               inc=(c.s_pe, 1), waits=[rtbs.done(), gA, (c.s_ve, c.ones_ready)])
        bv = p.val(c.s_pe)
        p.emit("vector", lambda e: e.tensor_copy(BETB, c.PA[:, 0:128]),
               inc=(c.s_ve, 1), waits=[(c.s_pe, bv)])
        c.set_bguard(c.PA, c.s_ve, p.val(c.s_ve))
        beta_bc = p.val(c.s_ve)

    gA = c.bguard(c.PA)
    for t in range(16):
        for dc in range(DC):
            p.emit("tensor", lambda e, t=t, dc=dc, xn=c.XN[dc]: e.matmul(
                c.PA[:, t * 8:(t + 1) * 8], xn[:, t * 128:(t + 1) * 128],
                WRS[:, dc * 8:(dc + 1) * 8],
                start=(t == 0 and dc == 0), stop=(t == 15 and dc == DC - 1)),
                inc=(c.s_pe, 1) if (t == 15 and dc == DC - 1) else None,
                waits=[(c.s_ve, c.xn_ready), rtb.done(), gA] if (t == 0 and dc == 0) else ())
    lg_mm = p.val(c.s_pe)
    p.emit("vector", lambda e: e.tensor_copy(LG, c.PA[:, 0:128]),
           inc=(c.s_ve, 1), waits=[(c.s_pe, lg_mm)])
    c.set_bguard(c.PA, c.s_ve, p.val(c.s_ve))

    if li == 0:
        p.emit("vector", lambda e: e.reduce_max(M16, v16(LG), axis=AX.X), inc=(c.s_ve, 1))
        p.emit("vector", lambda e: e.tensor_tensor(out=v16(TMPA), in0=v16(LG), in1=b16(M16), op=ALU.subtract),
               inc=(c.s_ve, 1))
        vv = p.val(c.s_ve)
        p.emit("scalar", lambda e: e.activation(out=EL, in_=TMPA, func=AF.Exp),
               inc=(c.s_sc, 1), waits=[(c.s_ve, vv)])
        sc_el = p.val(c.s_sc)
        p.emit("vector", lambda e: e.tensor_tensor(out=v16(MSK), in0=v16(LG), in1=b16(M16), op=ALU.is_ge),
               inc=(c.s_ve, 1))
        p.emit("vector", lambda e: e.tensor_scalar(out=TMPA, in0=MSK, scalar1=-1.0, scalar2=1.0,
                                                   op0=ALU.mult, op1=ALU.add),
               inc=(c.s_ve, 1), waits=[(c.s_sc, sc_el)])
        p.emit("vector", lambda e: e.tensor_tensor(out=TMPA, in0=EL, in1=TMPA, op=ALU.mult),
               inc=(c.s_ve, 1))
        p.emit("vector", lambda e: e.reduce_max(S16, v16(TMPA), axis=AX.X), inc=(c.s_ve, 1))
        p.emit("vector", lambda e: e.scalar_tensor_tensor(out=TMPB, in0=MSK, scalar=-1e30, in1=LG,
                                                          op0=ALU.mult, op1=ALU.add), inc=(c.s_ve, 1))
        p.emit("vector", lambda e: e.reduce_max(M16, v16(TMPB), axis=AX.X), inc=(c.s_ve, 1))
        p.emit("vector", lambda e: e.tensor_tensor(out=v16(MSK), in0=v16(LG), in1=b16(M16), op=ALU.is_ge),
               inc=(c.s_ve, 1))
        p.emit("vector", lambda e: e.tensor_scalar(out=S16, in0=S16, scalar1=1.0, scalar2=None,
                                                   op0=ALU.add), inc=(c.s_ve, 1))
        p.emit("vector", lambda e: e.reciprocal(out=RD16, in_=S16), inc=(c.s_ve, 1))
        p.emit("vector", lambda e: e.tensor_tensor(out=v16(WB), in0=v16(EL), in1=b16(RD16), op=ALU.mult),
               inc=(c.s_ve, 1))
        p.emit("vector", lambda e: e.tensor_tensor(out=WB, in0=WB, in1=MSK, op=ALU.mult),
               inc=(c.s_ve, 1))
    else:
        p.emit("vector", lambda e: e.tensor_tensor(out=TMPA, in0=LG, in1=BETB, op=ALU.add),
               inc=(c.s_ve, 1), waits=[(c.s_ve, beta_bc)])
        p.emit("vector", lambda e: e.reduce_max(M16, v16(TMPA), axis=AX.X), inc=(c.s_ve, 1))
        p.emit("vector", lambda e: e.tensor_tensor(out=v16(MSK), in0=v16(TMPA), in1=b16(M16), op=ALU.is_ge),
               inc=(c.s_ve, 1))
        p.emit("vector", lambda e: e.scalar_tensor_tensor(out=TMPB, in0=MSK, scalar=-1e30, in1=TMPA,
                                                          op0=ALU.mult, op1=ALU.add), inc=(c.s_ve, 1))
        p.emit("vector", lambda e: e.reduce_max(M16, v16(TMPB), axis=AX.X), inc=(c.s_ve, 1))
        p.emit("vector", lambda e: e.tensor_tensor(out=v16(MSK), in0=v16(TMPA), in1=b16(M16), op=ALU.is_ge),
               inc=(c.s_ve, 1))
        vv = p.val(c.s_ve)
        p.emit("scalar", lambda e: e.activation(out=EL, in_=LG, func=AF.Sigmoid),
               inc=(c.s_sc, 1), waits=[(c.s_ve, vv)])
        sg = p.val(c.s_sc)
        p.emit("vector", lambda e: e.tensor_tensor(out=WB, in0=EL, in1=MSK, op=ALU.mult),
               inc=(c.s_ve, 1), waits=[(c.s_sc, sg)])
    w_done = p.val(c.s_ve)

    banks = [c.PA, c.PB]
    for t in range(16):
        bank = banks[t % 2]
        gB = c.bguard(bank)
        p.emit("tensor", lambda e, t=t, ba=bank: e.matmul(
            ba[0:8, 0:128], WB[:, t * 8:(t + 1) * 8], c.IDF[:, :], start=True, stop=True),
            inc=(c.s_pe, 1), waits=[(c.s_ve, w_done), gB, c.id_ready] if t < 2 else [gB])
        tr = p.val(c.s_pe)
        p.emit("vector", lambda e, t=t, ba=bank: e.tensor_copy(WT8[0:8, t * 128:(t + 1) * 128],
                                                               ba[0:8, 0:128]),
               inc=(c.s_ve, 1), waits=[(c.s_pe, tr)])
        c.set_bguard(bank, c.s_ve, p.val(c.s_ve))
    c.router_done = p.val(c.s_ve)
    c._router_closers = [wrs_close, c.wt8_close]


def _mk_wbc(c, e_, into):
    p = c.p
    WT8 = c.WT8
    sel_sl = c.SEL[:, e_ * 128:(e_ + 1) * 128]
    banks = [c.PA, c.PB]
    for g in range(4):
        bank = banks[g % 2]
        gA = c.bguard(bank)
        for s in range(4):
            t = g * 4 + s
            p.emit("tensor", lambda en, t=t, s=s, ba=bank: en.matmul(
                ba[:, s * 128:(s + 1) * 128], sel_sl[0:8, :], WT8[0:8, t * 128:(t + 1) * 128],
                start=(s == 0), stop=(s == 3)),
                inc=(c.s_pe, 1) if s == 3 else None,
                waits=[gA, (c.s_ve, c.router_done)] if s == 0 else ())
        v = p.val(c.s_pe)
        if g % 2 == 0:
            p.emit("vector", lambda en, g=g, ba=bank: en.tensor_copy(into[:, g * 512:(g + 1) * 512], ba[:, 0:512]),
                   inc=(c.s_ve, 1), waits=[(c.s_pe, v)])
            c.set_bguard(bank, c.s_ve, p.val(c.s_ve))
        else:
            p.emit("scalar", lambda en, g=g, ba=bank: en.activation(
                out=into[:, g * 512:(g + 1) * 512], in_=ba[:, 0:512], func=AF.Identity),
                inc=(c.s_sc, 1), waits=[(c.s_pe, v)])
            c.set_bguard(bank, c.s_sc, p.val(c.s_sc))
    p.emit("vector", lambda en: en.engine_nop(), inc=(c.s_ve, 1), waits=[(c.s_sc, p.val(c.s_sc))])
    return p.val(c.s_ve)


def _emit_ffn_l0(c):
    p = c.p
    XN = c.XN
    WT8 = c.WT8
    p.barrier(c.all_sems)
    E, M = 8, 4096
    MC = M // 128
    PIECE, MSL, NBUF = 8, 2, 3
    NP = MC // PIECE
    NSL = PIECE // MSL

    w1 = c.din("l0f_w1t", (E, D, M))
    w2t = c.din("l0f_w2t", (E, M, D))
    b1c = c.din("l0f_b1c", (128, E * MC))
    b2d = c.din("l0f_b2", (8, D))

    closers = []

    def sbsc(name, shape, dt=F32):
        h, cl = c.sb_scoped(name, shape, dt)
        closers.append(cl)
        return h

    W1B = [sbsc(f"l0w1b{i}", [128, DC * MSL * 128], BF16) for i in range(NBUF)]
    W2P = sbsc("l0w2p", [128, PIECE * D], BF16)
    HP = [sbsc(f"l0hp{i}", [128, T], BF16) for i in range(PIECE)]
    B1C = sbsc("l0b1c", [128, E * MC])
    B2S = sbsc("l0b2s", [8, D], BF16)
    WBC = [sbsc(f"l0wbc{i}", [128, T], BF16) for i in range(2)]

    fb0 = c.Burst("s_fb0")
    fb0g = c.Burst("s_fb0g")
    fb0.load("sync", lambda e: e.dma_start(out=B1C[:, :], in_=b1c))
    fb0g.load("gpsimd", lambda e: e.dma_start(out=B2S[:, :], in_=b2d))

    w1_guard = {}
    w1_cnt = [0]
    w1_ld_val = {}

    def load_w1(e_, mp, sl):
        i = w1_cnt[0]
        buf = W1B[i % NBUF]
        m0 = (mp * PIECE + sl * MSL) * 128
        src = w1[e_, :, m0:m0 + MSL * 128].rearrange("(dc pp) m -> pp dc m", pp=128)
        dst = buf.rearrange("pp (dc m) -> pp dc m", m=MSL * 128)
        g = w1_guard.get(i % NBUF)
        w1_ld_val[i] = w1s.load("gpsimd", i % NBUF, lambda en: en.dma_start(out=dst, in_=src),
                                waits=[g] if g else [])
        w1_cnt[0] += 1
        return i

    w2_guard = [None]
    w1s = c.Stream("s_w1", NBUF)
    w2s = c.Stream("s_w2", 1)

    def load_w2(e_, mp):
        src = w2t[e_, mp * PIECE * 128:(mp + 1) * PIECE * 128, :].rearrange("(mi pp) h -> pp mi h", pp=128)
        dst = DPW.rearrange("pp (mi h) -> pp mi h", h=D) if False else W2P.rearrange("pp (mi h) -> pp mi h", h=D)
        g = w2_guard[0]
        return w2s.load("gpsimd", 0, lambda en: en.dma_start(out=dst, in_=src),
                        waits=[g] if g else [])

    order = [(e_, mp) for e_ in range(E) for mp in range(NP)]
    if _os.environ.get("K_NPIECE"):
        order = order[:int(_os.environ["K_NPIECE"])]
    slab_ids = {}
    for k in range(NBUF):
        slab_ids[(0, 0, k)] = load_w1(0, 0, k)

    ph_guard = [0, 0]
    pd_guard = {0: 0, 1: 0, 2: 0, 3: 0}
    h_guard = 0
    wbc_done = {}

    for oi, (e_, mp) in enumerate(order):
        if mp == 0:
            if KNOB("K_NOWBC"):
                wbc_done[e_ % 2] = 0
            else:
                wbc_done[e_ % 2] = _mk_wbc(c, e_, WBC[e_ % 2])
        w2_ld = load_w2(e_, mp)
        for mc_i in range(PIECE):
            sl, mi = divmod(mc_i, MSL)
            sid = slab_ids[(e_, mp, sl)]
            mc = mp * PIECE + mc_i
            for tt in range(4):
                bank = (mc_i * 4 + tt) % 2
                waits = [(c.s_sc, ph_guard[bank])]
                if mi == 0 and tt == 0:
                    waits.append(w1_ld_val[sid])
                if oi == 0 and mc_i == 0 and tt == 0:
                    waits += [(c.s_ve, c.xn_ready), fb0.done()]
                for kc in range(DC):
                    lhsT = W1B[sid % NBUF][:, (kc * MSL + mi) * 128:(kc * MSL + mi + 1) * 128]
                    p.emit("tensor", lambda en, l=lhsT, k=kc, b=bank, t=tt:
                           en.matmul(c.PH[b][:, :], l, XN[k][:, t * 512:(t + 1) * 512],
                                     start=(k == 0), stop=(k == DC - 1)),
                           inc=(c.s_pe, 1) if kc == DC - 1 else None,
                           waits=waits if kc == 0 else ())
                pe_cnt = p.val(c.s_pe)
                bcol = B1C[:, e_ * MC + mc:e_ * MC + mc + 1]
                w = [(c.s_pe, pe_cnt)]
                if mc_i == 0 and tt == 0:
                    w.append((c.s_pe, h_guard))
                p.emit("scalar", lambda en, b=bank, m=mc_i, t=tt, bc=bcol:
                       en.activation(out=HP[m][:, t * 512:(t + 1) * 512], in_=c.PH[b][:, :],
                                     func=AF.Gelu, bias=bc, scale=1.0),
                       inc=(c.s_sc, 1), waits=w)
                sc_v = p.val(c.s_sc)
                ph_guard[bank] = sc_v
                if not KNOB("K_NOGATE"):
                    p.emit("vector", lambda en, m=mc_i, t=tt, b=e_ % 2:
                           en.tensor_tensor(out=HP[m][:, t * 512:(t + 1) * 512],
                                            in0=HP[m][:, t * 512:(t + 1) * 512],
                                            in1=WBC[b][:, t * 512:(t + 1) * 512], op=ALU.mult),
                           inc=(c.s_ve, 1),
                           waits=[(c.s_sc, sc_v)] + ([(c.s_ve, wbc_done[e_ % 2])] if (mc_i == 0 and tt == 0) else []))
                else:
                    p.emit("vector", lambda en: en.engine_nop(), inc=(c.s_ve, 1),
                           waits=[(c.s_sc, sc_v)])
            if mi == MSL - 1:
                w1_guard[sid % NBUF] = (c.s_pe, p.val(c.s_pe))
                gidx = oi * NSL + sl + NBUF
                if gidx < len(order) * NSL:
                    noi, nsl = divmod(gidx, NSL)
                    ne, nmp = order[noi]
                    slab_ids[(ne, nmp, nsl)] = load_w1(ne, nmp, nsl)
        h_done = p.val(c.s_ve)
        last_piece = (e_ == E - 1 and mp == NP - 1)
        for tt in range(4):
            for dc in range(DC):
                bank = (tt * DC + dc) % 4
                waits = [(c.s_ve, pd_guard[bank])]
                if tt == 0 and dc == 0:
                    waits += [(c.s_ve, h_done), w2_ld]
                with_b2 = last_piece and not KNOB("K_NOB2")
                for mc_i in range(PIECE):
                    stop = (mc_i == PIECE - 1) and not last_piece
                    last_inst = (mc_i == PIECE - 1) and not with_b2
                    p.emit("tensor", lambda en, l=W2P[:, mc_i * D + dc * 128:mc_i * D + (dc + 1) * 128],
                           m=mc_i, b=bank, t=tt, st=stop:
                           en.matmul(c.PD[b][:, :], l, HP[m][:, t * 512:(t + 1) * 512],
                                     start=(m == 0), stop=st),
                           inc=(c.s_pe, 1) if last_inst else None,
                           waits=waits if mc_i == 0 else ())
                if with_b2:
                    for st_i in range(4):
                        t_ = tt * 4 + st_i
                        p.emit("tensor", lambda en, t_=t_, s=st_i, b=bank, dc=dc:
                               en.matmul(c.PD[b][:, s * 128:(s + 1) * 128],
                                         B2S[0:8, dc * 128:(dc + 1) * 128],
                                         WT8[0:8, t_ * 128:(t_ + 1) * 128],
                                         start=False, stop=(s == 3)),
                               inc=(c.s_pe, 1) if st_i == 3 else None,
                               waits=[fb0g.done()] if (tt == 0 and dc == 0 and st_i == 0) else ())
                pe_cnt = p.val(c.s_pe)
                p.emit("vector", lambda en, d=dc, b=bank, t=tt:
                       en.tensor_tensor(out=c.X[d][:, t * 512:(t + 1) * 512],
                                        in0=c.X[d][:, t * 512:(t + 1) * 512],
                                        in1=c.PD[b][:, :], op=ALU.add),
                       inc=(c.s_ve, 1), waits=[(c.s_pe, pe_cnt)])
                pd_guard[bank] = p.val(c.s_ve)
        h_guard = p.val(c.s_pe)
        w2_guard[0] = (c.s_pe, h_guard)
    if KNOB("K_DUMP"):
        dbg = c.nc.dram_tensor("dbg_gate", [3, 128, T], F32, kind="ExternalOutput").ap()
        vv = p.val(c.s_ve)
        p.emit("vector", lambda e: e.tensor_copy(c.SCR0[:, :], WBC[0][:, :]), inc=(c.s_ve, 1),
               waits=[(c.s_ve, vv)])
        dv = p.val(c.s_ve)
        p.emit("sync", lambda e: e.dma_start(out=dbg[0], in_=c.SCR0[:, :]), inc=(c.s_dbg, 16),
               waits=[(c.s_ve, dv)])
        p.emit("vector", lambda e: e.tensor_copy(c.SCR0[:, :], WBC[1][:, :]), inc=(c.s_ve, 1),
               waits=[(c.s_dbg, p.val(c.s_dbg))])
        dv = p.val(c.s_ve)
        p.emit("sync", lambda e: e.dma_start(out=dbg[1], in_=c.SCR0[:, :]), inc=(c.s_dbg, 16),
               waits=[(c.s_ve, dv)])
        p.emit("vector", lambda e: e.memset(c.SCR0[:, :], 0.0), inc=(c.s_ve, 1),
               waits=[(c.s_dbg, p.val(c.s_dbg))])
        p.emit("vector", lambda e: e.tensor_copy(c.SCR0[0:16, 0:2048], WT8[:, :]), inc=(c.s_ve, 1))
        dv = p.val(c.s_ve)
        p.emit("sync", lambda e: e.dma_start(out=dbg[2], in_=c.SCR0[:, :]), inc=(c.s_dbg, 16),
               waits=[(c.s_ve, dv)])
        p.wait("sync", c.s_dbg, p.val(c.s_dbg))
    c.set_bguard(c.PH[0], c.s_sc, ph_guard[0])
    c.set_bguard(c.PH[1], c.s_sc, ph_guard[1])
    for b in range(4):
        c.set_bguard(c.PD[b], c.s_ve, pd_guard[b])
    p.barrier(c.all_sems)
    for cl in reversed(closers):
        cl()
    for cl in reversed(c._router_closers):
        cl()


def _emit_uphase(c):
    p = c.p
    XN = c.XN
    W = 3 * D
    WC = W // 128
    win = c.din("l0a_win", (D, W))
    binc = c.din("l0a_binc", (128, WC))
    swc = c.din("l0a_swc", (128, WC * 3))
    sbc = c.din("l0a_sbc", (128, WC))

    NBUF = 3
    closers = []

    def sbsc(name, shape, dt=F32):
        h, cl = c.sb_scoped(name, shape, dt)
        closers.append(cl)
        return h

    WINB = [sbsc(f"winb{i}", [128, DC * 256], BF16) for i in range(NBUF)]
    SMALL = sbsc("usmall", [128, WC * 5])
    U = [sbsc(f"ubuf{i}", [128, T + 8]) for i in range(2)]
    X1 = [sbsc(f"x1_{i}", [128, T], BF16) for i in range(8)]

    ubst = c.Burst("s_ub")
    ubst.load("sync", lambda e: e.dma_start(out=SMALL[:, 0:WC], in_=binc))
    ubst.load("sync", lambda e: e.dma_start(out=SMALL[:, WC:WC * 4], in_=swc))
    ubst.load("sync", lambda e: e.dma_start(out=SMALL[:, WC * 4:WC * 5], in_=sbc))
    for i in range(2):
        p.emit("vector", lambda e, i=i: e.memset(U[i][:, 0:8], 0.0), inc=(c.s_ve, 1))
    u_zero = p.val(c.s_ve)

    guard = {}
    cnt = [0]
    win_ld_val = {}
    wins = c.Stream("s_win", NBUF)
    sp0 = c.Stream("s_sp", 2)
    spv = c.Burst("s_spv")

    def load_win(sl2):
        i = cnt[0]
        buf = WINB[i % NBUF]
        src = win[:, sl2 * 256:(sl2 + 1) * 256].rearrange("(dc pp) m -> pp dc m", pp=128)
        dst = buf.rearrange("pp (dc m) -> pp dc m", m=256)
        g = guard.get(i % NBUF)
        win_ld_val[i] = wins.load("gpsimd", i % NBUF, lambda en: en.dma_start(out=dst, in_=src),
                                  waits=[g] if g else [])
        cnt[0] += 1
        return i

    sids = {}
    for k in range(NBUF):
        sids[k] = load_win(k)

    u_guard = [0, 0]
    spill = [None, None]
    TMP = c.SCR0
    for wc in range(WC):
        sl, half = divmod(wc, 2)
        sid = sids[sl]
        ub = U[wc % 2]
        for tt in range(4):
            bank = tt % 2
            waits = [c.bguard(c.PH[bank])]
            if tt == 0:
                waits += [win_ld_val[sid], (c.s_ve, u_guard[wc % 2])]
                if wc == 0:
                    waits += [(c.s_ve, c.xn_ready), ubst.done(), (c.s_ve, u_zero)]
            for kc in range(DC):
                lhsT = WINB[sid % NBUF][:, (kc * 2 + half) * 128:(kc * 2 + half + 1) * 128]
                p.emit("tensor", lambda en, l=lhsT, k=kc, b=bank, t=tt:
                       en.matmul(c.PH[b][:, :], l, XN[k][:, t * 512:(t + 1) * 512],
                                 start=(k == 0), stop=(k == DC - 1)),
                       inc=(c.s_pe, 1) if kc == DC - 1 else None,
                       waits=waits if kc == 0 else ())
            pe_cnt = p.val(c.s_pe)
            p.emit("scalar", lambda en, b=bank, t=tt, wc=wc, ub=ub:
                   en.activation(out=ub[:, 8 + t * 512:8 + (t + 1) * 512], in_=c.PH[b][:, :],
                                 func=AF.Identity, bias=SMALL[:, wc:wc + 1], scale=1.0),
                   inc=(c.s_sc, 1), waits=[(c.s_pe, pe_cnt)])
            c.set_bguard(c.PH[bank], c.s_sc, p.val(c.s_sc))
        ev_done = p.val(c.s_sc)
        if half == 1:
            guard[sid % NBUF] = (c.s_pe, p.val(c.s_pe))
            if sl + NBUF < 12:
                sids[sl + NBUF] = load_win(sl + NBUF)
        sw0 = SMALL[:, WC + wc * 3:WC + wc * 3 + 1]
        sw1 = SMALL[:, WC + wc * 3 + 1:WC + wc * 3 + 2]
        sw2 = SMALL[:, WC + wc * 3 + 2:WC + wc * 3 + 3]
        sbb = SMALL[:, WC * 4 + wc:WC * 4 + wc + 1]
        p.emit("vector", lambda e, ub=ub, sw1=sw1, sbb=sbb: e.tensor_scalar(
            out=TMP[:, 0:T], in0=ub[:, 7:7 + T], scalar1=sw1, scalar2=sbb, op0=ALU.mult, op1=ALU.add),
            inc=(c.s_ve, 1), waits=[(c.s_sc, ev_done)])
        p.emit("vector", lambda e, ub=ub, sw0=sw0: e.scalar_tensor_tensor(
            out=TMP[:, 0:T], in0=ub[:, 6:6 + T], scalar=sw0, in1=TMP[:, 0:T], op0=ALU.mult, op1=ALU.add),
            inc=(c.s_ve, 1))
        if wc < 8:
            tgt = c.SCB[wc % 2]
            p.emit("vector", lambda e, ub=ub, sw2=sw2, tgt=tgt: e.scalar_tensor_tensor(
                out=tgt[:, :], in0=ub[:, 8:8 + T], scalar=sw2, in1=TMP[:, 0:T], op0=ALU.mult, op1=ALU.add),
                inc=(c.s_ve, 1), waits=[spill[wc % 2]] if spill[wc % 2] else ())
            vv = p.val(c.s_ve)
            spill[wc % 2] = sp0.load("sync", wc % 2, lambda e, wc=wc, tgt=tgt: e.dma_start(
                out=c.x0_spill[wc * 128:(wc + 1) * 128, :], in_=tgt[:, :]),
                waits=[(c.s_ve, vv)])
        elif wc < 16:
            p.emit("vector", lambda e, ub=ub, sw2=sw2, x1=X1[wc - 8]: e.scalar_tensor_tensor(
                out=x1[:, :], in0=ub[:, 8:8 + T], scalar=sw2, in1=TMP[:, 0:T], op0=ALU.mult, op1=ALU.add),
                inc=(c.s_ve, 1))
        else:
            j = wc - 16
            p.emit("vector", lambda e, ub=ub, sw2=sw2: e.scalar_tensor_tensor(
                out=TMP[:, 0:T], in0=ub[:, 8:8 + T], scalar=sw2, in1=TMP[:, 0:T], op0=ALU.mult, op1=ALU.add),
                inc=(c.s_ve, 1))
            p.emit("vector", lambda e, x1=X1[j]: e.tensor_tensor(
                out=x1[:, :], in0=TMP[:, 0:T], in1=x1[:, :], op=ALU.mult),
                inc=(c.s_ve, 1))
            vv = p.val(c.s_ve)
            spv.load("sync", lambda e, j=j, x1=X1[j]: e.dma_start(
                out=c.v2_spill[j * 128:(j + 1) * 128, :], in_=x1[:, :]),
                waits=[(c.s_ve, vv)])
        u_guard[wc % 2] = p.val(c.s_ve)
    c.uphase_done = p.val(c.s_ve)
    c.spills_done = [(sp0.sems[0], 16 * sp0.cnt[0]),
                     (sp0.sems[1], 16 * sp0.cnt[1]), spv.done()]
    p.barrier(c.all_sems)
    for _e in ENGINES:
        p.pending[_e] = p.pending[_e] + list(c.spills_done)
    for cl in reversed(closers):
        cl()


class ConvBundle:
    pass


def _alloc_conv_bundle(c, with_l0_extras):
    b = ConvBundle()
    b.closers = []
    if not hasattr(c, "fws"):
        c.fws = c.Stream("s_fw", 4)
        c.ivs = c.Stream("s_iv", 4)

    def sbsc(name, shape, dt=BF16):
        h, cl = c.sb_scoped(name, shape, dt)
        b.closers.append(cl)
        return h

    b.KF = sbsc("cv_kf", [128, NKC * 512])
    b.FWDB = sbsc("cv_fwdb", [128, 4 * 2048])
    b.INVB = sbsc("cv_invb", [128, 4 * 2048])
    b.SK = sbsc("cv_sk", [128, 16 * 512])
    b.KTMP = c.SCB[1][:, 0:1024]
    if with_l0_extras:
        b.X0Q = sbsc("cv_x0q", [128, 2 * T])
        b.V2Q = sbsc("cv_v2q", [128, 2 * T])
        b.WOUTB = sbsc("cv_woutb", [128, 2 * D])
    b.fwd_guard = {}
    b.inv_guard = {}
    return b


def _conv_kf(b, kc, ri):
    return b.KF[:, kc * 512 + ri * 256:kc * 512 + (ri + 1) * 256]


def _conv_quarter(c, b, sigtm_ready, kerq_ready, out_cb):
    p = c.p

    def load_fwd(kc, which):
        slot = (kc % 2) * 2 + which
        off = slot * 2048
        buf = b.FWDB[:, off:off + 2048]
        g = b.fwd_guard.get(slot)
        v = c.fws.load("sync", slot,
            lambda e, kc=kc, w=which, bu=buf: e.dma_start(
                out=bu, in_=c.fwd_t[w, kc].rearrange("pp nt j -> pp (nt j)")),
            waits=[g] if g else [])
        return v, buf

    fwd_banks = [c.PH[0], c.PH[1], c.PD[0], c.PD[1]]
    for kc in range(NKC):
        lc, fbc = load_fwd(kc, 0)
        ls, fbs = load_fwd(kc, 1)
        sig_banks = [fwd_banks[(kc % 2) * 2 + 0], fwd_banks[(kc % 2) * 2 + 1]]
        for ri in range(2):
            bank = sig_banks[ri]
            g = c.bguard(bank)
            fb = fbc if ri == 0 else fbs
            for nt in range(16):
                p.emit("tensor", lambda e, nt=nt, ba=bank, fb=fb: e.matmul(
                    ba[:, 0:512], fb[:, nt * 128:(nt + 1) * 128],
                    b.SK[:, nt * 512:(nt + 1) * 512],
                    start=(nt == 0), stop=(nt == 15)),
                    inc=(c.s_pe, 1) if nt == 15 else None,
                    waits=[g, lc, ls, (c.s_ve, sigtm_ready), (c.s_ve, kerq_ready)] if nt == 0 else ())
        mm = p.val(c.s_pe)
        ko = (kc % 2) * 512
        kr = b.KTMP[:, ko:ko + 256]
        ki = b.KTMP[:, ko + 256:ko + 512]
        sb0, sb1 = sig_banks
        kg = b.fwd_guard.get(8 + kc % 2)
        p.emit("scalar", lambda e, ba=sb0, kr=kr: e.activation(out=kr, in_=ba[:, 256:512], func=AF.Identity),
               inc=(c.s_sc, 1), waits=[(c.s_pe, mm)] + ([kg] if kg else []))
        p.emit("scalar", lambda e, ba=sb1, ki=ki: e.activation(out=ki, in_=ba[:, 256:512], func=AF.Identity),
               inc=(c.s_sc, 1))
        kev = p.val(c.s_sc)
        t1 = c.SCR0[:, 0:256]; t2 = c.SCR0[:, 256:512]
        t3 = c.SCR0[:, 512:768]; t4 = c.SCR0[:, 768:1024]
        pr = _conv_kf(b, kc, 0); pi = _conv_kf(b, kc, 1)
        w0 = [(c.s_pe, mm), (c.s_sc, kev)]
        p.emit("vector", lambda e, kr=kr, ba=sb0: e.tensor_tensor(out=t1, in0=ba[:, 0:256], in1=kr, op=ALU.mult),
               inc=(c.s_ve, 1), waits=w0)
        p.emit("vector", lambda e, ki=ki, ba=sb1: e.tensor_tensor(out=t2, in0=ba[:, 0:256], in1=ki, op=ALU.mult),
               inc=(c.s_ve, 1))
        p.emit("vector", lambda e, ki=ki, ba=sb0: e.tensor_tensor(out=t3, in0=ba[:, 0:256], in1=ki, op=ALU.mult),
               inc=(c.s_ve, 1))
        p.emit("vector", lambda e, kr=kr, ba=sb1: e.tensor_tensor(out=t4, in0=ba[:, 0:256], in1=kr, op=ALU.mult),
               inc=(c.s_ve, 1))
        p.emit("vector", lambda e, pr=pr: e.tensor_tensor(out=pr, in0=t1, in1=t2, op=ALU.subtract),
               inc=(c.s_ve, 1))
        p.emit("vector", lambda e, pi=pi: e.tensor_tensor(out=pi, in0=t3, in1=t4, op=ALU.add),
               inc=(c.s_ve, 1))
        pw = p.val(c.s_ve)
        c.set_bguard(sb0, c.s_ve, pw)
        c.set_bguard(sb1, c.s_ve, pw)
        b.fwd_guard[(kc % 2) * 2 + 0] = (c.s_pe, p.val(c.s_pe))
        b.fwd_guard[(kc % 2) * 2 + 1] = (c.s_pe, p.val(c.s_pe))
        b.fwd_guard[8 + kc % 2] = (c.s_ve, pw)
    p_done = p.val(c.s_ve)

    inv_banks = [c.PH[0], c.PH[1], c.PD[0], c.PD[1], c.PD[2], c.PD[3], c.PA, c.PB]
    for kc in range(NKC):
        inv_lds = []
        for ri in range(2):
            slot = (kc % 2) * 2 + ri
            off = slot * 2048
            g = b.inv_guard.get(slot)
            inv_lds.append(c.ivs.load("sync" if ri == 0 else "gpsimd", slot,
                lambda e, kc=kc, ri=ri, off=off: e.dma_start(
                    out=b.INVB[:, off:off + 2048], in_=c.inv_t[ri, kc]),
                waits=[g] if g else []))
        for ccl in range(2):
            for tt in range(4):
                bank = inv_banks[ccl * 4 + tt]
                for ri in range(2):
                    w = []
                    if kc == 0 and ri == 0:
                        w = [c.bguard(bank), (c.s_ve, p_done)]
                    if ccl == 0 and tt == 0 and ri == 0:
                        w.extend(inv_lds)
                    off = ((kc % 2) * 2 + ri) * 2048
                    psl = _conv_kf(b, kc, ri)[:, ccl * 128:(ccl + 1) * 128]
                    p.emit("tensor", lambda e, ps=psl, off=off, tt=tt, ba=bank, kc=kc, ri=ri:
                           e.matmul(ba[:, 0:512], ps,
                                    b.INVB[:, off + tt * 512:off + (tt + 1) * 512],
                                    start=(kc == 0 and ri == 0), stop=(kc == NKC - 1 and ri == 1)),
                           inc=(c.s_pe, 1) if (ccl == 1 and tt == 3 and ri == 1) else None,
                           waits=w)
        b.inv_guard[(kc % 2) * 2 + 0] = (c.s_pe, p.val(c.s_pe))
        b.inv_guard[(kc % 2) * 2 + 1] = (c.s_pe, p.val(c.s_pe))
    inv_mm = p.val(c.s_pe)
    for ccl in range(2):
        for tt in range(4):
            out_cb(ccl, tt, inv_banks[ccl * 4 + tt], [(c.s_pe, inv_mm)])


def _transpose_to(c, src_ap_fn, dst_ap_fn, n_tiles, ready, banks=None):
    p = c.p
    banks = banks or [c.PD[0], c.PD[1], c.PD[2], c.PD[3]]
    for i in range(n_tiles):
        bank = banks[i % len(banks)]
        g = c.bguard(bank)
        p.emit("tensor", lambda e, i=i, ba=bank: e.matmul(
            ba[:, 0:128], src_ap_fn(i), c.ID[:, :], start=True, stop=True),
            inc=(c.s_pe, 1),
            waits=[g, (c.s_ve, ready), c.id_ready] if i < len(banks) else [g])
        mm = p.val(c.s_pe)
        if i % 2 == 0:
            p.emit("vector", lambda e, i=i, ba=bank: e.tensor_copy(dst_ap_fn(i), ba[:, 0:128]),
                   inc=(c.s_ve, 1), waits=[(c.s_pe, mm)])
            c.set_bguard(bank, c.s_ve, p.val(c.s_ve))
        else:
            p.emit("scalar", lambda e, i=i, ba=bank: e.activation(
                out=dst_ap_fn(i), in_=ba[:, 0:128], func=AF.Identity),
                inc=(c.s_sc, 1), waits=[(c.s_pe, mm)])
            c.set_bguard(bank, c.s_sc, p.val(c.s_sc))
    p.emit("vector", lambda e: e.engine_nop(), inc=(c.s_ve, 1), waits=[(c.s_sc, p.val(c.s_sc))])
    return p.val(c.s_ve)


def _emit_filter_common(c):
    p = c.p
    f = Ctx()
    c._filt = f
    f.closers = []

    def sbsc(name, shape, dt=F32):
        h, cl = c.sb_scoped(name, shape, dt)
        f.closers.append(cl)
        return h

    pe_rev = c.din("l0a_pe_rev", (3, T))
    w1 = c.din("l0a_w1", (3, 64))
    w2 = c.din("l0a_w2", (64, 64))
    w3 = c.din("l0a_w3", (64, D))
    cols = c.din("l0a_fcols", (64, 4))
    trev = c.din("l0a_trev", (1, T))
    negd = c.din("l0a_negd", (1, D))
    b3r = c.din("l0a_b3r", (1, D))
    f.W3S = sbsc("f_w3", [64, D])
    f.COLS = sbsc("f_cols", [64, 8])
    f.TREV = sbsc("f_trev", [1, T])
    f.NEGD = sbsc("f_negd", [1, D])
    f.H2 = sbsc("f_h2", [64, T])
    f.B3BC = sbsc("f_b3bc", [128, D])
    f.FTA = [sbsc(f"f_fta{i}", [128, 256], BF16) for i in range(2)]
    f.FTB = [sbsc(f"f_ftb{i}", [128, 256], BF16) for i in range(2)]
    tcl = []
    def sbt(name, shape, dt=F32):
        h, cl = c.sb_scoped(name, shape, dt)
        tcl.append(cl)
        return h
    f.W1S = sbt("f_w1", [3, 64])
    f.W2S = sbt("f_w2", [64, 64])
    f.B3R = sbt("f_b3r", [1, D])
    f.PEV = sbt("f_pev", [3, T])
    f.H1 = sbt("f_h1", [64, T])
    flc = c.Burst("s_flc")
    for dst, src in [(f.PEV, pe_rev), (f.W1S, w1), (f.W2S, w2), (f.W3S, w3),
                     (f.TREV, trev), (f.NEGD, negd), (f.B3R, b3r)]:
        flc.load("sync", lambda e, d=dst, s=src: e.dma_start(out=d[:, :], in_=s))
    flc.load("sync", lambda e: e.dma_start(out=f.COLS[:, 0:4], in_=cols))
    ld = flc.done()
    p.emit("vector", lambda e: e.tensor_tensor(out=f.COLS[:, 4:5], in0=f.COLS[:, 0:1],
                                               in1=f.COLS[:, 1:2], op=ALU.mult),
           inc=(c.s_ve, 1), waits=[ld])
    p.emit("vector", lambda e: e.tensor_tensor(out=f.COLS[:, 5:6], in0=f.COLS[:, 2:3],
                                               in1=f.COLS[:, 3:4], op=ALU.mult), inc=(c.s_ve, 1))
    cols_done = p.val(c.s_ve)

    def sin_mlp(dst, rhs, lhsT, fcol, bfcol, extra):
        for tt in range(4):
            g = c.bguard(c.PA)
            p.emit("tensor", lambda e, t=tt, l=lhsT, r=rhs: e.matmul(
                c.PA[0:64, 0:512], l, r[:, t * 512:(t + 1) * 512], start=True, stop=True),
                inc=(c.s_pe, 1), waits=[g] + (extra if tt == 0 else []))
            mm = p.val(c.s_pe)
            sl = slice(tt * 512, (tt + 1) * 512)
            TMP = c.SCR1[0:64, 0:512]
            p.emit("vector", lambda e, sl=sl, fcol=fcol, bfcol=bfcol: e.tensor_scalar(
                out=dst[:, sl], in0=c.PA[0:64, 0:512], scalar1=fcol, scalar2=bfcol,
                op0=ALU.mult, op1=ALU.add), inc=(c.s_ve, 1), waits=[(c.s_pe, mm)])
            c.set_bguard(c.PA, c.s_ve, p.val(c.s_ve))
            p.emit("vector", lambda e, sl=sl: e.tensor_scalar(
                out=TMP, in0=dst[:, sl], scalar1=1.0 / (2 * math.pi), scalar2=None, op0=ALU.mult),
                inc=(c.s_ve, 1))
            p.emit("vector", lambda e: e.tensor_scalar(
                out=TMP, in0=TMP, scalar1=MAGIC, scalar2=MAGIC, op0=ALU.add, op1=ALU.subtract),
                inc=(c.s_ve, 1))
            p.emit("vector", lambda e, sl=sl: e.scalar_tensor_tensor(
                out=dst[:, sl], in0=TMP, scalar=-2 * math.pi, in1=dst[:, sl],
                op0=ALU.mult, op1=ALU.add), inc=(c.s_ve, 1))
            p.emit("vector", lambda e, sl=sl: e.tensor_scalar(
                out=dst[:, sl], in0=dst[:, sl], scalar1=math.pi, scalar2=-math.pi,
                op0=ALU.min, op1=ALU.max), inc=(c.s_ve, 1))
            rr = p.val(c.s_ve)
            p.emit("scalar", lambda e, sl=sl: e.activation(out=dst[:, sl], in_=dst[:, sl], func=AF.Sin),
                   inc=(c.s_sc, 1), waits=[(c.s_ve, rr)])

    sin_mlp(f.H1, f.PEV, f.W1S[0:3, :], f.COLS[:, 1:2], f.COLS[:, 4:5],
            [ld, (c.s_ve, cols_done)])
    h1_done = p.val(c.s_sc)
    sin_mlp(f.H2, f.H1, f.W2S[0:64, :], f.COLS[:, 3:4], f.COLS[:, 5:6], [(c.s_sc, h1_done)])
    f.h2_done = p.val(c.s_sc)
    for hh in range(2):
        g = c.bguard(c.PB)
        p.emit("tensor", lambda e, h=hh: e.matmul(
            c.PB[:, 0:512], c.ONESR[0:1, :], f.B3R[0:1, h * 512:(h + 1) * 512],
            start=True, stop=True), inc=(c.s_pe, 1), waits=[g, ld, (c.s_ve, c.ones_ready)])
        mm = p.val(c.s_pe)
        p.emit("vector", lambda e, h=hh: e.tensor_copy(f.B3BC[:, h * 512:(h + 1) * 512], c.PB[:, 0:512]),
               inc=(c.s_ve, 1), waits=[(c.s_pe, mm)])
        c.set_bguard(c.PB, c.s_ve, p.val(c.s_ve))
    f.b3_done = p.val(c.s_ve)
    p.barrier(c.all_sems)
    for cl in reversed(tcl):
        cl()


def _emit_filter_q(c, q, b):
    p = c.p
    f = c._filt
    sk_free = (c.s_pe, p.val(c.s_pe))
    mmbanks = [c.PA, c.PB]
    dbanks = [c.PD[2], c.PD[3]]
    prev_ta = [0, 0]
    prev_tb = [0, 0]
    prev_ker = [0, 0]
    for lt in range(16):
        ring = lt % 2
        mb = mmbanks[ring]
        db = dbanks[ring]
        gA = c.bguard(mb)
        gB = c.bguard(db)
        d0 = q * 256
        p.emit("tensor", lambda e, lt=lt, d0=d0, mb=mb: e.matmul(
            mb[:, 0:256], f.H2[:, lt * 128:(lt + 1) * 128], f.W3S[0:64, d0:d0 + 256],
            start=True, stop=True),
            inc=(c.s_pe, 1),
            waits=[gA] + ([(c.s_sc, f.h2_done)] if (q == 0 and lt == 0) else []))
        p.emit("tensor", lambda e, lt=lt, d0=d0, db=db: e.matmul(
            db[:, 0:256], f.TREV[0:1, lt * 128:(lt + 1) * 128], f.NEGD[0:1, d0:d0 + 256],
            start=True, stop=True), inc=(c.s_pe, 1), waits=[gB])
        mm = p.val(c.s_pe)
        TA = f.FTA[ring]
        TB = f.FTB[ring]
        p.emit("scalar", lambda e, db=db, TA=TA: e.activation(out=TA[:, :], in_=db[:, 0:256], func=AF.Exp),
               inc=(c.s_sc, 1), waits=[(c.s_pe, mm), (c.s_gp, prev_ker[ring])])
        sc_exp = p.val(c.s_sc)
        c.set_bguard(db, c.s_sc, sc_exp)
        p.emit("vector", lambda e, mb=mb, TB=TB, d0=d0: e.tensor_tensor(
            out=TB[:, :], in0=mb[:, 0:256], in1=f.B3BC[:, d0:d0 + 256], op=ALU.add),
            inc=(c.s_ve, 1),
            waits=[(c.s_pe, mm), (c.s_gp, prev_ker[ring])] + ([(c.s_ve, f.b3_done)] if (q == 0 and lt == 0) else []))
        tb_v = p.val(c.s_ve)
        c.set_bguard(mb, c.s_ve, tb_v)
        prev_ker[ring] = p.emit("gpsimd", lambda e, lt=lt, TA=TA, TB=TB: e.tensor_tensor(
            out=b.SK[:, lt * 512 + 256:(lt + 1) * 512], in0=TB[:, :], in1=TA[:, :], op=ALU.mult),
            inc=(c.s_gp, 1),
            waits=[(c.s_sc, sc_exp), (c.s_ve, tb_v)] + ([sk_free] if lt == 0 else []))
    p.emit("vector", lambda e: e.engine_nop(), inc=(c.s_ve, 1), waits=[(c.s_gp, p.val(c.s_gp))])
    return p.val(c.s_ve)


def _emit_conv_l0(c):
    p = c.p
    fbias = c.din("l0a_fbias", (128, 8))
    bout = c.din("l0a_bout", (128, 8))
    wout = c.din("l0a_wout", (D, D))
    _emit_filter_common(c)
    b = _alloc_conv_bundle(c, True)
    FB, fb_close = c.sb_scoped("l0fb", [128, 16])
    cfb = c.Burst("s_cfb")
    cfb.load("sync", lambda e: e.dma_start(out=FB[:, 0:8], in_=fbias))
    cfb.load("sync", lambda e: e.dma_start(out=FB[:, 8:16], in_=bout))
    fb_ld = cfb.done()
    xv = c.Burst("s_xv")
    x0b = c.Burst("s_x0b")

    wout_guard = [None]
    tr_guard = [None]
    yq_wout_guard = [0]
    wos = c.Stream("s_wo", 1)
    pending_wout = [None]

    def emit_wout(q, yq_tt, wout_ld):
        tmp2_rd = [0]
        for tt in range(4):
            for do in range(DC):
                bank = c.PD[(tt * DC + do) % 4]
                gb = c.bguard(bank)
                for ccl in range(2):
                    w = [gb] if ccl == 0 else []
                    if ccl == 0 and do == 0:
                        w += [(c.s_ve, max(yq_tt.get((0, tt), 0), yq_tt.get((1, tt), 0)))]
                    if ccl == 0 and tt == 0 and do == 0:
                        w += [wout_ld]
                    p.emit("tensor", lambda e, ccl=ccl, do=do, tt=tt, ba=bank: e.matmul(
                        ba[:, 0:512], b.WOUTB[:, ccl * D + do * 128:ccl * D + (do + 1) * 128],
                        b.X0Q[:, ccl * T + tt * 512:ccl * T + (tt + 1) * 512],
                        start=(ccl == 0), stop=(ccl == 1)),
                        inc=(c.s_pe, 1) if ccl == 1 else None, waits=w)
                mm = p.val(c.s_pe)
                if q == 3:
                    TMP2 = c.SCR1[:, 512:1024]
                    p.emit("scalar", lambda e, do=do, ba=bank: e.activation(
                        out=TMP2, in_=ba[:, 0:512], func=AF.Identity,
                        bias=FB[:, 8 + do:9 + do], scale=1.0),
                        inc=(c.s_sc, 1), waits=[(c.s_pe, mm), (c.s_ve, tmp2_rd[0])])
                    sv = p.val(c.s_sc)
                    p.emit("vector", lambda e, do=do, tt=tt: e.tensor_tensor(
                        out=c.X[do][:, tt * 512:(tt + 1) * 512],
                        in0=c.X[do][:, tt * 512:(tt + 1) * 512],
                        in1=TMP2, op=ALU.add),
                        inc=(c.s_ve, 1), waits=[(c.s_sc, sv)])
                    tmp2_rd[0] = p.val(c.s_ve)
                    c.set_bguard(bank, c.s_sc, sv)
                else:
                    p.emit("vector", lambda e, do=do, tt=tt, ba=bank: e.tensor_tensor(
                        out=c.X[do][:, tt * 512:(tt + 1) * 512],
                        in0=c.X[do][:, tt * 512:(tt + 1) * 512],
                        in1=ba[:, 0:512], op=ALU.add),
                        inc=(c.s_ve, 1), waits=[(c.s_pe, mm)])
                    c.set_bguard(bank, c.s_ve, p.val(c.s_ve))
        wout_guard[0] = (c.s_pe, p.val(c.s_pe))
        yq_wout_guard[0] = p.val(c.s_pe)

    for q in range(4):
        ker_ready = _emit_filter_q(c, q, b)
        g = tr_guard[0]
        for ccl in range(2):
            cc = 2 * q + ccl
            xv.load("sync", lambda e, cc=cc, ccl=ccl: e.dma_start(
                out=b.V2Q[:, ccl * T:(ccl + 1) * T], in_=c.v2_spill[cc * 128:(cc + 1) * 128, :]),
                waits=(list(c.spills_done) if (q == 0 and ccl == 0) else []) + ([g] if (g and ccl == 0) else []))
        sig_ld = xv.done()

        def src_fn(i):
            nt, ccl = divmod(i, 2)
            return b.V2Q[:, ccl * T + nt * 128:ccl * T + (nt + 1) * 128]

        def dst_fn(i):
            nt, ccl = divmod(i, 2)
            return b.SK[:, nt * 512 + ccl * 128:nt * 512 + (ccl + 1) * 128]

        p.emit("vector", lambda e: e.engine_nop(), inc=(c.s_ve, 1), waits=[sig_ld])
        sig_rdy = p.val(c.s_ve)
        sig_tm = _transpose_to(c, src_fn, dst_fn, 32, sig_rdy)
        tr_guard[0] = (c.s_pe, p.val(c.s_pe))

        if pending_wout[0] is not None:
            emit_wout(*pending_wout[0])
        g2 = yq_wout_guard[0]
        for ccl in range(2):
            cc = 2 * q + ccl
            x0b.load("sync", lambda e, cc=cc, ccl=ccl: e.dma_start(
                out=b.X0Q[:, ccl * T:(ccl + 1) * T], in_=c.x0_spill[cc * 128:(cc + 1) * 128, :]),
                waits=([(c.s_pe, g2)] if (ccl == 0 and g2) else []) + (list(c.spills_done) if (q == 0 and ccl == 0) else []))
        x0_ld = x0b.done()
        wout_ld = wos.load("gpsimd", 0, lambda e, q=q: e.dma_start(
            out=b.WOUTB.rearrange("pp (ccl do) -> pp ccl do", do=D),
            in_=wout[q * 256:(q + 1) * 256, :].rearrange("(ccl pp) do -> pp ccl do", pp=128)),
            waits=[wout_guard[0]] if wout_guard[0] else [])

        yq_tt = {}
        tmp_rd = [0]

        def out_cb(ccl, tt, bank, waits, q=q, x0_ld=x0_ld, yq_tt=yq_tt, tmp_rd=tmp_rd):
            TMP = c.SCR1[:, 0:512]
            p.emit("scalar", lambda e, ba=bank, q=q, ccl=ccl: e.activation(
                out=TMP, in_=ba[:, 0:512], func=AF.Identity,
                bias=FB[:, 2 * q + ccl:2 * q + ccl + 1], scale=1.0),
                inc=(c.s_sc, 1), waits=waits + [fb_ld, (c.s_ve, tmp_rd[0])])
            sv = p.val(c.s_sc)
            p.emit("vector", lambda e, ccl=ccl, tt=tt: e.tensor_tensor(
                out=b.X0Q[:, ccl * T + tt * 512:ccl * T + (tt + 1) * 512], in0=TMP,
                in1=b.X0Q[:, ccl * T + tt * 512:ccl * T + (tt + 1) * 512], op=ALU.mult),
                inc=(c.s_ve, 1), waits=[(c.s_sc, sv), x0_ld, (c.s_pe, yq_wout_guard[0])])
            c.set_bguard(bank, c.s_ve, p.val(c.s_ve))
            yq_tt[(ccl, tt)] = p.val(c.s_ve)
            tmp_rd[0] = p.val(c.s_ve)

        _conv_quarter(c, b, sig_tm, ker_ready, out_cb)
        pending_wout[0] = (q, yq_tt, wout_ld)
    emit_wout(*pending_wout[0])
    p.barrier(c.all_sems)
    fb_close()
    for cl in reversed(b.closers):
        cl()
    for cl in reversed(c._filt.closers):
        cl()


def _emit_omega_residual(c):
    p = c.p
    om = c.din("l1a_omega", (128, 8))
    OM, om_close = c.sb_scoped("l1om", [128, 8])
    omb = c.Burst("s_om")
    omb.load("sync", lambda e: e.dma_start(out=OM[:, :], in_=om))
    ld = omb.done()
    TMP = c.SCR1[:, 0:512]
    prev = 0
    for dc in range(DC):
        for tt in range(4):
            w = [(c.s_ve, prev)]
            if dc == 0 and tt == 0:
                w += [ld, (c.s_ve, c.xn_ready)]
            p.emit("scalar", lambda e, dc=dc, tt=tt, xn=c.XN[dc]: e.activation(
                out=TMP, in_=xn[:, tt * 512:(tt + 1) * 512],
                func=AF.Identity, bias=c.ONES[:, 3:4], scale=OM[:, dc:dc + 1]),
                inc=(c.s_sc, 1), waits=w)
            sv = p.val(c.s_sc)
            p.emit("vector", lambda e, dc=dc, tt=tt: e.tensor_tensor(
                out=c.X[dc][:, tt * 512:(tt + 1) * 512],
                in0=c.X[dc][:, tt * 512:(tt + 1) * 512],
                in1=TMP, op=ALU.add),
                inc=(c.s_ve, 1), waits=[(c.s_sc, sv)])
            prev = p.val(c.s_ve)
    c.omega_done = prev
    p.barrier(c.all_sems)
    for _e in ENGINES:
        p.pending[_e] = p.pending[_e] + [ld]
    om_close()


def _emit_ffn_l1(c):
    p = c.p
    XN = c.XN
    p.barrier(c.all_sems)
    E, M = 9, 2048
    PIECE, MSL, NBUF = 8, 2, 2
    NP = (M // 128) // PIECE
    NSL = PIECE // MSL

    gw = c.din("l1f_gate9", (E, D, M))
    uw = c.din("l1f_up9", (E, D, M))
    dw = c.din("l1f_down9", (E, M, D))

    closers = []

    def sbsc(name, shape, dt=F32):
        h, cl = c.sb_scoped(name, shape, dt)
        closers.append(cl)
        return h

    GB = [sbsc(f"l1gb{i}", [128, DC * MSL * 128], BF16) for i in range(NBUF)]
    UB = [sbsc(f"l1ub{i}", [128, DC * MSL * 128], BF16) for i in range(NBUF)]
    DP = sbsc("l1dp", [128, PIECE * D], BF16)
    HP = [sbsc(f"l1hp{i}", [128, T], BF16) for i in range(PIECE)]
    WBC = [sbsc(f"l1wbc{i}", [128, T], BF16) for i in range(2)]
    SG = [sbsc(f"l1sg{i}", [128, 512], BF16) for i in range(2)]

    guards = {"g": {}, "u": {}}
    cnts = {"g": [0], "u": [0]}
    slab_ld_val = {"g": {}, "u": {}}
    gstr = c.Stream("s_gs", 2)
    ustr = c.Stream("s_us", 2)
    dstr = c.Stream("s_dpw", 1)

    def load_slab(kind, e_, mp, sl):
        src_t = gw if kind == "g" else uw
        bufs = GB if kind == "g" else UB
        i = cnts[kind][0]
        buf = bufs[i % NBUF]
        m0 = (mp * PIECE + sl * MSL) * 128
        src = src_t[e_, :, m0:m0 + MSL * 128].rearrange("(dc pp) m -> pp dc m", pp=128)
        dst = buf.rearrange("pp (dc m) -> pp dc m", m=MSL * 128)
        g = guards[kind].get(i % NBUF)
        st = gstr if kind == "g" else ustr
        slab_ld_val[kind][i] = st.load("gpsimd", i % NBUF, lambda en: en.dma_start(out=dst, in_=src),
                                       waits=[g] if g else [])
        cnts[kind][0] += 1
        return i

    dpg = [None]

    def load_dp(e_, mp):
        src = dw[e_, mp * PIECE * 128:(mp + 1) * PIECE * 128, :].rearrange("(mi pp) h -> pp mi h", pp=128)
        dst = DP.rearrange("pp (mi h) -> pp mi h", h=D)
        g = dpg[0]
        return dstr.load("gpsimd", 0, lambda en: en.dma_start(out=dst, in_=src),
                         waits=[g] if g else [])

    order = [(e_, mp) for e_ in range(E) for mp in range(NP)]
    sids = {}
    for k in range(NBUF):
        sids[("g", 0, 0, k)] = load_slab("g", 0, 0, k)
        sids[("u", 0, 0, k)] = load_slab("u", 0, 0, k)

    ph_guard = [0, 0]
    pu_guard = [0, 0]
    pdown_guard = {2: 0, 3: 0}
    sg_guard = [0, 0]
    h_guard = 0
    wbc_done = {}

    for oi, (e_, mp) in enumerate(order):
        gated = e_ < 8
        if mp == 0 and gated:
            wbc_done[e_ % 2] = _mk_wbc(c, e_, WBC[e_ % 2])
        dp_ld = load_dp(e_, mp)
        for mc_i in range(PIECE):
            sl, mi = divmod(mc_i, MSL)
            gsid = sids[("g", e_, mp, sl)]
            usid = sids[("u", e_, mp, sl)]
            for tt in range(4):
                bank = (mc_i * 4 + tt) % 2
                w_g = [(c.s_sc, ph_guard[bank])]
                w_u = [(c.s_ve, pu_guard[bank])]
                if mi == 0 and tt == 0:
                    w_g.append(slab_ld_val["g"][gsid])
                    w_u.append(slab_ld_val["u"][usid])
                if oi == 0 and mc_i == 0 and tt == 0:
                    w_g.append((c.s_ve, c.xn_ready))
                for kc in range(DC):
                    lhsT = GB[gsid % NBUF][:, (kc * MSL + mi) * 128:(kc * MSL + mi + 1) * 128]
                    p.emit("tensor", lambda en, l=lhsT, k=kc, b=bank, t=tt:
                           en.matmul(c.PH[b][:, :], l, XN[k][:, t * 512:(t + 1) * 512],
                                     start=(k == 0), stop=(k == DC - 1)),
                           inc=(c.s_pe, 1) if kc == DC - 1 else None,
                           waits=w_g if kc == 0 else ())
                g_mm = p.val(c.s_pe)
                for kc in range(DC):
                    lhsT = UB[usid % NBUF][:, (kc * MSL + mi) * 128:(kc * MSL + mi + 1) * 128]
                    p.emit("tensor", lambda en, l=lhsT, k=kc, b=bank, t=tt:
                           en.matmul(c.PD[b][:, :], l, XN[k][:, t * 512:(t + 1) * 512],
                                     start=(k == 0), stop=(k == DC - 1)),
                           inc=(c.s_pe, 1) if kc == DC - 1 else None,
                           waits=w_u if kc == 0 else ())
                u_mm = p.val(c.s_pe)
                p.emit("scalar", lambda en, b=bank: en.activation(
                    out=SG[b][:, :], in_=c.PH[b][:, :], func=AF.Silu),
                    inc=(c.s_sc, 1), waits=[(c.s_pe, g_mm), (c.s_ve, sg_guard[bank])])
                sg_v = p.val(c.s_sc)
                ph_guard[bank] = sg_v
                w_h = [(c.s_sc, sg_v), (c.s_pe, u_mm)]
                if mc_i == 0 and tt == 0:
                    w_h.append((c.s_pe, h_guard))
                if gated:
                    p.emit("vector", lambda en, b=bank: en.tensor_tensor(
                        out=c.SCB[0][:, 0:512], in0=c.PD[b][:, :], in1=SG[b][:, :], op=ALU.mult),
                        inc=(c.s_ve, 1), waits=w_h)
                    p.emit("vector", lambda en, m=mc_i, t=tt, b=e_ % 2: en.tensor_tensor(
                        out=HP[m][:, t * 512:(t + 1) * 512], in0=c.SCB[0][:, 0:512],
                        in1=WBC[b][:, t * 512:(t + 1) * 512], op=ALU.mult),
                        inc=(c.s_ve, 1),
                        waits=[(c.s_ve, wbc_done[e_ % 2])] if (mc_i == 0 and tt == 0) else ())
                else:
                    p.emit("vector", lambda en, m=mc_i, t=tt, b=bank: en.tensor_tensor(
                        out=HP[m][:, t * 512:(t + 1) * 512], in0=c.PD[b][:, :], in1=SG[b][:, :], op=ALU.mult),
                        inc=(c.s_ve, 1), waits=w_h)
                vv = p.val(c.s_ve)
                pu_guard[bank] = vv
                sg_guard[bank] = vv
            if mi == MSL - 1:
                guards["g"][gsid % NBUF] = (c.s_pe, p.val(c.s_pe))
                guards["u"][usid % NBUF] = (c.s_pe, p.val(c.s_pe))
                gidx = oi * NSL + sl + NBUF
                if gidx < len(order) * NSL:
                    noi, nsl = divmod(gidx, NSL)
                    ne, nmp = order[noi]
                    sids[("g", ne, nmp, nsl)] = load_slab("g", ne, nmp, nsl)
                    sids[("u", ne, nmp, nsl)] = load_slab("u", ne, nmp, nsl)
        h_done = p.val(c.s_ve)
        for tt in range(4):
            for dc in range(DC):
                bank = 2 + (tt * DC + dc) % 2
                waits = [(c.s_ve, pdown_guard[bank])]
                if tt == 0 and dc == 0:
                    waits += [(c.s_ve, h_done), dp_ld]
                for mc_i in range(PIECE):
                    p.emit("tensor", lambda en, l=DP[:, mc_i * D + dc * 128:mc_i * D + (dc + 1) * 128],
                           m=mc_i, b=bank, t=tt:
                           en.matmul(c.PD[b][:, :], l, HP[m][:, t * 512:(t + 1) * 512],
                                     start=(m == 0), stop=(m == PIECE - 1)),
                           inc=(c.s_pe, 1) if mc_i == PIECE - 1 else None,
                           waits=waits if mc_i == 0 else ())
                mm = p.val(c.s_pe)
                p.emit("vector", lambda en, d=dc, b=bank, t=tt: en.tensor_tensor(
                    out=c.X[d][:, t * 512:(t + 1) * 512],
                    in0=c.X[d][:, t * 512:(t + 1) * 512],
                    in1=c.PD[b][:, :], op=ALU.add),
                    inc=(c.s_ve, 1), waits=[(c.s_pe, mm)])
                pdown_guard[bank] = p.val(c.s_ve)
        h_guard = p.val(c.s_pe)
        dpg[0] = (c.s_pe, h_guard)
    p.barrier(c.all_sems)
    for cl in reversed(closers):
        cl()
    for cl in reversed(c._router_closers):
        cl()


def _emit_kergen_common(c):
    p = c.p
    f = Ctx()
    c._l1k = f
    f.closers = []

    def sbsc(name, shape, dt=F32):
        h, cl = c.sb_scoped(name, shape, dt)
        f.closers.append(cl)
        return h

    delta = c.din("l1a_delta_t", (128, 128))
    alpha = c.din("l1a_alpha_t", (128, 128))
    betap = c.din("l1a_beta_t", (128, 128))
    gamma = c.din("l1a_gamma_t", (128, 128))
    mask16 = c.din("c_mask16", (128, 16, 128), BF16)
    f.DSB = sbsc("l1k_d", [128, 128])
    f.ASB = sbsc("l1k_a", [128, 128])
    f.BSB = sbsc("l1k_b", [128, 128])
    f.GSB = sbsc("l1k_g", [128, 128])
    f.LQ = sbsc("l1k_lq", [128, 128])
    f.CC = sbsc("l1k_c", [128, 128])
    f.MSK = sbsc("l1k_msk", [128, 16 * 128], BF16)
    f.ABT = [sbsc(f"l1k_abt{i}", [128, 96]) for i in range(4)]
    f.LH2 = [sbsc(f"l1k_lh{i}", [128, 128], BF16) for i in range(4)]
    f.J32 = sbsc("l1k_j32", [128, 32])
    f.I64 = sbsc("l1k_i64", [128, 64])
    f.CC4 = sbsc("l1k_c4", [128, 128])
    f.TGRID = sbsc("l1k_tg", [128, 2048])
    kg = c.Burst("s_kg")
    for dst, src in [(f.DSB, delta), (f.ASB, alpha), (f.BSB, betap), (f.GSB, gamma)]:
        kg.load("sync", lambda e, d=dst, s=src: e.dma_start(out=d[:, :], in_=s))
    kg.load("sync", lambda e: e.dma_start(
        out=f.MSK.rearrange("pp (l cc) -> pp l cc", cc=128), in_=mask16))
    ld = kg.done()
    p.emit("gpsimd", lambda e: e.iota(f.J32[:, :], pattern=[[1, 32]], base=0,
                                      channel_multiplier=0, allow_small_or_imprecise_dtypes=True),
           inc=(c.s_gp, 1))
    p.emit("gpsimd", lambda e: e.iota(f.I64[:, :], pattern=[[1, 64]], base=0,
                                      channel_multiplier=0, allow_small_or_imprecise_dtypes=True),
           inc=(c.s_gp, 1))
    p.emit("gpsimd", lambda e: e.iota(f.TGRID[:, :], pattern=[[1, 2048]], base=0,
                                      channel_multiplier=0, allow_small_or_imprecise_dtypes=True),
           inc=(c.s_gp, 1))
    f.iota_done = p.val(c.s_gp)
    p.emit("scalar", lambda e: e.activation(out=f.DSB[:, :], in_=f.DSB[:, :], func=AF.Sigmoid),
           inc=(c.s_sc, 1), waits=[ld])
    p.emit("scalar", lambda e: e.activation(out=f.ASB[:, :], in_=f.ASB[:, :], func=AF.Sigmoid),
           inc=(c.s_sc, 1))
    sg = p.val(c.s_sc)
    p.emit("vector", lambda e: e.tensor_tensor(out=f.LQ[:, :], in0=f.DSB[:, :], in1=f.ASB[:, :],
                                               op=ALU.mult), inc=(c.s_ve, 1), waits=[(c.s_sc, sg)])
    p.emit("vector", lambda e: e.tensor_scalar(out=f.LQ[:, :], in0=f.LQ[:, :], scalar1=-1.0,
                                               scalar2=1.0, op0=ALU.mult, op1=ALU.add), inc=(c.s_ve, 1))
    lq_v = p.val(c.s_ve)
    p.emit("scalar", lambda e: e.activation(out=f.LQ[:, :], in_=f.LQ[:, :], func=AF.Ln),
           inc=(c.s_sc, 1), waits=[(c.s_ve, lq_v)])
    ln_v = p.val(c.s_sc)
    p.emit("vector", lambda e: e.tensor_tensor(out=f.CC[:, :], in0=f.DSB[:, :], in1=f.BSB[:, :],
                                               op=ALU.mult), inc=(c.s_ve, 1), waits=[(c.s_sc, ln_v)])
    p.emit("vector", lambda e: e.tensor_tensor(out=f.CC[:, :], in0=f.CC[:, :], in1=f.GSB[:, :],
                                               op=ALU.mult), inc=(c.s_ve, 1))
    p.emit("vector", lambda e: e.tensor_scalar(out=f.CC4[:, :], in0=f.CC[:, :], scalar1=0.25,
                                               scalar2=None, op0=ALU.mult), inc=(c.s_ve, 1))
    f.pre_done = p.val(c.s_ve)


def _emit_kergen_l1_q(c, q, b):
    p = c.p
    f = c._l1k
    kt_guard = (c.s_pe, p.val(c.s_pe))
    EBoff = [i * 2048 for i in range(4)]
    KT = [b.INVB[:, 0:2048], b.INVB[:, 2048:4096]]
    eb_guard = {i: (c.s_pe, p.val(c.s_pe)) for i in range(4)}
    lh_guard = [0, 0, 0, 0]
    ab_guard = [None, None, None, None]
    steps = [(chl, ci_l) for chl in range(2) for ci_l in range(16)]
    adone = {}

    def emitA(idx):
        chl, ci_l = steps[idx]
        if ci_l % 2 == 1:
            adone[idx] = None
            return
        ci = (2 * q + chl) * 16 + ci_l
        ring = ci_l % 4
        w = [ab_guard[ring]] if ab_guard[ring] else []
        if idx == 0:
            w += [(c.s_ve, f.pre_done), (c.s_gp, f.iota_done)]
        p.emit("vector", lambda e, ci=ci, r=ring: e.tensor_scalar(
            out=f.ABT[r][:, 0:32], in0=f.J32[:, :], scalar1=f.LQ[:, ci:ci + 1], scalar2=64.0,
            op0=ALU.mult, op1=ALU.mult), inc=(c.s_ve, 1), waits=w)
        p.emit("vector", lambda e, ci=ci, r=ring: e.tensor_scalar(
            out=f.ABT[r][:, 32:96], in0=f.I64[:, :], scalar1=f.LQ[:, ci:ci + 1], scalar2=None,
            op0=ALU.mult), inc=(c.s_ve, 1))
        vv = p.val(c.s_ve)
        p.emit("scalar", lambda e, r=ring: e.activation(out=f.ABT[r][:, :], in_=f.ABT[r][:, :],
                                                        func=AF.Exp),
               inc=(c.s_sc, 1), waits=[(c.s_ve, vv)])
        adone[idx] = p.val(c.s_sc)

    egs = {}
    lhds = {}

    def emitEB(idx):
        chl, ci_l = steps[idx]
        ci = (2 * q + chl) * 16 + ci_l
        lb = ci % 16
        ring = ci_l % 4
        ebuf = b.FWDB[:, EBoff[ci_l % 4]:EBoff[ci_l % 4] + 2048]
        g = eb_guard.get(ci_l % 4)
        e_done = adone[idx]
        if ci_l % 2 == 1:
            w = ([g] if g else [])
            if idx == 1:
                w += [(c.s_ve, f.pre_done), (c.s_gp, f.iota_done)]
            p.emit("scalar", lambda e, ci=ci, eb=ebuf: e.activation(
                out=eb, in_=f.TGRID[:, :], func=AF.Exp, scale=f.LQ[:, ci:ci + 1]),
                inc=(c.s_sc, 1), waits=w)
            egs[idx] = (c.s_sc, p.val(c.s_sc))
        else:
            p.emit("vector", lambda e, r=ring, eb=ebuf: e.tensor_tensor(
                out=eb.rearrange("pp (a bb) -> pp a bb", bb=64),
                in0=f.ABT[r][:, 0:32].unsqueeze(-1).broadcast_to([128, 32, 64]),
                in1=f.ABT[r][:, 32:96].unsqueeze(1).broadcast_to([128, 32, 64]),
                op=ALU.mult),
                inc=(c.s_ve, 1), waits=[(c.s_sc, e_done)] + ([g] if g else []))
            egs[idx] = (c.s_ve, p.val(c.s_ve))
            ab_guard[ring] = egs[idx]
        lh_w = [(c.s_pe, lh_guard[ring])]
        if idx == 0:
            lh_w.append((c.s_ve, f.pre_done))
        p.emit("gpsimd", lambda e, lb=lb, ci=ci, r=ring: e.tensor_scalar(
            out=f.LH2[r][:, :], in0=f.MSK[:, lb * 128:(lb + 1) * 128], scalar1=f.CC4[:, ci:ci + 1],
            scalar2=None, op0=ALU.mult),
            inc=(c.s_gp, 1), waits=lh_w)
        lhds[idx] = p.val(c.s_gp)

    def emitMM(idx):
        chl, ci_l = steps[idx]
        ring = ci_l % 4
        ebuf = b.FWDB[:, EBoff[ci_l % 4]:EBoff[ci_l % 4] + 2048]
        for tt in range(4):
            bank = c.PD[tt]
            w = [(c.s_gp, lhds[idx]), egs[idx]]
            if ci_l == 0:
                w.append(c.bguard(bank))
            p.emit("tensor", lambda e, t=tt, eb=ebuf, ba=bank, ci_l=ci_l, r=ring: e.matmul(
                ba[:, :], f.LH2[r][:, :], eb[:, t * 512:(t + 1) * 512],
                start=(ci_l == 0), stop=(ci_l == 15)),
                inc=(c.s_pe, 1) if tt == 3 else None,
                waits=w)
        eb_guard[ci_l % 4] = (c.s_pe, p.val(c.s_pe))
        lh_guard[ring] = p.val(c.s_pe)
        if ci_l == 15:
            mm = p.val(c.s_pe)
            for tt in range(4):
                bank = c.PD[tt]
                p.emit("vector", lambda e, t=tt, chl=chl, ba=bank: e.tensor_copy(
                    KT[chl][:, t * 512:(t + 1) * 512], ba[:, :]),
                    inc=(c.s_ve, 1), waits=[(c.s_pe, mm), kt_guard])
                c.set_bguard(bank, c.s_ve, p.val(c.s_ve))

    emitA(0)
    emitA(1)
    emitA(2)
    emitEB(0)
    for idx in range(32):
        if idx + 3 < 32:
            emitA(idx + 3)
        if idx + 1 < 32:
            emitEB(idx + 1)
        emitMM(idx)
    ktfm_done = p.val(c.s_ve)

    def src_fn(i):
        nt, chl = divmod(i, 2)
        return KT[chl][:, nt * 128:(nt + 1) * 128]

    def dst_fn(i):
        nt, chl = divmod(i, 2)
        return b.SK[:, nt * 512 + 256 + chl * 128:nt * 512 + 256 + (chl + 1) * 128]

    return _transpose_to(c, src_fn, dst_fn, 32, ktfm_done, banks=[c.PA, c.PB, c.PD[2], c.PD[3]])


def _emit_conv_l1(c):
    p = c.p
    p.barrier(c.all_sems)
    _emit_kergen_common(c)
    b = _alloc_conv_bundle(c, False)
    for q in range(4):
        ker_ready = _emit_kergen_l1_q(c, q, b)
        for _i in range(4):
            b.fwd_guard[_i] = (c.s_pe, c.p.val(c.s_pe))
            b.inv_guard[_i] = (c.s_pe, c.p.val(c.s_pe))
        XN = c.XN

        def src_fn(i, q=q, XN=XN):
            nt, ccl = divmod(i, 2)
            return XN[2 * q + ccl][:, nt * 128:(nt + 1) * 128]

        def dst_fn(i):
            nt, ccl = divmod(i, 2)
            return b.SK[:, nt * 512 + ccl * 128:nt * 512 + (ccl + 1) * 128]

        sig_tm = _transpose_to(c, src_fn, dst_fn, 32, ker_ready, banks=[c.PA, c.PB, c.PD[2], c.PD[3]])

        def out_cb(ccl, tt, bank, waits, q=q):
            cc = 2 * q + ccl
            p.emit("vector", lambda e, cc=cc, tt=tt, ba=bank: e.tensor_tensor(
                out=c.X[cc][:, tt * 512:(tt + 1) * 512],
                in0=c.X[cc][:, tt * 512:(tt + 1) * 512],
                in1=ba[:, 0:512], op=ALU.add),
                inc=(c.s_ve, 1), waits=waits)
            c.set_bguard(bank, c.s_ve, p.val(c.s_ve))

        _conv_quarter(c, b, sig_tm, ker_ready, out_cb)
    p.barrier(c.all_sems)
    for cl in reversed(b.closers):
        cl()
    for cl in reversed(c._l1k.closers):
        cl()


def _prep_inputs(inputs):
    cc = host_consts()
    asf = lambda a: np.ascontiguousarray(np.asarray(a), dtype=np.float32)

    common = {"c_fwd": cc["fwd_t"], "c_inv": cc["inv_t"], "c_id": cc["ident"],
              "c_idf": np.eye(128, dtype=np.float32), "c_sel": cc["sel"]}

    d0 = dict(common)
    w1 = asf(inputs["L0_ffn_W1"])
    d0["l0f_w1t"] = np.ascontiguousarray(w1.transpose(0, 2, 1))
    d0["l0f_w2t"] = np.ascontiguousarray(asf(inputs["L0_ffn_W2"]).transpose(0, 2, 1))
    b1 = asf(inputs["L0_ffn_b1"])
    d0["l0f_b1c"] = np.ascontiguousarray(b1.reshape(8, 32, 128).transpose(2, 0, 1).reshape(128, 256))
    d0["l0f_b2"] = asf(inputs["L0_ffn_b2"])
    d0["l0_wr_t"] = np.ascontiguousarray(asf(inputs["L0_ffn_Wr"]).T)
    d0["l0a_win"] = asf(inputs["L0_attn_W_in"])
    d0["l0a_binc"] = np.ascontiguousarray(asf(inputs["L0_attn_b_in"]).reshape(24, 128).T)
    sw = asf(inputs["L0_attn_short_w"])
    d0["l0a_swc"] = np.ascontiguousarray(sw.reshape(24, 128, 3).transpose(1, 0, 2).reshape(128, 72))
    d0["l0a_sbc"] = np.ascontiguousarray(asf(inputs["L0_attn_short_b"]).reshape(24, 128).T)
    d0["l0a_pe_rev"] = cc["pe_rev"]
    d0["l0a_w1"] = asf(inputs["L0_attn_W1"])
    d0["l0a_w2"] = asf(inputs["L0_attn_W2"])
    d0["l0a_w3"] = asf(inputs["L0_attn_W3"])
    d0["l0a_fcols"] = np.ascontiguousarray(np.stack([
        asf(inputs["L0_attn_b1"]), asf(inputs["L0_attn_f1"]),
        asf(inputs["L0_attn_b2"]), asf(inputs["L0_attn_f2"])], axis=1))
    d0["l0a_trev"] = cc["t_rev"]
    d0["l0a_negd"] = cc["negdeltas"]
    d0["l0a_b3r"] = asf(inputs["L0_attn_b3"])[None, :]
    d0["l0a_fbias"] = np.ascontiguousarray(asf(inputs["L0_attn_filter_bias"]).reshape(8, 128).T)
    d0["l0a_bout"] = np.ascontiguousarray(asf(inputs["L0_attn_b_out"]).reshape(8, 128).T)
    d0["l0a_wout"] = asf(inputs["L0_attn_W_out"])

    d1 = dict(common)
    d1["l1_gate_t"] = np.ascontiguousarray(asf(inputs["L1_ffn_gate_w"]).T)
    d1["l1_beta_r16"] = np.ascontiguousarray(np.tile(asf(inputs["L1_ffn_beta"]), 16))[None, :]
    d1["l1f_gate9"] = np.ascontiguousarray(np.concatenate(
        [asf(inputs["L1_ffn_gate_proj_w"]), asf(inputs["L1_ffn_shared_gate_w"])], axis=0))
    d1["l1f_up9"] = np.ascontiguousarray(np.concatenate(
        [asf(inputs["L1_ffn_up_proj_w"]), asf(inputs["L1_ffn_shared_up_w"])], axis=0))
    d1["l1f_down9"] = np.ascontiguousarray(np.concatenate(
        [asf(inputs["L1_ffn_down_proj_w"]), asf(inputs["L1_ffn_shared_down_w"])], axis=0))
    d1["l1a_omega"] = np.ascontiguousarray(asf(inputs["L1_attn_omega"]).reshape(8, 128).T)

    def dn_tile(a):
        return np.ascontiguousarray(asf(a).reshape(128, 128).T)

    d1["l1a_delta_t"] = dn_tile(inputs["L1_attn_delta"])
    d1["l1a_alpha_t"] = dn_tile(inputs["L1_attn_alpha"])
    d1["l1a_beta_t"] = dn_tile(inputs["L1_attn_beta"])
    d1["l1a_gamma_t"] = dn_tile(inputs["L1_attn_gamma"])
    d1["c_mask16"] = cc["mask16"]
    return d0, d1


_BUILD_CACHE = {}


def _get_program(layers):
    key = (tuple(layers), tuple(LIMIT_PHASES) if LIMIT_PHASES is not None else None)
    if key not in _BUILD_CACHE:
        _BUILD_CACHE[key] = build_model(layers)
    return _BUILD_CACHE[key]


def kernel(**inputs):
    d0, d1 = _prep_inputs(inputs)
    x = np.ascontiguousarray(np.asarray(inputs["L0_attn_x"], dtype=np.float32))
    n = 8

    if KNOB("K_UNFUSED"):
        nc0, names0 = _get_program((0,))
        nc1, names1 = _get_program((1,))
        in_maps = []
        for bb in range(n):
            m = {k: v for k, v in d0.items() if k in names0}
            m["x_fm"] = np.ascontiguousarray(x[bb].T)
            in_maps.append(m)
        res0 = run_bass_kernel_spmd(nc0, in_maps, core_ids=list(range(n)))
        in_maps = []
        for bb in range(n):
            m = {k: v for k, v in d1.items() if k in names1}
            m["x_fm"] = np.ascontiguousarray(res0.results[bb]["y_fm"])
            in_maps.append(m)
        res1 = run_bass_kernel_spmd(nc1, in_maps, core_ids=list(range(n)))
        out = np.stack([res1.results[bb]["y_fm"].T for bb in range(n)], axis=0)
        return np.ascontiguousarray(out, dtype=np.float32)

    nc01, names = _get_program((0, 1))
    dall = dict(d0)
    dall.update(d1)
    in_maps = []
    for bb in range(n):
        m = {k: v for k, v in dall.items() if k in names}
        m["x_fm"] = np.ascontiguousarray(x[bb].T)
        in_maps.append(m)
    res = run_bass_kernel_spmd(nc01, in_maps, core_ids=list(range(n)))
    out = np.stack([res.results[bb]["y_fm"].T for bb in range(n)], axis=0)
    return np.ascontiguousarray(out, dtype=np.float32)
